# version 23
# baseline (speedup 1.0000x reference)
import numpy as np
from contextlib import ExitStack

import concourse.bass as bass
import concourse.mybir as mybir
from concourse.bass_utils import run_bass_kernel_spmd
from concourse.bass_core_rust import IndirectOffsetOnAxis

F32 = mybir.dt.float32
BF16 = mybir.dt.bfloat16
I32 = mybir.dt.int32
U32 = mybir.dt.uint32
AF = mybir.ActivationFunctionType
ALU = mybir.AluOpType
AX = mybir.AxisListType

T = 4096
D = 1024
NH = 16
SCALE = 0.125
EPS = 1e-5
NT = 32
NBLK = 8
DC = 8
FF_L = 512
FF_H = 4096
NSEL = 1024
LOGK = float(np.log(1152.0))
NITER = 50
NBIS = 48

ENGINES = ("sync", "gpsimd", "vector", "scalar", "tensor")


class Sem:
    def __init__(self, handle):
        self.h = handle
        self.n = 0


def _apname(a):
    try:
        return a.tensor.name
    except AttributeError:
        return None


class Sched:

    AUTOSYNC = ("vector", "scalar", "gpsimd", "sync")

    def __init__(self):
        self.ops = {e: [] for e in ENGINES}
        self.pending = {e: {} for e in ENGINES}
        self.selfsem = {}

    def set_selfsems(self, d):
        self.selfsem = d

    def add(self, eng, emit, waits=(), incs=(), touch=()):
        ws = [(s, v) for (s, v) in waits if v > 0]
        ics = list(incs)
        assert len(ics) <= 1, "one semaphore update per instruction"
        names = [n for n in (_apname(a) if not isinstance(a, str) else a
                             for a in touch) if n]
        if eng in self.AUTOSYNC and names:
            pend = self.pending[eng]
            extra = {}
            for n in names:
                if n in pend:
                    s, v = pend.pop(n)
                    if extra.get(id(s), (None, 0))[1] < v:
                        extra[id(s)] = (s, v)
            ws = ws + list(extra.values())
            if not ics and eng in self.selfsem:
                ics = [(self.selfsem[eng], 1)]
        self.ops[eng].append((ws, emit, ics))
        for s, k in ics:
            s.n += k
        if eng in self.AUTOSYNC and names and ics:
            marker = (ics[0][0], ics[0][0].n)
            pend = self.pending[eng]
            for n in names:
                pend[n] = marker

    def replay(self, eng_obj, name):
        for ws, emit, ics in self.ops[name]:
            for s, v in ws:
                eng_obj.wait_ge(s.h, v)
            ins = emit(eng_obj)
            for s, k in ics:
                ins.then_inc(s.h, k)


class Pool:

    def __init__(self, p, c, nslots):
        self.p = p
        self.c = c
        self.nslots = nslots
        self.uses = 0

    def acquire(self):
        i = self.uses % self.nslots
        need = self.uses - self.nslots + 1
        w = [(self.c, self.c.n - (self.uses - need))] if need > 0 else []
        self.uses += 1
        return i, w

    def pdone(self):
        return (self.p, 1)

    def cwait(self):
        return [(self.p, self.p.n)]

    def cdone(self):
        return (self.c, 1)


def build_nc(dbg=False):
    nc = bass.Bass("TRN2", target_bir_lowering=False,
                   detect_race_conditions=False)

    def din(name, shape):
        return nc.dram_tensor(name, list(shape), F32, kind="ExternalInput").ap()

    x_d = din("x", (T, D))
    gam_d = din("ln_gamma", (D,))
    bet_d = din("ln_beta", (D,))
    wqkv_d = din("W_qkv", (D, 3 * D))
    bqkv_d = din("b_qkv", (3 * D,))
    wprj_d = din("W_proj", (D, D))
    bprj_d = din("b_proj", (D,))
    rt_d = din("routing_token", (D,))
    lgam_d = din("light_gamma", (D,))
    lw1_d = din("light_w1", (D, FF_L))
    lb1_d = din("light_b1", (FF_L,))
    lw2_d = din("light_w2", (FF_L, D))
    lb2_d = din("light_b2", (D,))
    hgam_d = din("heavy_gamma", (D,))
    hw1_d = din("heavy_w1", (D, FF_H))
    hb1_d = din("heavy_b1", (FF_H,))
    hw2_d = din("heavy_w2", (FF_H, D))
    hb2_d = din("heavy_b2", (D,))
    idn_d = din("cc_ident", (128, 128))
    lt_d = din("cc_lt", (128, 128))
    iota_d = din("cc_iota", (128, 32))

    out_d = nc.dram_tensor("out", [T, D], F32, kind="ExternalOutput").ap()

    KD = "ExternalOutput" if dbg else "Internal"
    qT_h = nc.dram_tensor("qT_h", [D, T], BF16, kind=KD).ap()
    kT_h = nc.dram_tensor("kT_h", [D, T], BF16, kind=KD).ap()
    v_h = nc.dram_tensor("v_h", [128, 8, 32, 130], BF16, kind=KD).ap()
    z_h = nc.dram_tensor("z_h", [T, D], BF16, kind=KD).ap()
    zperm_h = nc.dram_tensor("zperm_h", [T, D], BF16, kind=KD).ap()
    x1_h = nc.dram_tensor("x1_h", [T, D], F32, kind=KD).ap()
    light_h = nc.dram_tensor("light_h", [T, D], BF16, kind=KD).ap()
    h2b_h = nc.dram_tensor("h2b_h", [T, D], BF16, kind=KD).ap()
    w1h16_h = nc.dram_tensor("w1h16_h", [D, FF_H], BF16, kind="Internal").ap()
    w2h16_h = nc.dram_tensor("w2h16_h", [FF_H, D], BF16, kind="Internal").ap()
    bias_h = nc.dram_tensor("bias_h", [D], F32, kind="Internal").ap()
    red_h = nc.dram_tensor("red_h", [128], F32, kind="Internal").ap()
    if dbg:
        dbg_s = nc.dram_tensor("dbg_s", [128, 32], F32, kind="ExternalOutput").ap()
        dbg_u = nc.dram_tensor("dbg_u", [128, 32], F32, kind="ExternalOutput").ap()
        dbg_dest = nc.dram_tensor("dbg_dest", [128, 32], F32, kind="ExternalOutput").ap()
        dbg_u8 = nc.dram_tensor("dbg_u8", [T, D], F32, kind="ExternalOutput").ap()
        dbg_xnT = nc.dram_tensor("dbg_xnT", [NBLK, 128, 8 * 512], BF16,
                                 kind="ExternalOutput").ap()

    S = Sched()
    es = ExitStack()

    def sem(name):
        return Sem(es.enter_context(nc.semaphore(name)))

    s_ldw = sem("ldw")
    g_ldw = sem("gldw")
    g_ldw2 = sem("gldw2")
    g_ldk = sem("gldk")
    g_msc = sem("gmsc")
    g_pcs = sem("gpcs")
    g_zf = sem("gzf")
    g_ldh = sem("gldh")
    g_sct = sem("gsct")
    g_sct2 = sem("gsct2")
    g_sct3 = sem("gsct3")
    g_sct4 = sem("gsct4")
    g_fin = sem("gfin")
    g_fin2 = sem("gfin2")
    g_fin3 = sem("gfin3")
    g_fin4 = sem("gfin4")
    s_ldx = sem("ldx")
    s_stq = sem("stq")
    s_stq2 = sem("stq2")
    s_stq3 = sem("stq3")
    s_stq4 = sem("stq4")
    s_stv = sem("stv")
    s_stv2 = sem("stv2")
    s_stv3 = sem("stv3")
    s_stv4 = sem("stv4")
    s_ldk = sem("ldk")
    s_ldp = sem("ldp")
    s_stz = sem("stz")
    s_msc = sem("msc")
    s_ldh = sem("ldh")
    s_sct = sem("sct")
    s_fin = sem("fin")
    s_fl = sem("fl")
    s_out = sem("out")
    c_ev = sem("ev")
    c_ac = sem("ac")
    c_xfree = sem("xfree")
    c_qkv = sem("qkv")
    c_qkp = sem("qkp")
    c_exp = sem("exp")
    c_avg = sem("avg")
    pA_p, pA_c = sem("pA_p"), sem("pA_c")
    pB_p, pB_c = sem("pB_p"), sem("pB_c")
    pU_c = sem("pU_c")
    pG_p, pG_c = sem("pG_p"), sem("pG_c")
    v_self = sem("v_self")
    a_self = sem("a_self")
    g_self = sem("g_self")
    S.set_selfsems({"vector": v_self, "scalar": a_self, "gpsimd": g_self})

    def sb(name, shape, dt):
        return es.enter_context(nc.sbuf_tensor(name, list(shape), dt))

    ident_f = sb("ident_f", (128, 128), F32)
    lt_f = sb("lt_f", (128, 128), F32)
    iota_f = sb("iota_f", (128, 32), F32)
    ones_r128 = sb("ones_r128", (1, 128), F32)
    ones_r64 = sb("ones_r64", (1, 64), F32)
    ones_c128 = sb("ones_c128", (128, 1), F32)
    eps_t = sb("eps_t", (128, 1), F32)
    logk_t = sb("logk_t", (128, 1), F32)
    rt_bc = sb("rt_bc", (128, D), F32)
    row_g = sb("row_g", (1, D), F32)
    bqa = sb("bqa", (128, 24), F32)
    bkv = sb("bkv", (128, 24), F32)
    gam_c = sb("gam_c", (128, 8), F32)
    lgam_c = sb("lgam_c", (128, 8), F32)
    hgam_c = sb("hgam_c", (128, 8), F32)
    bprj_bc = sb("bprj_bc", (128, D), BF16)
    b1l = sb("b1l", (128, 4), F32)
    b2l_bc = sb("b2l_bc", (128, D), BF16)
    b1h = sb("b1h", (128, 32), F32)
    b2h_bc = sb("b2h_bc", (128, D), BF16)
    xa = sb("xa", (128, 4 * D), F32)
    t1 = sb("t1", (128, D), F32)
    w1k = sb("w1k", (128, 4 * D), F32)
    jnk = sb("jnk", (128, D), F32)
    u8 = sb("u8", (128, D), F32)
    x1t = sb("x1t", (128, D), F32)
    lb16 = sb("lb16", (128, D), BF16)
    out_t = sb("out_t", (128, D), F32)
    zro16 = sb("zro16", (128, 512), BF16)
    zsc = sb("zsc", (128, 2 * D), BF16)
    lstg = sb("lstg", (128, 2 * D), BF16)
    colA = sb("colA", (128, 1), F32)
    colB = sb("colB", (128, 1), F32)
    colC = sb("colC", (128, 1), F32)
    colD = sb("colD", (128, 1), F32)
    colE = sb("colE", (128, 1), F32)
    colF = sb("colF", (128, 1), F32)
    colG = sb("colG", (128, 1), F32)
    colI = sb("colI", (128, 1), F32)
    s_parts = sb("s_parts", (128, 32), F32)
    m_t = sb("m_t", (128, 32), F32)
    bb32 = sb("bb32", (128, 32), F32)
    tmp32 = sb("tmp32", (128, 32), F32)
    e32 = sb("e32", (128, 32), F32)
    u32 = sb("u32", (128, 32), F32)
    q32 = sb("q32", (128, 32), F32)
    um32 = sb("um32", (128, 32), F32)
    mku = sb("mku", (128, 32), U32)
    nbig32 = sb("nbig32", (128, 32), F32)
    eqm32 = sb("eqm32", (128, 32), F32)
    sel32 = sb("sel32", (128, 32), F32)
    rkp32 = sb("rkp32", (128, 32), F32)
    rk32 = sb("rk32", (128, 32), F32)
    dest_f = sb("dest_f", (128, 32), F32)
    dest_i = sb("dest_i", (128, 32), I32)
    row32a = sb("row32a", (1, 32), F32)
    row32b = sb("row32b", (1, 32), F32)
    row32c = sb("row32c", (1, 32), F32)
    row128 = sb("row128", (1, 128), F32)
    sc_a = sb("sc_a", (1, 1), F32)
    sc_lo = sb("sc_lo", (1, 1), F32)
    sc_hi = sb("sc_hi", (1, 1), F32)
    sc_mid = sb("sc_mid", (1, 1), F32)
    sc_t1 = sb("sc_t1", (1, 1), F32)
    sc_t2 = sb("sc_t2", (1, 1), F32)
    sc_cnt = sb("sc_cnt", (1, 1), F32)
    sc_flg = sb("sc_flg", (1, 1), F32)
    sc_vs = sb("sc_vs", (1, 1), F32)
    sc_teq = sb("sc_teq", (1, 1), F32)

    def psum(name, shape, dt):
        return es.enter_context(nc.psum_tensor(name, list(shape), dt))

    ps_stA = psum("ps_stA", (128, 1024), F32)
    ps_stB = psum("ps_stB", (128, 1024), F32)
    ps_u = psum("ps_u", (65, 512), F32)
    ps_gen = psum("ps_gen", (128, 1536), F32)

    def dma(q, out_ap, in_ap, sem_, waits=(), slow=False):
        def emit(e, o=out_ap, i=in_ap, sl=slow):
            if sl:
                return e.dma_start(out=o, in_=i, allow_slow_non_contiguous=True)
            return e.dma_start(out=o, in_=i)
        S.add(q, emit, waits=waits, incs=[(sem_, 16)], touch=[out_ap, in_ap])

    def mm(lhsT, rhs, out, start, stop, waits=(), incs=()):
        S.add("tensor",
              lambda e, l=lhsT, r=rhs, o=out, st=start, sp=stop:
                  e.matmul(o, l, r, start=st, stop=sp),
              waits=waits, incs=incs)

    def ts(eng, out, in0, s1, s2, op0, op1=None, accum=None, waits=(), incs=()):
        if accum is not None and op1 is None:
            op1, s2 = ALU.add, None
        def emit(e, o=out, i=in0, a=s1, b=s2, p0=op0, p1=op1, ac=accum):
            kw = {}
            if ac is not None:
                kw["accum_out"] = ac
            if p1 is None:
                return e.tensor_scalar(o, i, a, None, p0, **kw)
            return e.tensor_scalar(o, i, a, b, p0, p1, **kw)
        tch = [out, in0] + [a for a in (s1, s2, accum) if not isinstance(a, (float, int, type(None)))]
        S.add(eng, emit, waits=waits, incs=incs, touch=tch)

    def tt(eng, out, in0, in1, op, waits=(), incs=()):
        S.add(eng, lambda e, o=out, a=in0, b=in1, p=op: e.tensor_tensor(o, a, b, p),
              waits=waits, incs=incs, touch=[out, in0, in1])

    def stt(eng, out, in0, sc, in1, op0, op1, accum=None, waits=(), incs=()):
        def emit(e, o=out, i0=in0, s=sc, i1=in1, p0=op0, p1=op1, ac=accum):
            kw = {}
            if ac is not None:
                kw["accum_out"] = ac
            return e.scalar_tensor_tensor(o, i0, s, i1, p0, p1, **kw)
        tch = [out, in0, in1] + [a for a in (sc, accum) if not isinstance(a, (float, int, type(None)))]
        S.add(eng, emit, waits=waits, incs=incs, touch=tch)

    def act(out, in_, func, bias=None, scale=None, accum=None, waits=(), incs=()):
        def emit(e, o=out, i=in_, f=func, b=bias, sc=scale, ac=accum):
            kw = {}
            if b is not None:
                kw["bias"] = b
            if sc is not None:
                kw["scale"] = sc
            if ac is not None:
                kw["accum_out"] = ac
            return e.activation(out=o, in_=i, func=f, **kw)
        tch = [out, in_] + [a for a in (bias, scale, accum) if not isinstance(a, (float, int, type(None)))]
        S.add("scalar", emit, waits=waits, incs=incs, touch=tch)

    def cp(eng, out, in_, waits=(), incs=()):
        S.add(eng, lambda e, o=out, i=in_: e.tensor_copy(out=o, in_=i),
              waits=waits, incs=incs, touch=[out, in_])

    def recip(out, in_, waits=(), incs=()):
        S.add("vector", lambda e, o=out, i=in_: e.reciprocal(out=o, in_=i),
              waits=waits, incs=incs, touch=[out, in_])

    def red(out, in_, op, waits=(), incs=()):
        S.add("vector", lambda e, o=out, i=in_, p=op: e.tensor_reduce(o, i, AX.X, p),
              waits=waits, incs=incs, touch=[out, in_])

    def mset(eng, ap, val, waits=(), incs=()):
        S.add(eng, lambda e, a=ap, v=val: e.memset(a, v), waits=waits, incs=incs,
              touch=[ap])

    def transpose(out, in_, waits=(), incs=()):
        S.add("tensor",
              lambda e, o=out, i=in_: e.transpose(out=o, in_=i, identity=ident_f[:]),
              waits=waits, incs=incs)

    def waitonly(eng, waits):
        S.add(eng, lambda e: None, waits=waits)

    def gslc(i, rows=128, cols=512):
        return ps_gen[0:rows, i * 512:i * 512 + cols]

    def bcast_row_b(src_row_dram, dst, extraw=()):
        dma("sync", row_g[:], src_row_dram.rearrange("(o d) -> o d", o=1),
            s_msc, waits=list(extraw) + [(pG_p, pG_p.n)])
        ldn = s_msc.n
        for half in range(2):
            gi, w = poolG.acquire()
            mm(ones_r128[:, 0:128], row_g[0:1, half * 512:(half + 1) * 512],
               gslc(gi), start=True, stop=True,
               waits=w + [(s_msc, ldn)], incs=[poolG.pdone()])
            cp("vector", dst[:, half * 512:(half + 1) * 512], gslc(gi),
               waits=poolG.cwait(), incs=[poolG.cdone()])

    pT_p, pT_c = sem("pT_p"), sem("pT_c")
    poolG = Pool(pG_p, pG_c, 3)

    with nc.sbuf_tensor("Wq_sb", [128, 8 * 3072], BF16) as Wq_sb, \
         nc.sbuf_tensor("beta16", [128, 8], BF16) as beta16, \
         nc.sbuf_tensor("xnT", [128, 2 * 8 * 512], BF16) as xnT, \
         nc.sbuf_tensor("vstage", [128, 2080], BF16) as vstage, \
         nc.sbuf_tensor("bvb", [128, 1024], BF16) as bvb, \
         nc.sbuf_tensor("rowD", [1, D], F32) as rowD, \
         nc.sbuf_tensor("wcast", [128, FF_H], BF16) as wcast, \
         nc.sbuf_tensor("stg", [128, 4 * 512], BF16) as stg, \
         nc.sbuf_tensor("u8b", [128, D], F32) as u8b, \
         nc.sbuf_tensor("t14", [128, 4 * D], F32) as t14, \
         nc.sbuf_tensor("cA4", [128, 4], F32) as cA4, \
         nc.sbuf_tensor("cB4", [128, 4], F32) as cB4, \
         nc.sbuf_tensor("cC4", [128, 4], F32) as cC4, \
         nc.sbuf_tensor("cE4", [128, 4], F32) as cE4, \
         nc.sbuf_tensor("cF4", [128, 4], F32) as cF4, \
         nc.sbuf_tensor("cG4", [128, 4], F32) as cG4:

        for dc in range(DC):
            dma("gpsimd", Wq_sb[:, dc * 3072:(dc + 1) * 3072],
                wqkv_d[dc * 128:(dc + 1) * 128, :],
                g_ldw if dc % 2 == 0 else g_ldw2)
        dma("sync", ident_f[:], idn_d, s_ldw)
        dma("sync", lt_f[:], lt_d, s_ldw)
        dma("sync", iota_f[:], iota_d, s_ldw)
        dma("sync", gam_c[:], gam_d.rearrange("(c p) -> p c", p=128), s_ldw, slow=True)
        dma("sync", lgam_c[:], lgam_d.rearrange("(c p) -> p c", p=128), s_ldw, slow=True)
        dma("sync", hgam_c[:], hgam_d.rearrange("(c p) -> p c", p=128), s_ldw, slow=True)
        dma("gpsimd", beta16[:], bet_d.rearrange("(c p) -> p c", p=128), g_ldw, slow=True)
        dma("sync", bkv[:], bqkv_d.rearrange("(c p) -> p c", p=128), s_ldw, slow=True)
        dma("sync", b1l[:], lb1_d.rearrange("(c p) -> p c", p=128), s_ldw, slow=True)
        dma("sync", b1h[:], hb1_d.rearrange("(c p) -> p c", p=128), s_ldw, slow=True)
        ldw0 = s_ldw.n
        gldw0 = g_ldw.n
        gldw20 = g_ldw2.n

        mset("vector", ones_r128[:], 1.0)
        mset("vector", ones_r64[:], 1.0)
        mset("vector", ones_c128[:], 1.0)
        mset("vector", eps_t[:], EPS)
        mset("vector", logk_t[:], LOGK)
        mset("vector", zro16[:], 0.0)
        mset("vector", nbig32[:], -3.0e38)
        mset("vector", sc_hi[:], 1.0)
        mset("vector",
             vstage[:].rearrange("p (h c) -> p h c", c=65)[:, :, 64:65], 1.0,
             incs=[(c_ev, 1)])
        msets_done = c_ev.n

        for j in range(24):
            for hh in range(2):
                dma("gpsimd",
                    h2b_h[NSEL + j * 128: NSEL + (j + 1) * 128,
                          hh * 512:(hh + 1) * 512],
                    zro16[:], g_zf, waits=[(c_ev, msets_done)])
        zfill_cnt = g_zf.n

        for oc in range(24):
            gi, w = poolG.acquire()
            for dc in range(DC):
                mm(Wq_sb[:, dc * 3072 + oc * 128: dc * 3072 + (oc + 1) * 128],
                   beta16[:, dc:dc + 1], gslc(gi, 128, 1),
                   start=(dc == 0), stop=(dc == DC - 1),
                   waits=(w + [(s_ldw, ldw0), (g_ldw, gldw0),
                               (g_ldw2, gldw20)]
                          if dc == 0 else ()),
                   incs=([poolG.pdone()] if dc == DC - 1 else ()))
            tt("vector", bqa[:, oc:oc + 1], gslc(gi, 128, 1), bkv[:, oc:oc + 1],
               ALU.add, waits=poolG.cwait(), incs=[poolG.cdone()])
        bqa_done = pG_c.n

        for dc in range(DC):
            ts("vector", Wq_sb[:, dc * 3072:(dc + 1) * 3072],
               Wq_sb[:, dc * 3072:(dc + 1) * 3072], gam_c[:, dc:dc + 1], None,
               ALU.mult,
               waits=([(pG_c, bqa_done), (s_ldw, ldw0)] if dc == 0 else ()),
               incs=([(c_ev, 1)] if dc == DC - 1 else ()))
        wqscale_mark = c_ev.n

        dma("sync", bias_h.rearrange("(c p) -> p c", p=128), bqa[:, 16:24], s_msc,
            waits=[(pG_c, bqa_done)], slow=True)
        dma("sync", rowD[:], bias_h.rearrange("(o d) -> o d", o=1), s_msc,
            waits=[(s_msc, s_msc.n)])

        def bcast_row(src_row_dram, dst, via_rowD=False):
            if not via_rowD:
                dma("sync", rowD[:], src_row_dram.rearrange("(o d) -> o d", o=1),
                    s_msc, waits=[(pG_p, pG_p.n)])
            ldn = s_msc.n
            for half in range(2):
                gi, w = poolG.acquire()
                mm(ones_r128[:, 0:128], rowD[0:1, half * 512:(half + 1) * 512],
                   gslc(gi), start=True, stop=True,
                   waits=w + [(s_msc, ldn), (c_ev, msets_done)],
                   incs=[poolG.pdone()])
                cp("vector", dst[:, half * 512:(half + 1) * 512], gslc(gi),
                   waits=poolG.cwait(), incs=[poolG.cdone()])

        bcast_row(None, bvb, via_rowD=True)
        bcast_row(bprj_d, bprj_bc)
        bcast_row(lb2_d, b2l_bc)
        bcast_row(hb2_d, b2h_bc)
        bcast_row(rt_d, rt_bc)

        pA = Pool(pA_p, pA_c, 4)
        psQ = [ps_stA[:, 0:512], ps_stA[:, 512:1024],
               ps_stB[:, 0:512], ps_stB[:, 512:1024]]

        precast = [(hw1_d[dc * 128:(dc + 1) * 128, :],
                    w1h16_h[dc * 128:(dc + 1) * 128, :], FF_H) for dc in range(DC)]
        precast += [(hw2_d[fc * 128:(fc + 1) * 128, :],
                     w2h16_h[fc * 128:(fc + 1) * 128, :], D) for fc in range(32)]
        pc_i = [0]

        def emit_precast():
            if pc_i[0] >= len(precast):
                return
            src, dst, width = precast[pc_i[0]]
            pc_i[0] += 1
            dma("gpsimd", wcast[:, 0:width], src, g_msc,
                waits=[(g_pcs, g_pcs.n)])
            dma("gpsimd", dst, wcast[:, 0:width], g_pcs,
                waits=[(g_msc, g_msc.n)])

        stats_marks = {}

        def stats_part(tb):
            x0 = s_ldx.n
            for tt4 in range(4):
                c = tb * 4 + tt4
                slot = c % 4
                dma("scalar", xa[:, slot * D:(slot + 1) * D],
                    x_d[c * 128:(c + 1) * 128, :], s_ldx,
                    waits=[(c_xfree, max(0, c - 3))])
            for tt4 in range(4):
                c = tb * 4 + tt4
                slot = c % 4
                act(t14[:, tt4 * D:(tt4 + 1) * D], xa[:, slot * D:(slot + 1) * D],
                    AF.Identity, accum=cA4[:, tt4:tt4 + 1],
                    waits=[(s_ldx, x0 + 16 * (tt4 + 1))])
            act(cB4[:], cA4[:], AF.Identity, scale=-1.0 / D)
            for tt4 in range(4):
                c = tb * 4 + tt4
                slot = c % 4
                act(t14[:, tt4 * D:(tt4 + 1) * D], xa[:, slot * D:(slot + 1) * D],
                    AF.Identity, bias=cB4[:, tt4:tt4 + 1],
                    accum=cA4[:, tt4:tt4 + 1],
                    incs=[(c_xfree, 1)])
            for tt4 in range(4):
                act(jnk[:] if tt4 % 2 == 0 else t1[:],
                    t14[:, tt4 * D:(tt4 + 1) * D],
                    AF.Square, accum=cC4[:, tt4:tt4 + 1])
            act(cE4[:], cC4[:], AF.Ln, bias=eps_t[:, 0:1], scale=1.0 / D)
            act(cE4[:], cE4[:], AF.Exp, scale=-0.5)
            act(cG4[:], cA4[:], AF.Identity, scale=1.0 / D)
            act(cB4[:], cA4[:], AF.Identity, scale=-1.0 / D)
            for j in range(4):
                act(cF4[:, j:j + 1], cG4[:, j:j + 1], AF.Identity,
                    scale=cB4[:, j:j + 1])
            for j in range(4):
                act(cF4[:, j:j + 1], cC4[:, j:j + 1], AF.Identity,
                    scale=1.0 / D, bias=cF4[:, j:j + 1])
            for j in range(4):
                act(cF4[:, j:j + 1], cF4[:, j:j + 1], AF.Identity,
                    scale=cE4[:, j:j + 1])
            for j in range(4):
                act(cF4[:, j:j + 1], cF4[:, j:j + 1], AF.Identity,
                    scale=cE4[:, j:j + 1])
            act(cF4[:], cF4[:], AF.Ln, bias=eps_t[:, 0:1])
            act(cF4[:], cF4[:], AF.Exp, scale=-0.5)
            for j in range(4):
                act(cG4[:, j:j + 1], cE4[:, j:j + 1], AF.Identity,
                    scale=cF4[:, j:j + 1])
            for j in range(4):
                act(cA4[:, j:j + 1], cB4[:, j:j + 1], AF.Identity,
                    scale=cG4[:, j:j + 1])

        def trans_part(tb):
            xb = (tb % 2) * 4096
            xnT_v = xnT[:, xb:xb + 4096].rearrange("p (dc q) -> p dc q", q=512)
            for tt4 in range(4):
                ui = tt4 % 2
                ub = u8 if ui == 0 else u8b
                act(ub[:], t14[:, tt4 * D:(tt4 + 1) * D], AF.Identity,
                    scale=cG4[:, tt4:tt4 + 1], bias=cA4[:, tt4:tt4 + 1],
                    waits=([(pG_p, u8_marks[ui])]
                           if u8_marks.get(ui) else []),
                    incs=[(c_ac, 1)])
                acu = c_ac.n
                for grp in range(2):
                    gi, w = poolG.acquire()
                    for j in range(4):
                        transpose(ps_gen[:, gi * 512 + j * 128: gi * 512 + (j + 1) * 128],
                                  ub[:, (grp * 4 + j) * 128:(grp * 4 + j + 1) * 128],
                                  waits=(w + [(c_ac, acu)] +
                                         ([(s_ldw, ldw0), (c_qkv, max(0, tb - 1))]
                                          if (tt4 == 0 and grp == 0) else [])
                                         if j == 0 else ()),
                                  incs=([poolG.pdone()] if j == 3 else ()))
                    cp("vector",
                       xnT_v[:, grp * 4:(grp + 1) * 4, tt4 * 128:(tt4 + 1) * 128],
                       gslc(gi).rearrange("p (dc q) -> p dc q", q=128),
                       waits=poolG.cwait(), incs=[poolG.cdone()])
                u8_marks[ui] = pG_p.n
                emit_precast()

        def qkv_part(tb):
            xb = (tb % 2) * 4096
            xnT_mark = qkv_marks[tb]
            for which, dst, boff in ((0, qT_h, 0), (1, kT_h, 8)):
                for oc in range(DC):
                    i, w = pA.acquire()
                    pslc = psQ[i]
                    for dc in range(DC):
                        mm(Wq_sb[:, dc * 3072 + (boff + oc) * 128:
                                 dc * 3072 + (boff + oc + 1) * 128],
                           xnT[:, xb + dc * 512: xb + (dc + 1) * 512], pslc,
                           start=(dc == 0), stop=(dc == DC - 1),
                           waits=(w + [(pG_c, xnT_mark)] +
                                  ([(c_ev, wqscale_mark)]
                                   if (tb == 0 and boff == 0 and oc == 0)
                                   else []) if dc == 0 else ()),
                           incs=([pA.pdone()] if dc == DC - 1 else ()))
                    qu = qk_u[0]
                    ssl = stg[:, (qu % 4) * 512:(qu % 4 + 1) * 512]
                    ts("vector", ssl, pslc, bqa[:, boff + oc: boff + oc + 1],
                       None, ALU.add,
                       waits=pA.cwait() + ([qk_stok[qu - 4]] if qu >= 4 else []),
                       incs=[pA.cdone()])
                    qsm = qk_sems[qu % 4]
                    dma("sync", dst[oc * 128:(oc + 1) * 128, tb * 512:(tb + 1) * 512],
                        ssl, qsm, waits=[(pA_c, pA_c.n)])
                    qk_stok.append((qsm, qsm.n))
                    qk_u[0] += 1
            for tt4 in range(4):
                g = tb * 4 + tt4
                for half in range(2):
                    i, w = pA.acquire()
                    pslc = psQ[i]
                    for dc in range(DC):
                        mm(xnT[:, xb + dc * 512 + tt4 * 128:
                               xb + dc * 512 + (tt4 + 1) * 128],
                           Wq_sb[:, dc * 3072 + 2048 + half * 512:
                                 dc * 3072 + 2048 + (half + 1) * 512],
                           pslc, start=(dc == 0), stop=(dc == DC - 1),
                           waits=(w if dc == 0 else ()),
                           incs=([pA.pdone()] if dc == DC - 1 else ()))
                    vu = v_u[0]
                    vbase = (vu % 4) * 520
                    vdst = vstage[:, vbase:vbase + 520].rearrange(
                        "p (h c) -> p h c", c=65)[:, :, 0:64]
                    psrc = pslc.rearrange("p (h c) -> p h c", c=64)
                    bsrc = bvb[:, half * 512:(half + 1) * 512].rearrange(
                        "p (h c) -> p h c", c=64)
                    S.add("vector",
                          lambda e, o=vdst, a=psrc, b=bsrc:
                              e.tensor_tensor(o, a, b, ALU.add),
                          waits=pA.cwait() + [(c_ev, msets_done)] +
                                ([v_stok[vu - 4]] if vu >= 4 else []),
                          incs=[pA.cdone()])
                    vsm = v_sems[vu % 4]
                    dma("sync",
                        v_h[:, half * 4:(half + 1) * 4, g:g + 1, :],
                        vstage[:, vbase:vbase + 520].rearrange(
                            "p (h o c) -> p h o c", h=4, o=1),
                        vsm, waits=[(pA_c, pA_c.n)])
                    v_stok.append((vsm, vsm.n))
                    v_u[0] += 1
            S.add("tensor", lambda e: e.nop(), incs=[(c_qkv, 1)])

        qkv_marks = {}
        u8_marks = {}
        qk_sems = [s_stq, s_stq2, s_stq3, s_stq4]
        v_sems = [s_stv, s_stv2, s_stv3, s_stv4]
        qk_stok, v_stok = [], []
        qk_u, v_u = [0], [0]
        for tb in range(NBLK):
            stats_part(tb)
            if tb >= 1:
                qkv_part(tb - 1)
            trans_part(tb)
            qkv_marks[tb] = pG_c.n
        qkv_part(NBLK - 1)
        while pc_i[0] < len(precast):
            emit_precast()
        phaseA_qkv = ([(s, s.n) for s in (s_stq, s_stq2, s_stq3, s_stq4)] +
                      [(s, s.n) for s in (s_stv, s_stv2, s_stv3, s_stv4)])
        phaseA_msc = s_msc.n
        phaseA_gmsc = g_msc.n

    with nc.sbuf_tensor("Wp_sb", [128, 8 * 1024], BF16) as Wp_sb, \
         nc.sbuf_tensor("W1l_sb", [128, 8 * 512], BF16) as W1l_sb, \
         nc.sbuf_tensor("W2l_sb", [128, 4 * 1024], BF16) as W2l_sb, \
         nc.sbuf_tensor("kTs", [128, 3 * T], BF16) as kTs, \
         nc.sbuf_tensor("vs", [128, 3 * 32 * 130], BF16) as vs, \
         nc.sbuf_tensor("qpad", [128, 2 * 512], BF16) as qpad, \
         nc.sbuf_tensor("eT", [128, 6 * 512], BF16) as eT, \
         nc.sbuf_tensor("cxT", [128, 8 * 512], BF16) as cxT, \
         nc.sbuf_tensor("zTl", [128, 8 * 512], BF16) as zTl, \
         nc.sbuf_tensor("h1l", [128, 4 * 512], BF16) as h1l, \
         nc.sbuf_tensor("u_sb", [65, 2 * 512], F32) as u_sb, \
         nc.sbuf_tensor("rsg", [1, 2 * 512], F32) as rsg:

        wB = [(c_qkv, NBLK), (s_msc, phaseA_msc), (g_msc, phaseA_gmsc)]
        for dc in range(DC):
            dma("gpsimd", Wp_sb[:, dc * 1024:(dc + 1) * 1024],
                wprj_d[dc * 128:(dc + 1) * 128, :], g_ldw,
                waits=(wB if dc == 0 else ()))
        for dc in range(DC):
            dma("gpsimd", W1l_sb[:, dc * 512:(dc + 1) * 512],
                lw1_d[dc * 128:(dc + 1) * 128, :], g_ldw)
        for fc in range(4):
            dma("gpsimd", W2l_sb[:, fc * 1024:(fc + 1) * 1024],
                lw2_d[fc * 128:(fc + 1) * 128, :], g_ldw)
        ldw1 = g_ldw.n
        for dc in range(DC):
            ts("vector", W1l_sb[:, dc * 512:(dc + 1) * 512],
               W1l_sb[:, dc * 512:(dc + 1) * 512], lgam_c[:, dc:dc + 1], None,
               ALU.mult,
               waits=([(g_ldw, ldw1), (s_ldw, ldw0)] if dc == 0 else ()),
               incs=([(c_ev, 1)] if dc == DC - 1 else ()))
        w1l_mark = c_ev.n
        mset("vector", qpad[:], 0.0, waits=wB, incs=[(c_ev, 1)])
        qpad_z = c_ev.n

        gidx = [0]
        pend_tail = []
        zt_w = [0]
        pend_tstate = {}
        lout_mark = [pG_p.n]
        stok = {"x1": 0, "z": 0, "l0": 0, "l1": 0}
        GRP = [2] * 16
        GOF = [sum(GRP[:i]) for i in range(len(GRP))]
        last_z_act = [0]
        prj_mark = [pG_p.n]
        lgt_mark = [pG_p.n]

        for qb in range(NBLK):
            if pend_tail:
                xldns = []
                for tt4 in range(4):
                    c = (qb - 1) * 4 + tt4
                    slot = c % 4
                    dma("sync", xa[:, slot * D:(slot + 1) * D],
                        x_d[c * 128:(c + 1) * 128, :], s_ldx,
                        waits=[(c_xfree, NT + max(0, c - 3))])
                    xldns.append(s_ldx.n)
                pend_tstate["xldn"] = xldns
            def kv_loads(hp):
                idx = qb * 8 + hp
                wk = [(pU_c, max(0, 2 * idx - 4))]
                if idx == 0:
                    wk = wk + phaseA_qkv + [(c_qkv, NBLK)]
                dma("sync", kTs[:, (idx % 3) * T:(idx % 3 + 1) * T],
                    kT_h[hp * 128:(hp + 1) * 128, :], s_ldk, waits=wk)
                dma("gpsimd", vs[:, (idx % 3) * 4160:(idx % 3 + 1) * 4160],
                    v_h[:, hp, :, :].rearrange("p g c -> p (g c)"), g_ldk,
                    waits=wk)

            def qp_load(h):
                hidx = qb * NH + h
                par = h % 2
                wq = [(c_qkp, 16 * max(0, hidx - 1)), (c_ev, qpad_z)]
                if hidx == 0:
                    wq = wq + [phaseA_qkv[qi] for qi in range(4)]
                dma("gpsimd",
                    qpad[par * 64:(par + 1) * 64, par * 512:(par + 1) * 512],
                    qT_h[h * 64:(h + 1) * 64, qb * 512:(qb + 1) * 512],
                    s_ldp, waits=wq)

            ld_sched = {}
            for hp in range(8):
                ld_sched.setdefault(max(0, 32 * hp - 16), []).append(
                    lambda hp=hp: kv_loads(hp))
            for h in range(NH):
                ld_sched.setdefault(max(0, 16 * h - 10), []).append(
                    lambda h=h: qp_load(h))

            units = [(h, g) for h in range(NH) for g in range(16)]
            head_ap = {}
            for h in range(NH):
                hp = h // 2
                idx = qb * 8 + hp
                head_ap[h] = (
                    kTs[:, (idx % 3) * T:(idx % 3 + 1) * T],
                    vs[:, (idx % 3) * 4160:(idx % 3 + 1) * 4160],
                    qpad[:, (h % 2) * 512:(h % 2 + 1) * 512],
                    16 * (qb * 8 + hp + 1),
                    16 * (qb * NH + h + 1),
                )
            g0 = gidx[0]

            def emit_qk(u):
                h, g = units[u]
                kslc, vslc, qslc, ldk_need, ldp_need = head_ap[h]
                gg = g0 + u
                pst = ps_stA if gg % 2 == 0 else ps_stB
                gw = [(c_exp, max(0, gg - 1)), (s_ldk, ldk_need),
                      (g_ldk, ldk_need), (s_ldp, ldp_need)]
                for j in range(GRP[g]):
                    kt = GOF[g] + j
                    mm(kslc[:, kt * 128:(kt + 1) * 128], qslc,
                       pst[:, j * 512:(j + 1) * 512], start=True, stop=True,
                       waits=(gw if j == 0 else ()),
                       incs=([(c_qkp, 1)] if j == GRP[g] - 1 else ()))

            def emit_exp(u):
                h, g = units[u]
                gg = g0 + u
                pst = ps_stA if gg % 2 == 0 else ps_stB
                ring = (gg % 3) * 1024
                act(eT[:, ring:ring + GRP[g] * 512], pst[:, 0:GRP[g] * 512],
                    AF.Exp, scale=SCALE,
                    waits=[(c_qkp, gg + 1), (c_avg, max(0, gg - 2))],
                    incs=[(c_exp, 1)])

            def emit_av(u):
                h, g = units[u]
                kslc, vslc, qslc, _, _ = head_ap[h]
                hidx = qb * NH + h
                par = h % 2
                gg = g0 + u
                ring = (gg % 3) * 1024
                for j in range(GRP[g]):
                    kt = GOF[g] + j
                    avw = [(c_exp, gg + 1)] if j == 0 else []
                    if g == 0 and j == 0:
                        avw = avw + [(pU_c, hidx)]
                    mm(vslc[:, kt * 130 + par * 65: kt * 130 + (par + 1) * 65],
                       eT[:, ring + j * 512: ring + (j + 1) * 512],
                       ps_u[0:65, :], start=(kt == 0), stop=(kt == 31),
                       waits=avw,
                       incs=([(c_avg, 1)] if j == GRP[g] - 1 else ()))

            pend_norm = []

            def emit_norm_dve(h):
                hidx = qb * NH + h
                sl = (h % 2) * 512
                cp("vector", u_sb[:, sl:sl + 512], ps_u[0:65, :],
                   waits=[(c_avg, g0 + 16 * (h + 1))], incs=[(pU_c, 1)])
                recip(rsg[:, sl:sl + 512], u_sb[64:65, sl:sl + 512],
                      incs=[(c_ev, 1)])
                pend_norm.append((h, c_ev.n))

            def emit_norm_mm(h, rs_done):
                hp = h // 2
                par = h % 2
                sl = (h % 2) * 512
                gi, wg = poolG.acquire()
                mm(ones_r64[:, 0:64], rsg[0:1, sl:sl + 512], gslc(gi, 64, 512),
                   start=True, stop=True, waits=wg + [(c_ev, rs_done)],
                   incs=[poolG.pdone()])
                tt("vector", cxT[par * 64:(par + 1) * 64, hp * 512:(hp + 1) * 512],
                   u_sb[0:64, sl:sl + 512], gslc(gi, 64, 512), ALU.mult,
                   waits=poolG.cwait() + [(pG_p, prj_mark[0])],
                   incs=[poolG.cdone()])

            tail_sched = {}
            for k in range(8):
                tail_sched[1 + k] = k
            for k in range(4):
                tail_sched[20 + 14 * k] = 8 + k
                tail_sched[24 + 14 * k] = 12 + k
                tail_sched[27 + 14 * k] = 16 + k
                tail_sched[30 + 14 * k] = 20 + k
                tail_sched[33 + 14 * k] = 24 + k
            for k in range(4):
                tail_sched[84 + 3 * k] = 28 + k
            for k in range(4):
                tail_sched[100 + 14 * k] = 32 + k
            for u in range(len(units)):
                for f in ld_sched.get(u, ()):
                    f()
                emit_qk(u)
                emit_exp(u)
                if u >= 1:
                    emit_av(u - 1)
                    if units[u - 1][1] == 15:
                        emit_norm_dve(units[u - 1][0])
                while pend_norm and pend_norm[0][0] <= units[u][0] - 2:
                    emit_norm_mm(*pend_norm.pop(0))
                if u in tail_sched and tail_sched[u] < len(pend_tail):
                    pend_tail[tail_sched[u]]()
                    pend_tail[tail_sched[u]] = None
            emit_av(len(units) - 1)
            emit_norm_dve(NH - 1)
            while pend_norm:
                emit_norm_mm(*pend_norm.pop(0))
            gidx[0] += len(units)
            cx_mark = pG_c.n

            def mk_tail(qb, cx_mark):
                pieces = []
                tstate = {"zt_mark": 0, "gelu_mark": 0}

                def proj_piece(tt4, oc):
                    def f():
                        c = qb * 4 + tt4
                        gi, w = poolG.acquire()
                        for dcin in range(DC):
                            mm(cxT[:, dcin * 512 + tt4 * 128:
                                   dcin * 512 + (tt4 + 1) * 128],
                               Wp_sb[:, dcin * 1024 + oc * 512:
                                     dcin * 1024 + (oc + 1) * 512],
                               gslc(gi), start=(dcin == 0),
                               stop=(dcin == DC - 1),
                               waits=(w + [(pG_c, cx_mark), (g_ldw, ldw1)]
                                      if dcin == 0 else ()),
                               incs=([poolG.pdone()] if dcin == DC - 1 else ()))
                        if tt4 == 3 and oc == 1:
                            prj_mark[0] = pG_p.n
                        tt("vector", w1k[:, tt4 * D + oc * 512:
                                         tt4 * D + (oc + 1) * 512],
                           gslc(gi),
                           bprj_bc[:, oc * 512:(oc + 1) * 512], ALU.add,
                           waits=poolG.cwait(), incs=[poolG.cdone()])
                    return f

                def x1z_s1(tt4):
                    def f():
                        c = qb * 4 + tt4
                        slot = c % 4
                        xs = xa[:, slot * D:(slot + 1) * D]
                        if "xldn" in tstate:
                            xldn = tstate["xldn"][tt4]
                        else:
                            dma("sync", xs, x_d[c * 128:(c + 1) * 128, :], s_ldx,
                                waits=[(c_xfree, NT + max(0, c - 3))])
                            xldn = s_ldx.n
                        tstate[("xldn2", tt4)] = xldn
                        wk = w1k[:, tt4 * D:(tt4 + 1) * D]
                        red(colA[:], wk, ALU.add)
                        ts("vector", colA[:], colA[:], 1.0 / D, None, ALU.mult)
                        ts("vector", t1[:], wk, colA[:, 0:1], None, ALU.subtract,
                           waits=[(c_ac, last_z_act[0])])
                        stt("vector", jnk[:], t1[:], 1.0, t1[:], ALU.mult,
                            ALU.mult, accum=colC[:, 0:1], incs=[(c_ev, 1)])
                        tstate[("m1", tt4)] = c_ev.n
                    return f

                def x1z_s2(tt4):
                    def f():
                        c = qb * 4 + tt4
                        slot = c % 4
                        xs = xa[:, slot * D:(slot + 1) * D]
                        xldn = tstate[("xldn2", tt4)]
                        act(colD[:], colC[:], AF.Ln, bias=eps_t[:, 0:1],
                            scale=1.0 / D, waits=[(c_ev, tstate[("m1", tt4)])])
                        act(colD[:], colD[:], AF.Exp, scale=-0.5,
                            incs=[(c_ac, 1)])
                        m2 = c_ac.n
                        stt("vector", x1t[:], t1[:], colD[:, 0:1], xs, ALU.mult,
                            ALU.add,
                            waits=[(s_ldx, xldn), (s_stz, stok["x1"]),
                                   (c_ac, m2)],
                            incs=[(c_xfree, 1)])
                        dma("sync", x1_h[c * 128:(c + 1) * 128, :], x1t[:],
                            s_stz, waits=[(c_xfree, c_xfree.n)])
                        stok["x1"] = s_stz.n
                        red(colA[:], x1t[:], ALU.add)
                        ts("vector", colA[:], colA[:], 1.0 / D, None, ALU.mult)
                        ts("vector", t1[:], x1t[:], colA[:, 0:1], None,
                           ALU.subtract)
                        stt("vector", jnk[:], t1[:], 1.0, t1[:], ALU.mult,
                            ALU.mult, accum=colC[:, 0:1], incs=[(c_ev, 1)])
                        tstate[("m3", tt4)] = c_ev.n
                    return f

                def x1z_s3(tt4):
                    def f():
                        dvy = tstate[("m3", tt4)]
                        act(colD[:], colC[:], AF.Ln, bias=eps_t[:, 0:1],
                            scale=1.0 / D, waits=[(c_ev, dvy)])
                        act(colD[:], colD[:], AF.Exp, scale=-0.5,
                            incs=[(c_ac, 1)])
                        act(colE[:], colC[:], AF.Ln)
                        act(colE[:], colE[:], AF.Exp, scale=0.5,
                            incs=[(c_ac, 1)])
                        m4 = c_ac.n
                        tt("vector", colE[:], colE[:], colD[:], ALU.mult,
                           waits=[(c_ac, m4)])
                        ts("vector", colE[:], colE[:], 1e-12, None, ALU.max)
                        recip(colE[:], colE[:])
                        ts("vector", colE[:], colE[:], 32.0, None, ALU.mult)
                        tt("vector", colE[:], colE[:], colD[:], ALU.mult,
                           incs=[(c_ev, 1)])
                        tstate[("dvz", tt4)] = c_ev.n
                    return f

                def x1z_s4(tt4):
                    def f():
                        c = qb * 4 + tt4
                        S.add("scalar",
                              lambda e, o=u8[:], i=t1[:], m=colE[:, 0:1]:
                                  e.mul(out=o, in_=i, mul=m),
                              waits=[(c_ev, tstate[("dvz", tt4)]),
                                     (pG_p, zt_w[0])],
                              incs=[(c_ac, 1)],
                              touch=[u8[:], t1[:], colE[:]])
                        acz = c_ac.n
                        last_z_act[0] = acz
                        tstate[("acz", tt4)] = acz
                        cp("vector", lb16[:], u8[:],
                           waits=[(c_ac, acz), (s_stz, stok["z"])],
                           incs=[(c_ev, 1)])
                        dma("sync", z_h[c * 128:(c + 1) * 128, :], lb16[:],
                            s_stz, waits=[(c_ev, c_ev.n)])
                        stok["z"] = s_stz.n
                        stt("vector", jnk[:], t1[:], 1.0, rt_bc[:], ALU.mult,
                            ALU.mult, accum=colF[:, 0:1])
                        tt("vector", colF[:], colF[:], colD[:], ALU.mult)
                        cp("vector", s_parts[:, c:c + 1], colF[:, 0:1])
                    return f

                def x1zT_piece(tt4):
                    def f():
                        acz = tstate[("acz", tt4)]
                        zT_v = zTl[:].rearrange("p (dc q) -> p dc q", q=512)
                        for grp in range(2):
                            gi2, w = poolG.acquire()
                            for j in range(4):
                                dc = grp * 4 + j
                                transpose(ps_gen[:, gi2 * 512 + j * 128:
                                                 gi2 * 512 + (j + 1) * 128],
                                          u8[:, dc * 128:(dc + 1) * 128],
                                          waits=(w + [(c_ac, acz)] +
                                                 ([(pG_p, lgt_mark[0])]
                                                  if (grp == 0 and tt4 == 0)
                                                  else [])
                                                 if j == 0 else ()),
                                          incs=([poolG.pdone()] if j == 3 else ()))
                            cp("vector",
                               zT_v[:, grp * 4:(grp + 1) * 4,
                                    tt4 * 128:(tt4 + 1) * 128],
                               gslc(gi2).rearrange("p (dc q) -> p dc q", q=128),
                               waits=poolG.cwait(), incs=[poolG.cdone()])
                        zt_w[0] = pG_p.n
                        if tt4 == 3:
                            tstate["zt_mark"] = pG_c.n
                    return f

                def lh1_piece(fc):
                    def f():
                        gi, w = poolG.acquire()
                        for dc in range(DC):
                            mm(W1l_sb[:, dc * 512 + fc * 128:
                                      dc * 512 + (fc + 1) * 128],
                               zTl[:, dc * 512:(dc + 1) * 512], gslc(gi),
                               start=(dc == 0), stop=(dc == DC - 1),
                               waits=(w + [(pG_c, tstate["zt_mark"]),
                                           (c_ev, w1l_mark), (pG_p, lgt_mark[0])]
                                      if dc == 0 else ()),
                               incs=([poolG.pdone()] if dc == DC - 1 else ()))
                        if fc == 3:
                            lgt_mark[0] = pG_p.n
                        act(h1l[:, fc * 512:(fc + 1) * 512], gslc(gi),
                            AF.Gelu, bias=b1l[:, fc:fc + 1],
                            waits=poolG.cwait() + [(pG_p, lout_mark[0])],
                            incs=[poolG.cdone()])
                        if fc == 3:
                            tstate["gelu_mark"] = pG_c.n
                    return f

                def lout_piece(tt4):
                    def f():
                        c = qb * 4 + tt4
                        ls = (tt4 % 2) * D
                        lkey = "l%d" % (tt4 % 2)
                        for oc in range(2):
                            gi, w = poolG.acquire()
                            for fc in range(4):
                                mm(h1l[:, fc * 512 + tt4 * 128:
                                       fc * 512 + (tt4 + 1) * 128],
                                   W2l_sb[:, fc * 1024 + oc * 512:
                                         fc * 1024 + (oc + 1) * 512],
                                   gslc(gi), start=(fc == 0),
                                   stop=(fc == 3),
                                   waits=(w + [(pG_c, tstate["gelu_mark"])]
                                          if fc == 0 else ()),
                                   incs=([poolG.pdone()] if fc == 3 else ()))
                            tt("vector", lstg[:, ls + oc * 512:ls + (oc + 1) * 512],
                               gslc(gi),
                               b2l_bc[:, oc * 512:(oc + 1) * 512], ALU.add,
                               waits=poolG.cwait() + [(s_stz, stok[lkey])],
                               incs=[poolG.cdone()])
                        dma("sync", light_h[c * 128:(c + 1) * 128, :],
                            lstg[:, ls:ls + D], s_stz, waits=[(pG_c, pG_c.n)])
                        stok[lkey] = s_stz.n
                        if tt4 == 3:
                            lout_mark[0] = pG_p.n
                    return f

                for tt4 in range(4):
                    for oc in range(2):
                        pieces.append(proj_piece(tt4, oc))
                for tt4 in range(4):
                    pieces.append(x1z_s1(tt4))
                for tt4 in range(4):
                    pieces.append(x1z_s2(tt4))
                for tt4 in range(4):
                    pieces.append(x1z_s3(tt4))
                for tt4 in range(4):
                    pieces.append(x1z_s4(tt4))
                for tt4 in range(4):
                    pieces.append(x1zT_piece(tt4))
                for fc in range(4):
                    pieces.append(lh1_piece(fc))
                for tt4 in range(4):
                    pieces.append(lout_piece(tt4))
                return pieces, tstate

            flush_order = list(range(8)) + \
                [8 + k0 + 4 * s0 for k0 in range(4) for s0 in range(5)] + \
                list(range(28, 36))
            for fi in flush_order:
                if fi < len(pend_tail) and pend_tail[fi] is not None:
                    pend_tail[fi]()
            pend_tail.clear()
            new_pieces, pend_tstate = mk_tail(qb, cx_mark)
            pend_tail.extend(new_pieces)
        for fi in (list(range(8)) +
                   [8 + k0 + 4 * s0 for k0 in range(4) for s0 in range(5)] +
                   list(range(28, 36))):
            if fi < len(pend_tail) and pend_tail[fi] is not None:
                pend_tail[fi]()
        pend_tail.clear()
        phaseB_stz = s_stz.n
        attn_done = [(pU_c, 2 * NBLK * 8), (pG_p, lgt_mark[0]),
                     (pG_c, pG_c.n), (c_exp, gidx[0])]

    ts("vector", bb32[:], s_parts[:], 1.0, None, ALU.mult, incs=[(c_ev, 1)])
    if dbg:
        dma("sync", dbg_s, s_parts[:], s_msc, waits=[(c_ev, c_ev.n)])

    def colsum_11(src_col, waits):
        gi, w = poolG.acquire()
        mm(src_col, ones_c128[:, 0:1], gslc(gi, 1, 1), start=True, stop=True,
           waits=w + list(waits), incs=[poolG.pdone()])
        return gi

    def bcast_col(src11, waits):
        gi, w = poolG.acquire()
        mm(ones_r128[:, 0:128], src11, gslc(gi, 128, 1), start=True, stop=True,
           waits=w + list(waits), incs=[poolG.pdone()])
        cp("vector", colB[:], gslc(gi, 128, 1), waits=poolG.cwait(),
           incs=[poolG.cdone()])

    relu_mark = [c_ac.n]
    for it in range(NITER):
        tt("vector", tmp32[:], s_parts[:], bb32[:], ALU.subtract,
           waits=[(c_ac, relu_mark[0])], incs=[(c_ev, 1)])
        act(e32[:], tmp32[:], AF.Exp, accum=colA[:, 0:1],
            waits=[(c_ev, c_ev.n)], incs=[(c_ac, 1)])
        gi = colsum_11(colA[:, 0:1], [(c_ac, c_ac.n)])
        act(sc_t1[:], gslc(gi, 1, 1), AF.Ln, waits=poolG.cwait(),
            incs=[poolG.cdone()])
        gi2, w2 = poolG.acquire()
        mm(ones_r128[:, 0:128], sc_t1[0:1, 0:1], gslc(gi2, 128, 1),
           start=True, stop=True, waits=w2 + [(pG_c, pG_c.n)],
           incs=[poolG.pdone()])
        act(colB[:], gslc(gi2, 128, 1), AF.Identity, scale=-1.0,
            bias=logk_t[:, 0:1],
            waits=poolG.cwait(), incs=[poolG.cdone()])
        act(bb32[:], s_parts[:], AF.Relu, bias=colB[:, 0:1],
            incs=[(c_ac, 1)])
        relu_mark[0] = c_ac.n
    ts("vector", u32[:], s_parts[:], colB[:, 0:1], None, ALU.add,
       waits=[(c_ac, c_ac.n)])
    tt("vector", u32[:], u32[:], bb32[:], ALU.subtract, incs=[(c_ev, 1)])
    u_done = c_ev.n
    if dbg:
        dma("sync", dbg_u, u32[:], s_msc, waits=[(c_ev, u_done)])
    act(q32[:], u32[:], AF.Exp, waits=[(c_ev, u_done)], incs=[(c_ac, 1)])
    ts("vector", tmp32[:], q32[:], -1.0, 1.0, ALU.mult, ALU.add,
       waits=[(c_ac, c_ac.n)])
    tt("vector", m_t[:], q32[:], tmp32[:], ALU.add)
    red(colA[:], u32[:], ALU.min, incs=[(c_ev, 1)])
    dma("sync", red_h.rearrange("(p o) -> p o", o=1), colA[:, 0:1], s_msc,
        waits=[(c_ev, c_ev.n)])
    dma("sync", row128[:], red_h.rearrange("(o p) -> o p", o=1), s_msc,
        waits=[(s_msc, s_msc.n)])
    red(sc_lo[:], row128[:], ALU.min, waits=[(s_msc, s_msc.n)])
    ts("vector", sc_lo[:], sc_lo[:], 1.0, None, ALU.subtract, incs=[(c_ev, 1)])
    for it in range(NBIS):
        tt("vector", sc_mid[:], sc_lo[:], sc_hi[:], ALU.add)
        ts("vector", sc_mid[:], sc_mid[:], 0.5, None, ALU.mult, incs=[(c_ev, 1)])
        bcast_col(sc_mid[0:1, 0:1], [(c_ev, c_ev.n)])
        ts("vector", tmp32[:], u32[:], colB[:, 0:1], None, ALU.is_gt,
           accum=colA[:, 0:1], incs=[(c_ev, 1)])
        gi = colsum_11(colA[:, 0:1], [(c_ev, c_ev.n)])
        cp("vector", sc_cnt[:], gslc(gi, 1, 1), waits=poolG.cwait(),
           incs=[poolG.cdone()])
        ts("vector", sc_flg[:], sc_cnt[:], float(NSEL), None, ALU.is_ge)
        tt("vector", sc_t1[:], sc_mid[:], sc_lo[:], ALU.subtract)
        tt("vector", sc_t1[:], sc_t1[:], sc_flg[:], ALU.mult)
        tt("vector", sc_lo[:], sc_lo[:], sc_t1[:], ALU.add)
        ts("vector", sc_t2[:], sc_flg[:], -1.0, 1.0, ALU.mult, ALU.add)
        tt("vector", sc_t1[:], sc_mid[:], sc_hi[:], ALU.subtract)
        tt("vector", sc_t1[:], sc_t1[:], sc_t2[:], ALU.mult)
        tt("vector", sc_hi[:], sc_hi[:], sc_t1[:], ALU.add, incs=[(c_ev, 1)])
    bcast_col(sc_hi[0:1, 0:1], [(c_ev, c_ev.n)])
    ts("vector", mku[:], u32[:], colB[:, 0:1], None, ALU.is_gt)
    cp("vector", um32[:], u32[:])
    S.add("vector",
          lambda e: e.copy_predicated(out=um32[:], mask=mku[:], data=nbig32[:]),
          touch=[um32[:], mku[:], nbig32[:]])
    red(colA[:], um32[:], ALU.max, incs=[(c_ev, 1)])
    dma("sync", red_h.rearrange("(p o) -> p o", o=1), colA[:, 0:1], s_msc,
        waits=[(c_ev, c_ev.n)])
    dma("sync", row128[:], red_h.rearrange("(o p) -> o p", o=1), s_msc,
        waits=[(s_msc, s_msc.n)])
    red(sc_vs[:], row128[:], ALU.max, waits=[(s_msc, s_msc.n)],
        incs=[(c_ev, 1)])
    bcast_col(sc_vs[0:1, 0:1], [(c_ev, c_ev.n)])
    cp("vector", colI[:], colB[:])
    ts("vector", sel32[:], u32[:], colI[:, 0:1], None, ALU.is_gt,
       accum=colA[:, 0:1], incs=[(c_ev, 1)])
    gi = colsum_11(colA[:, 0:1], [(c_ev, c_ev.n)])
    cp("vector", sc_cnt[:], gslc(gi, 1, 1), waits=poolG.cwait(),
       incs=[poolG.cdone()])
    ts("vector", sc_teq[:], sc_cnt[:], -1.0, float(NSEL), ALU.mult, ALU.add)
    ts("vector", eqm32[:], u32[:], colI[:, 0:1], None, ALU.is_equal,
       incs=[(c_ev, 1)])

    def rank_and_prefix(mask32, out32):
        mk_done = c_ev.n
        gi, w = poolG.acquire()
        mm(lt_f[:], mask32[:], gslc(gi, 128, 32), start=True, stop=True,
           waits=w + [(c_ev, mk_done)], incs=[poolG.pdone()])
        cp("vector", rkp32[:], gslc(gi, 128, 32), waits=poolG.cwait(),
           incs=[poolG.cdone()])
        gi, w = poolG.acquire()
        mm(mask32[:], ones_c128[:, 0:1], gslc(gi, 32, 1), start=True, stop=True,
           waits=w + [(c_ev, mk_done)], incs=[poolG.pdone()])
        cp("vector", colA[0:32, 0:1], gslc(gi, 32, 1), waits=poolG.cwait(),
           incs=[poolG.cdone()])
        dma("sync", red_h[0:32].rearrange("(p o) -> p o", o=1), colA[0:32, 0:1],
            s_msc, waits=[(pG_c, pG_c.n)])
        dma("sync", row32a[:], red_h[0:32].rearrange("(o p) -> o p", o=1), s_msc,
            waits=[(s_msc, s_msc.n)])
        cp("vector", row32b[:], row32a[:], waits=[(s_msc, s_msc.n)])
        for k in (1, 2, 4, 8, 16):
            cp("vector", row32c[0:1, 0:k], row32b[0:1, 0:k])
            tt("vector", row32c[0:1, k:32], row32b[0:1, k:32],
               row32b[0:1, 0:32 - k], ALU.add)
            cp("vector", row32b[:], row32c[:])
        tt("vector", row32b[:], row32b[:], row32a[:], ALU.subtract,
           incs=[(c_ev, 1)])
        gi, w = poolG.acquire()
        mm(ones_r128[:, 0:128], row32b[0:1, :], gslc(gi, 128, 32), start=True,
           stop=True, waits=w + [(c_ev, c_ev.n)], incs=[poolG.pdone()])
        tt("vector", out32[:], rkp32[:], gslc(gi, 128, 32), ALU.add,
           waits=poolG.cwait(), incs=[poolG.cdone()])

    rank_and_prefix(eqm32, rk32)
    bcast_col(sc_teq[0:1, 0:1], [(v_self, v_self.n)])
    ts("vector", tmp32[:], rk32[:], colB[:, 0:1], None, ALU.is_lt)
    tt("vector", tmp32[:], tmp32[:], eqm32[:], ALU.mult)
    tt("vector", sel32[:], sel32[:], tmp32[:], ALU.add, incs=[(c_ev, 1)])
    rank_and_prefix(sel32, rk32)
    ts("vector", tmp32[:], sel32[:], -1.0, 1.0, ALU.mult, ALU.add)
    tt("vector", dest_f[:], iota_f[:], rk32[:], ALU.subtract)
    ts("vector", dest_f[:], dest_f[:], float(NSEL), None, ALU.add)
    tt("vector", dest_f[:], dest_f[:], tmp32[:], ALU.mult)
    tt("vector", tmp32[:], sel32[:], rk32[:], ALU.mult)
    tt("vector", dest_f[:], dest_f[:], tmp32[:], ALU.add)
    cp("vector", dest_i[:], dest_f[:], incs=[(c_ev, 1)])
    dest_done = c_ev.n
    if dbg:
        dma("sync", dbg_dest, dest_f[:], s_msc, waits=[(c_ev, dest_done)])

    with nc.sbuf_tensor("W64", [128, 8 * FF_H], BF16) as W64, \
         nc.sbuf_tensor("h1h", [128, 32 * 512], BF16) as h1h, \
         nc.sbuf_tensor("zsT", [128, 8 * 512], BF16) as zsT, \
         nc.sbuf_tensor("hstg", [128, 2 * D], BF16) as hstg:

        wF = attn_done + [(s_stz, phaseB_stz), (s_msc, s_msc.n),
                          (g_msc, g_msc.n), (c_ev, dest_done)]
        sct_sems = [g_sct, g_sct2, g_sct3, g_sct4]
        sct_tok = []
        for c in range(NT):
            slot = c % 2
            wz = ((list(wF) if c == 0 else []) +
                  ([sct_tok[c - 2]] if c >= 2 else []))
            dma("sync", zsc[:, slot * D:(slot + 1) * D],
                z_h[c * 128:(c + 1) * 128, :], s_fin, waits=wz)
            gs = sct_sems[c % 4]
            S.add("gpsimd",
                  lambda e, o=zperm_h, i=zsc[:, slot * D:(slot + 1) * D],
                         ix=dest_i[:, c:c + 1]:
                      e.indirect_dma_start(out=o,
                                           out_offset=IndirectOffsetOnAxis(ix, 0),
                                           in_=i, in_offset=None),
                  waits=[(s_fin, s_fin.n)] +
                        ([(c_ev, dest_done)] if c < 4 else []),
                  incs=[(gs, 16)], touch=[zperm_h, zsc[:], dest_i[:]])
            sct_tok.append((gs, gs.n))
        sct_toks = [(gs, gs.n) for gs in sct_sems]

        def load_W(dram16, nchunk, width, waits):
            for k in range(nchunk):
                dma("sync", W64[:, k * width:(k + 1) * width],
                    dram16[k * 128:(k + 1) * 128, :], s_ldh,
                    waits=(list(waits) if k == 0 else ()))

        pA3 = Pool(pA_p, pA_c, 2)
        pB3 = Pool(pB_p, pB_c, 2)
        load_W(w1h16_h, DC, FF_H,
               attn_done + [(s_stz, phaseB_stz)])
        tp_mark = [pB_p.n]
        for sb_i in range(2):
            ldh_wa = s_ldh.n
            for tt4 in range(4):
                g = sb_i * 4 + tt4
                slot = g % 2
                dma("gpsimd", zsc[:, slot * D:(slot + 1) * D],
                    zperm_h[g * 128:(g + 1) * 128, :], g_ldh,
                    waits=sct_toks +
                          ([(pB_p, tp_mark[0])] if g >= 2 else []))
                cp("vector", u8[:], zsc[:, slot * D:(slot + 1) * D],
                   waits=[(g_ldh, g_ldh.n)] +
                         (list(wF) if (sb_i == 0 and tt4 == 0) else []),
                   incs=[(c_ev, 1)])
                zbd = c_ev.n
                for dc in range(DC):
                    i, w = pB3.acquire()
                    transpose(ps_stB[:, i * 512:i * 512 + 128],
                              u8[:, dc * 128:(dc + 1) * 128],
                              waits=w + [(c_ev, zbd)],
                              incs=[pB3.pdone()])
                    ts("vector",
                       zsT[:, dc * 512 + tt4 * 128: dc * 512 + (tt4 + 1) * 128],
                       ps_stB[:, i * 512:i * 512 + 128], hgam_c[:, dc:dc + 1], None,
                       ALU.mult, waits=pB3.cwait(), incs=[pB3.cdone()])
                tp_mark[0] = pB_p.n
            zsT_mark = pB_c.n
            for fc in range(32):
                i, w = pA3.acquire()
                pslc = ps_stA[:, i * 512:(i + 1) * 512]
                for dc in range(DC):
                    mm(W64[:, dc * FF_H + fc * 128: dc * FF_H + (fc + 1) * 128],
                       zsT[:, dc * 512:(dc + 1) * 512], pslc,
                       start=(dc == 0), stop=(dc == DC - 1),
                       waits=((w + [(pB_c, zsT_mark)] if dc == 0 else []) +
                              ([(s_ldh, ldh_wa - 128 + 16 * (dc + 1))]
                               if fc == 0 else [])),
                       incs=([pA3.pdone()] if dc == DC - 1 else ()))
                act(h1h[:, fc * 512:(fc + 1) * 512], pslc, AF.Gelu,
                    bias=b1h[:, fc:fc + 1], waits=pA3.cwait(),
                    incs=[pA3.cdone()])
            h1mm_mark = pA_p.n
            h1_mark = pA_c.n
            load_W(w2h16_h, 32, D, [(pA_p, h1mm_mark)])
            ldh_wb = s_ldh.n
            for tt4 in range(4):
                g = sb_i * 4 + tt4
                for oc in range(2):
                    i, w = pB3.acquire()
                    pslc = ps_stB[:, i * 512:(i + 1) * 512]
                    for fc in range(32):
                        mm(h1h[:, fc * 512 + tt4 * 128: fc * 512 + (tt4 + 1) * 128],
                           W64[:, fc * 1024 + oc * 512: fc * 1024 + (oc + 1) * 512],
                           pslc, start=(fc == 0), stop=(fc == 31),
                           waits=((w + [(pA_c, h1_mark)] if fc == 0 else []) +
                                  ([(s_ldh, ldh_wb - 512 + 16 * (fc + 1))]
                                   if (tt4 == 0 and oc == 0) else [])),
                           incs=([pB3.pdone()] if fc == 31 else ()))
                    tt("vector",
                       hstg[:, (g % 2) * D + oc * 512:(g % 2) * D + (oc + 1) * 512],
                       pslc, b2h_bc[:, oc * 512:(oc + 1) * 512], ALU.add,
                       waits=pB3.cwait() +
                             [(s_sct, max(0, s_sct.n - 16))],
                       incs=[pB3.cdone()])
                dma("sync", h2b_h[g * 128:(g + 1) * 128, :],
                    hstg[:, (g % 2) * D:(g % 2 + 1) * D], s_sct,
                    waits=[(pB_c, pB_c.n)])
            h2mm_mark = pB_p.n
            if sb_i == 0:
                load_W(w1h16_h, DC, FF_H, [(pB_p, h2mm_mark)])
        h2_all = s_sct.n

    with nc.sbuf_tensor("hg4", [128, 4 * 4 * D], BF16) as hg4, \
         nc.sbuf_tensor("lg4", [128, 2 * 4 * D], BF16) as lg4, \
         nc.sbuf_tensor("xf8", [128, 2 * 4 * D], F32) as xf8, \
         nc.sbuf_tensor("out2", [128, 2 * D], F32) as out2, \
         nc.sbuf_tensor("fA4", [128, 8], F32) as fA4, \
         nc.sbuf_tensor("fB4", [128, 8], F32) as fB4, \
         nc.sbuf_tensor("fC4", [128, 8], F32) as fC4, \
         nc.sbuf_tensor("fD4", [128, 8], F32) as fD4:

        gf_sems = [g_fin, g_fin2, g_fin3, g_fin4]
        ffn_use = {-4: 0, -3: 0, -2: 0, -1: 0}
        ld_toks = {}
        gl_toks = {}

        def emit_wave_gathers(w8):
            pbg = (w8 % 4) * 4 * D
            gate = [(c_ev, ffn_use[w8 - 4]), (s_sct, h2_all),
                    (g_zf, zfill_cnt)]
            for tt4 in range(4):
                c = w8 * 4 + tt4
                gs = gf_sems[tt4]
                S.add("gpsimd",
                      lambda e, o=hg4[:, pbg + tt4 * D:pbg + (tt4 + 1) * D],
                             i=h2b_h, ix=dest_i[:, c:c + 1]:
                          e.indirect_dma_start(out=o, out_offset=None, in_=i,
                                               in_offset=IndirectOffsetOnAxis(ix, 0)),
                      waits=(gate if tt4 == 0 else ()),
                      incs=[(gs, 16)])
            gl_toks[w8] = [(gs2, gs2.n) for gs2 in gf_sems]

        def emit_wave_sloads(w8):
            par = w8 % 2
            pb = par * 4 * D
            gate = [(c_ev, ffn_use[w8 - 2]), (s_sct, h2_all),
                    (c_xfree, 2 * NT + 4 * max(0, w8 - 1))]
            for tt4 in range(4):
                c = w8 * 4 + tt4
                extra = [(s_stz, phaseB_stz)] if c == 0 else []
                dma("sync", xf8[:, pb + tt4 * D:pb + (tt4 + 1) * D],
                    x1_h[c * 128:(c + 1) * 128, :], s_fin,
                    waits=gate + extra)
                dma("scalar", lg4[:, pb + tt4 * D:pb + (tt4 + 1) * D],
                    light_h[c * 128:(c + 1) * 128, :], s_fl,
                    waits=gate + extra)
            ld_toks[w8] = [(s_fin, s_fin.n), (s_fl, s_fl.n)]

        for wg in range(4):
            emit_wave_gathers(wg)
        emit_wave_sloads(0)
        for w8 in range(8):
            par = w8 % 2
            pb = par * 4 * D
            cb = par * 4
            if w8 + 1 < 8:
                emit_wave_sloads(w8 + 1)
            for tt4 in range(4):
                c = w8 * 4 + tt4
                stt("vector", w1k[:, tt4 * D:(tt4 + 1) * D],
                    hg4[:, (w8 % 4) * 4 * D + tt4 * D:
                        (w8 % 4) * 4 * D + (tt4 + 1) * D], m_t[:, c:c + 1],
                    lg4[:, pb + tt4 * D:pb + (tt4 + 1) * D], ALU.mult, ALU.add,
                    accum=fA4[:, cb + tt4:cb + tt4 + 1],
                    waits=((ld_toks[w8] + gl_toks[w8]) if tt4 == 0 else ()),
                    incs=([(c_ev, 1)] if tt4 == 3 else ()))
            ffn_use[w8] = c_ev.n
            ffn_mark = c_ev.n
            if w8 + 4 < 8:
                emit_wave_gathers(w8 + 4)
            for tt4 in range(4):
                act(jnk[:] if tt4 % 2 == 0 else t1[:],
                    w1k[:, tt4 * D:(tt4 + 1) * D], AF.Square,
                    accum=fC4[:, cb + tt4:cb + tt4 + 1],
                    waits=([(c_ev, ffn_mark)] if tt4 == 0 else ()))
            act(fB4[:, cb:cb + 4], fA4[:, cb:cb + 4], AF.Identity,
                scale=1.0 / D)
            act(fA4[:, cb:cb + 4], fA4[:, cb:cb + 4], AF.Identity,
                scale=-1.0 / D)
            for tt4 in range(4):
                act(fD4[:, cb + tt4:cb + tt4 + 1],
                    fB4[:, cb + tt4:cb + tt4 + 1], AF.Identity,
                    scale=fA4[:, cb + tt4:cb + tt4 + 1])
            for tt4 in range(4):
                act(fD4[:, cb + tt4:cb + tt4 + 1],
                    fC4[:, cb + tt4:cb + tt4 + 1], AF.Identity,
                    scale=1.0 / D, bias=fD4[:, cb + tt4:cb + tt4 + 1])
            act(fD4[:, cb:cb + 4], fD4[:, cb:cb + 4], AF.Ln,
                bias=eps_t[:, 0:1])
            act(fD4[:, cb:cb + 4], fD4[:, cb:cb + 4], AF.Exp, scale=-0.5,
                incs=[(c_ac, 1)])
            stat_mark = c_ac.n
            for tt4 in range(4):
                c = w8 * 4 + tt4
                ob = (c % 2) * D
                ts("vector", t1[:], w1k[:, tt4 * D:(tt4 + 1) * D],
                   fB4[:, cb + tt4:cb + tt4 + 1],
                   fD4[:, cb + tt4:cb + tt4 + 1],
                   ALU.subtract, ALU.mult,
                   waits=([(c_ac, stat_mark)] if tt4 == 0 else ()))
                tt("vector", out2[:, ob:ob + D], t1[:],
                   xf8[:, pb + tt4 * D:pb + (tt4 + 1) * D],
                   ALU.add, waits=[(s_out, 16 * max(0, c - 1))],
                   incs=[(c_xfree, 1)])
                dma("sync", out_d[c * 128:(c + 1) * 128, :],
                    out2[:, ob:ob + D], s_out,
                    waits=[(c_xfree, c_xfree.n)])
        waitonly("sync", [(s_out, 16 * NT)])

    with nc.Block() as blk:
        blk.sync(lambda e: S.replay(e, "sync"))
        blk.gpsimd(lambda e: S.replay(e, "gpsimd"))
        blk.vector(lambda e: S.replay(e, "vector"))
        blk.scalar(lambda e: S.replay(e, "scalar"))
        blk.tensor(lambda e: S.replay(e, "tensor"))

    es.close()
    return nc


_CONSTS = None


def _consts():
    global _CONSTS
    if _CONSTS is None:
        _CONSTS = {
            "cc_ident": np.eye(128, dtype=np.float32),
            "cc_lt": np.triu(np.ones((128, 128), np.float32), 1),
            "cc_iota": (np.arange(32, dtype=np.float32)[None, :] * 128.0
                        + np.arange(128, dtype=np.float32)[:, None]),
        }
    return _CONSTS


def kernel(L0_attn_x, L0_attn_ln_gamma, L0_attn_ln_beta, L0_attn_W_qkv,
           L0_attn_b_qkv, L0_attn_W_proj, L0_attn_b_proj, L0_ffn_routing_token,
           L0_ffn_light_gamma, L0_ffn_light_w1, L0_ffn_light_b1, L0_ffn_light_w2,
           L0_ffn_light_b2, L0_ffn_heavy_gamma, L0_ffn_heavy_w1, L0_ffn_heavy_b1,
           L0_ffn_heavy_w2, L0_ffn_heavy_b2):
    x = np.ascontiguousarray(np.asarray(L0_attn_x, dtype=np.float32))
    f32c = lambda a: np.ascontiguousarray(np.asarray(a, np.float32))
    shared = {
        "ln_gamma": f32c(L0_attn_ln_gamma),
        "ln_beta": f32c(L0_attn_ln_beta),
        "W_qkv": f32c(L0_attn_W_qkv),
        "b_qkv": f32c(L0_attn_b_qkv),
        "W_proj": f32c(L0_attn_W_proj),
        "b_proj": f32c(L0_attn_b_proj),
        "routing_token": f32c(L0_ffn_routing_token),
        "light_gamma": f32c(L0_ffn_light_gamma),
        "light_w1": f32c(L0_ffn_light_w1),
        "light_b1": f32c(L0_ffn_light_b1),
        "light_w2": f32c(L0_ffn_light_w2),
        "light_b2": f32c(L0_ffn_light_b2),
        "heavy_gamma": f32c(L0_ffn_heavy_gamma),
        "heavy_w1": f32c(L0_ffn_heavy_w1),
        "heavy_b1": f32c(L0_ffn_heavy_b1),
        "heavy_w2": f32c(L0_ffn_heavy_w2),
        "heavy_b2": f32c(L0_ffn_heavy_b2),
    }
    shared.update(_consts())
    nc = build_nc()
    in_maps = [dict(shared, x=np.ascontiguousarray(x[b])) for b in range(8)]
    res = run_bass_kernel_spmd(nc, in_maps, core_ids=list(range(8)))
    out = np.stack([res.results[b]["out"] for b in range(8)], axis=0)
    return out


# version 25
# speedup vs baseline: 1.0037x; 1.0037x over previous
import numpy as np
from contextlib import ExitStack

import concourse.bass as bass
import concourse.mybir as mybir
from concourse.bass_utils import run_bass_kernel_spmd
from concourse.bass_core_rust import IndirectOffsetOnAxis

F32 = mybir.dt.float32
BF16 = mybir.dt.bfloat16
I32 = mybir.dt.int32
U32 = mybir.dt.uint32
AF = mybir.ActivationFunctionType
ALU = mybir.AluOpType
AX = mybir.AxisListType

T = 4096
D = 1024
NH = 16
SCALE = 0.125
EPS = 1e-5
NT = 32
NBLK = 8
DC = 8
FF_L = 512
FF_H = 4096
NSEL = 1024
LOGK = float(np.log(1152.0))
NITER = 50
NBIS = 48

ENGINES = ("sync", "gpsimd", "vector", "scalar", "tensor")


class Sem:
    def __init__(self, handle):
        self.h = handle
        self.n = 0


def _apname(a):
    try:
        return a.tensor.name
    except AttributeError:
        return None


class Sched:

    AUTOSYNC = ("vector", "scalar", "gpsimd", "sync")

    def __init__(self):
        self.ops = {e: [] for e in ENGINES}
        self.pending = {e: {} for e in ENGINES}
        self.selfsem = {}

    def set_selfsems(self, d):
        self.selfsem = d

    def add(self, eng, emit, waits=(), incs=(), touch=()):
        ws = [(s, v) for (s, v) in waits if v > 0]
        ics = list(incs)
        assert len(ics) <= 1, "one semaphore update per instruction"
        names = [n for n in (_apname(a) if not isinstance(a, str) else a
                             for a in touch) if n]
        if eng in self.AUTOSYNC and names:
            pend = self.pending[eng]
            extra = {}
            for n in names:
                if n in pend:
                    s, v = pend.pop(n)
                    if extra.get(id(s), (None, 0))[1] < v:
                        extra[id(s)] = (s, v)
            ws = ws + list(extra.values())
            if not ics and eng in self.selfsem:
                ics = [(self.selfsem[eng], 1)]
        self.ops[eng].append((ws, emit, ics))
        for s, k in ics:
            s.n += k
        if eng in self.AUTOSYNC and names and ics:
            marker = (ics[0][0], ics[0][0].n)
            pend = self.pending[eng]
            for n in names:
                pend[n] = marker

    def replay(self, eng_obj, name):
        for ws, emit, ics in self.ops[name]:
            for s, v in ws:
                eng_obj.wait_ge(s.h, v)
            ins = emit(eng_obj)
            for s, k in ics:
                ins.then_inc(s.h, k)


class Pool:

    def __init__(self, p, c, nslots):
        self.p = p
        self.c = c
        self.nslots = nslots
        self.uses = 0

    def acquire(self):
        i = self.uses % self.nslots
        need = self.uses - self.nslots + 1
        w = [(self.c, self.c.n - (self.uses - need))] if need > 0 else []
        self.uses += 1
        return i, w

    def pdone(self):
        return (self.p, 1)

    def cwait(self):
        return [(self.p, self.p.n)]

    def cdone(self):
        return (self.c, 1)


def build_nc(dbg=False):
    nc = bass.Bass("TRN2", target_bir_lowering=False,
                   detect_race_conditions=False)

    def din(name, shape):
        return nc.dram_tensor(name, list(shape), F32, kind="ExternalInput").ap()

    x_d = din("x", (T, D))
    gam_d = din("ln_gamma", (D,))
    bet_d = din("ln_beta", (D,))
    wqkv_d = din("W_qkv", (D, 3 * D))
    bqkv_d = din("b_qkv", (3 * D,))
    wprj_d = din("W_proj", (D, D))
    bprj_d = din("b_proj", (D,))
    rt_d = din("routing_token", (D,))
    lgam_d = din("light_gamma", (D,))
    lw1_d = din("light_w1", (D, FF_L))
    lb1_d = din("light_b1", (FF_L,))
    lw2_d = din("light_w2", (FF_L, D))
    lb2_d = din("light_b2", (D,))
    hgam_d = din("heavy_gamma", (D,))
    hw1_d = din("heavy_w1", (D, FF_H))
    hb1_d = din("heavy_b1", (FF_H,))
    hw2_d = din("heavy_w2", (FF_H, D))
    hb2_d = din("heavy_b2", (D,))
    idn_d = din("cc_ident", (128, 128))
    lt_d = din("cc_lt", (128, 128))
    iota_d = din("cc_iota", (128, 32))

    out_d = nc.dram_tensor("out", [T, D], F32, kind="ExternalOutput").ap()

    KD = "ExternalOutput" if dbg else "Internal"
    qT_h = nc.dram_tensor("qT_h", [D, T], BF16, kind=KD).ap()
    kT_h = nc.dram_tensor("kT_h", [D, T], BF16, kind=KD).ap()
    v_h = nc.dram_tensor("v_h", [128, 8, 32, 130], BF16, kind=KD).ap()
    z_h = nc.dram_tensor("z_h", [T, D], BF16, kind=KD).ap()
    zperm_h = nc.dram_tensor("zperm_h", [T, D], BF16, kind=KD).ap()
    x1_h = nc.dram_tensor("x1_h", [T, D], F32, kind=KD).ap()
    light_h = nc.dram_tensor("light_h", [T, D], BF16, kind=KD).ap()
    h2b_h = nc.dram_tensor("h2b_h", [T, D], BF16, kind=KD).ap()
    w1h16_h = nc.dram_tensor("w1h16_h", [D, FF_H], BF16, kind="Internal").ap()
    w2h16_h = nc.dram_tensor("w2h16_h", [FF_H, D], BF16, kind="Internal").ap()
    bias_h = nc.dram_tensor("bias_h", [D], F32, kind="Internal").ap()
    red_h = nc.dram_tensor("red_h", [128], F32, kind="Internal").ap()
    if dbg:
        dbg_s = nc.dram_tensor("dbg_s", [128, 32], F32, kind="ExternalOutput").ap()
        dbg_u = nc.dram_tensor("dbg_u", [128, 32], F32, kind="ExternalOutput").ap()
        dbg_dest = nc.dram_tensor("dbg_dest", [128, 32], F32, kind="ExternalOutput").ap()
        dbg_u8 = nc.dram_tensor("dbg_u8", [T, D], F32, kind="ExternalOutput").ap()
        dbg_xnT = nc.dram_tensor("dbg_xnT", [NBLK, 128, 8 * 512], BF16,
                                 kind="ExternalOutput").ap()

    S = Sched()
    es = ExitStack()

    def sem(name):
        return Sem(es.enter_context(nc.semaphore(name)))

    s_ldw = sem("ldw")
    g_ldw = sem("gldw")
    g_ldk = sem("gldk")
    g_msc = sem("gmsc")
    g_pcs = sem("gpcs")
    g_zf = sem("gzf")
    g_ldh = sem("gldh")
    g_sct = sem("gsct")
    g_fin = sem("gfin")
    g_fin2 = sem("gfin2")
    g_fin3 = sem("gfin3")
    g_fin4 = sem("gfin4")
    s_ldx = sem("ldx")
    s_stq = sem("stq")
    s_stq2 = sem("stq2")
    s_stq3 = sem("stq3")
    s_stq4 = sem("stq4")
    s_stv = sem("stv")
    s_stv2 = sem("stv2")
    s_stv3 = sem("stv3")
    s_stv4 = sem("stv4")
    s_ldk = sem("ldk")
    s_ldp = sem("ldp")
    s_stz = sem("stz")
    s_msc = sem("msc")
    s_ldh = sem("ldh")
    s_sct = sem("sct")
    s_fin = sem("fin")
    s_fl = sem("fl")
    s_out = sem("out")
    c_ev = sem("ev")
    c_ac = sem("ac")
    c_xfree = sem("xfree")
    c_qkv = sem("qkv")
    c_qkp = sem("qkp")
    c_exp = sem("exp")
    c_avg = sem("avg")
    pA_p, pA_c = sem("pA_p"), sem("pA_c")
    pB_p, pB_c = sem("pB_p"), sem("pB_c")
    pU_c = sem("pU_c")
    pG_p, pG_c = sem("pG_p"), sem("pG_c")
    v_self = sem("v_self")
    a_self = sem("a_self")
    g_self = sem("g_self")
    S.set_selfsems({"vector": v_self, "scalar": a_self, "gpsimd": g_self})

    def sb(name, shape, dt):
        return es.enter_context(nc.sbuf_tensor(name, list(shape), dt))

    ident_f = sb("ident_f", (128, 128), F32)
    lt_f = sb("lt_f", (128, 128), F32)
    iota_f = sb("iota_f", (128, 32), F32)
    ones_r128 = sb("ones_r128", (1, 128), F32)
    half_r128 = sb("half_r128", (1, 128), F32)
    ones_r64 = sb("ones_r64", (1, 64), F32)
    ones_c128 = sb("ones_c128", (128, 1), F32)
    eps_t = sb("eps_t", (128, 1), F32)
    logk_t = sb("logk_t", (128, 1), F32)
    rt_bc = sb("rt_bc", (128, D), F32)
    row_g = sb("row_g", (1, D), F32)
    bqa = sb("bqa", (128, 24), F32)
    bkv = sb("bkv", (128, 24), F32)
    gam_c = sb("gam_c", (128, 8), F32)
    lgam_c = sb("lgam_c", (128, 8), F32)
    hgam_c = sb("hgam_c", (128, 8), F32)
    bprj_bc = sb("bprj_bc", (128, D), BF16)
    b1l = sb("b1l", (128, 4), F32)
    b2l_bc = sb("b2l_bc", (128, D), BF16)
    b1h = sb("b1h", (128, 32), F32)
    b2h_bc = sb("b2h_bc", (128, D), BF16)
    xa = sb("xa", (128, 4 * D), F32)
    t1 = sb("t1", (128, D), F32)
    w1k = sb("w1k", (128, 4 * D), F32)
    jnk = sb("jnk", (128, D), F32)
    u8 = sb("u8", (128, D), F32)
    x1t = sb("x1t", (128, D), F32)
    lb16 = sb("lb16", (128, D), BF16)
    out_t = sb("out_t", (128, D), F32)
    zro16 = sb("zro16", (128, 512), BF16)
    zsc = sb("zsc", (128, 2 * D), BF16)
    lstg = sb("lstg", (128, 2 * D), BF16)
    colA = sb("colA", (128, 1), F32)
    colB = sb("colB", (128, 1), F32)
    colC = sb("colC", (128, 1), F32)
    colD = sb("colD", (128, 1), F32)
    colE = sb("colE", (128, 1), F32)
    colF = sb("colF", (128, 1), F32)
    colG = sb("colG", (128, 1), F32)
    colI = sb("colI", (128, 1), F32)
    s_parts = sb("s_parts", (128, 32), F32)
    m_t = sb("m_t", (128, 32), F32)
    bb32 = sb("bb32", (128, 32), F32)
    tmp32 = sb("tmp32", (128, 32), F32)
    e32 = sb("e32", (128, 32), F32)
    u32 = sb("u32", (128, 32), F32)
    q32 = sb("q32", (128, 32), F32)
    um32 = sb("um32", (128, 32), F32)
    mku = sb("mku", (128, 32), U32)
    nbig32 = sb("nbig32", (128, 32), F32)
    eqm32 = sb("eqm32", (128, 32), F32)
    sel32 = sb("sel32", (128, 32), F32)
    rkp32 = sb("rkp32", (128, 32), F32)
    rk32 = sb("rk32", (128, 32), F32)
    dest_f = sb("dest_f", (128, 32), F32)
    dest_i = sb("dest_i", (128, 32), I32)
    row32a = sb("row32a", (1, 32), F32)
    row32b = sb("row32b", (1, 32), F32)
    row32c = sb("row32c", (1, 32), F32)
    row128 = sb("row128", (1, 128), F32)
    sc_a = sb("sc_a", (1, 1), F32)
    sc_lo = sb("sc_lo", (1, 1), F32)
    sc_hi = sb("sc_hi", (1, 1), F32)
    sc_mid = sb("sc_mid", (1, 1), F32)
    sc_t1 = sb("sc_t1", (1, 1), F32)
    sc_t2 = sb("sc_t2", (1, 1), F32)
    sc_cnt = sb("sc_cnt", (1, 1), F32)
    sc_flg = sb("sc_flg", (1, 1), F32)
    sc_vs = sb("sc_vs", (1, 1), F32)
    sc_teq = sb("sc_teq", (1, 1), F32)
    sc_fu = sb("sc_fu", (1, 1), U32)
    sc_nu = sb("sc_nu", (1, 1), U32)

    def psum(name, shape, dt):
        return es.enter_context(nc.psum_tensor(name, list(shape), dt))

    ps_stA = psum("ps_stA", (128, 1024), F32)
    ps_stB = psum("ps_stB", (128, 1024), F32)
    ps_u = psum("ps_u", (65, 512), F32)
    ps_gen = psum("ps_gen", (128, 1536), F32)

    def dma(q, out_ap, in_ap, sem_, waits=(), slow=False):
        def emit(e, o=out_ap, i=in_ap, sl=slow):
            if sl:
                return e.dma_start(out=o, in_=i, allow_slow_non_contiguous=True)
            return e.dma_start(out=o, in_=i)
        S.add(q, emit, waits=waits, incs=[(sem_, 16)], touch=[out_ap, in_ap])

    def mm(lhsT, rhs, out, start, stop, waits=(), incs=()):
        S.add("tensor",
              lambda e, l=lhsT, r=rhs, o=out, st=start, sp=stop:
                  e.matmul(o, l, r, start=st, stop=sp),
              waits=waits, incs=incs)

    def ts(eng, out, in0, s1, s2, op0, op1=None, accum=None, waits=(), incs=()):
        if accum is not None and op1 is None:
            op1, s2 = ALU.add, None
        def emit(e, o=out, i=in0, a=s1, b=s2, p0=op0, p1=op1, ac=accum):
            kw = {}
            if ac is not None:
                kw["accum_out"] = ac
            if p1 is None:
                return e.tensor_scalar(o, i, a, None, p0, **kw)
            return e.tensor_scalar(o, i, a, b, p0, p1, **kw)
        tch = [out, in0] + [a for a in (s1, s2, accum) if not isinstance(a, (float, int, type(None)))]
        S.add(eng, emit, waits=waits, incs=incs, touch=tch)

    def tt(eng, out, in0, in1, op, waits=(), incs=()):
        S.add(eng, lambda e, o=out, a=in0, b=in1, p=op: e.tensor_tensor(o, a, b, p),
              waits=waits, incs=incs, touch=[out, in0, in1])

    def stt(eng, out, in0, sc, in1, op0, op1, accum=None, waits=(), incs=()):
        def emit(e, o=out, i0=in0, s=sc, i1=in1, p0=op0, p1=op1, ac=accum):
            kw = {}
            if ac is not None:
                kw["accum_out"] = ac
            return e.scalar_tensor_tensor(o, i0, s, i1, p0, p1, **kw)
        tch = [out, in0, in1] + [a for a in (sc, accum) if not isinstance(a, (float, int, type(None)))]
        S.add(eng, emit, waits=waits, incs=incs, touch=tch)

    def act(out, in_, func, bias=None, scale=None, accum=None, waits=(), incs=()):
        def emit(e, o=out, i=in_, f=func, b=bias, sc=scale, ac=accum):
            kw = {}
            if b is not None:
                kw["bias"] = b
            if sc is not None:
                kw["scale"] = sc
            if ac is not None:
                kw["accum_out"] = ac
            return e.activation(out=o, in_=i, func=f, **kw)
        tch = [out, in_] + [a for a in (bias, scale, accum) if not isinstance(a, (float, int, type(None)))]
        S.add("scalar", emit, waits=waits, incs=incs, touch=tch)

    def cp(eng, out, in_, waits=(), incs=()):
        S.add(eng, lambda e, o=out, i=in_: e.tensor_copy(out=o, in_=i),
              waits=waits, incs=incs, touch=[out, in_])

    def recip(out, in_, waits=(), incs=()):
        S.add("vector", lambda e, o=out, i=in_: e.reciprocal(out=o, in_=i),
              waits=waits, incs=incs, touch=[out, in_])

    def red(out, in_, op, waits=(), incs=()):
        S.add("vector", lambda e, o=out, i=in_, p=op: e.tensor_reduce(o, i, AX.X, p),
              waits=waits, incs=incs, touch=[out, in_])

    def mset(eng, ap, val, waits=(), incs=()):
        S.add(eng, lambda e, a=ap, v=val: e.memset(a, v), waits=waits, incs=incs,
              touch=[ap])

    def transpose(out, in_, waits=(), incs=()):
        S.add("tensor",
              lambda e, o=out, i=in_: e.transpose(out=o, in_=i, identity=ident_f[:]),
              waits=waits, incs=incs)

    def waitonly(eng, waits):
        S.add(eng, lambda e: None, waits=waits)

    def gslc(i, rows=128, cols=512):
        return ps_gen[0:rows, i * 512:i * 512 + cols]

    def bcast_row_b(src_row_dram, dst, extraw=()):
        dma("sync", row_g[:], src_row_dram.rearrange("(o d) -> o d", o=1),
            s_msc, waits=list(extraw) + [(pG_p, pG_p.n)])
        ldn = s_msc.n
        for half in range(2):
            gi, w = poolG.acquire()
            mm(ones_r128[:, 0:128], row_g[0:1, half * 512:(half + 1) * 512],
               gslc(gi), start=True, stop=True,
               waits=w + [(s_msc, ldn)], incs=[poolG.pdone()])
            cp("vector", dst[:, half * 512:(half + 1) * 512], gslc(gi),
               waits=poolG.cwait(), incs=[poolG.cdone()])

    pT_p, pT_c = sem("pT_p"), sem("pT_c")
    poolG = Pool(pG_p, pG_c, 3)

    with nc.sbuf_tensor("Wq_sb", [128, 8 * 3072], BF16) as Wq_sb, \
         nc.sbuf_tensor("beta16", [128, 8], BF16) as beta16, \
         nc.sbuf_tensor("xnT", [128, 2 * 8 * 512], BF16) as xnT, \
         nc.sbuf_tensor("vstage", [128, 2080], BF16) as vstage, \
         nc.sbuf_tensor("bvb", [128, 1024], BF16) as bvb, \
         nc.sbuf_tensor("rowD", [1, D], F32) as rowD, \
         nc.sbuf_tensor("wcast", [128, FF_H], BF16) as wcast, \
         nc.sbuf_tensor("stg", [128, 4 * 512], BF16) as stg, \
         nc.sbuf_tensor("u8b", [128, D], F32) as u8b, \
         nc.sbuf_tensor("t14", [128, 4 * D], F32) as t14, \
         nc.sbuf_tensor("cA4", [128, 4], F32) as cA4, \
         nc.sbuf_tensor("cB4", [128, 4], F32) as cB4, \
         nc.sbuf_tensor("cC4", [128, 4], F32) as cC4, \
         nc.sbuf_tensor("cE4", [128, 4], F32) as cE4, \
         nc.sbuf_tensor("cF4", [128, 4], F32) as cF4, \
         nc.sbuf_tensor("cG4", [128, 4], F32) as cG4:

        for dc in range(DC):
            dma("gpsimd", Wq_sb[:, dc * 3072:(dc + 1) * 3072],
                wqkv_d[dc * 128:(dc + 1) * 128, :], g_ldw)
        dma("sync", ident_f[:], idn_d, s_ldw)
        dma("sync", lt_f[:], lt_d, s_ldw)
        dma("sync", iota_f[:], iota_d, s_ldw)
        dma("sync", gam_c[:], gam_d.rearrange("(c p) -> p c", p=128), s_ldw, slow=True)
        dma("sync", lgam_c[:], lgam_d.rearrange("(c p) -> p c", p=128), s_ldw, slow=True)
        dma("sync", hgam_c[:], hgam_d.rearrange("(c p) -> p c", p=128), s_ldw, slow=True)
        dma("gpsimd", beta16[:], bet_d.rearrange("(c p) -> p c", p=128), g_ldw, slow=True)
        dma("sync", bkv[:], bqkv_d.rearrange("(c p) -> p c", p=128), s_ldw, slow=True)
        dma("sync", b1l[:], lb1_d.rearrange("(c p) -> p c", p=128), s_ldw, slow=True)
        dma("sync", b1h[:], hb1_d.rearrange("(c p) -> p c", p=128), s_ldw, slow=True)
        ldw0 = s_ldw.n
        gldw0 = g_ldw.n

        mset("vector", ones_r128[:], 1.0)
        mset("vector", half_r128[:], 0.5)
        mset("vector", ones_r64[:], 1.0)
        mset("vector", ones_c128[:], 1.0)
        mset("vector", eps_t[:], EPS)
        mset("vector", logk_t[:], LOGK)
        mset("vector", zro16[:], 0.0)
        mset("vector", nbig32[:], -3.0e38)
        mset("vector", sc_hi[:], 1.0)
        mset("vector",
             vstage[:].rearrange("p (h c) -> p h c", c=65)[:, :, 64:65], 1.0,
             incs=[(c_ev, 1)])
        msets_done = c_ev.n

        for j in range(24):
            for hh in range(2):
                dma("gpsimd",
                    h2b_h[NSEL + j * 128: NSEL + (j + 1) * 128,
                          hh * 512:(hh + 1) * 512],
                    zro16[:], g_zf, waits=[(c_ev, msets_done)])
        zfill_cnt = g_zf.n

        for oc in range(24):
            gi, w = poolG.acquire()
            for dc in range(DC):
                mm(Wq_sb[:, dc * 3072 + oc * 128: dc * 3072 + (oc + 1) * 128],
                   beta16[:, dc:dc + 1], gslc(gi, 128, 1),
                   start=(dc == 0), stop=(dc == DC - 1),
                   waits=(w + [(s_ldw, ldw0), (g_ldw, gldw0)]
                          if dc == 0 else ()),
                   incs=([poolG.pdone()] if dc == DC - 1 else ()))
            tt("vector", bqa[:, oc:oc + 1], gslc(gi, 128, 1), bkv[:, oc:oc + 1],
               ALU.add, waits=poolG.cwait(), incs=[poolG.cdone()])
        bqa_done = pG_c.n

        for dc in range(DC):
            ts("vector", Wq_sb[:, dc * 3072:(dc + 1) * 3072],
               Wq_sb[:, dc * 3072:(dc + 1) * 3072], gam_c[:, dc:dc + 1], None,
               ALU.mult,
               waits=([(pG_c, bqa_done), (s_ldw, ldw0)] if dc == 0 else ()),
               incs=([(c_ev, 1)] if dc == DC - 1 else ()))
        wqscale_mark = c_ev.n

        dma("sync", bias_h.rearrange("(c p) -> p c", p=128), bqa[:, 16:24], s_msc,
            waits=[(pG_c, bqa_done)], slow=True)
        dma("sync", rowD[:], bias_h.rearrange("(o d) -> o d", o=1), s_msc,
            waits=[(s_msc, s_msc.n)])

        def bcast_row(src_row_dram, dst, via_rowD=False):
            if not via_rowD:
                dma("sync", rowD[:], src_row_dram.rearrange("(o d) -> o d", o=1),
                    s_msc, waits=[(pG_p, pG_p.n)])
            ldn = s_msc.n
            for half in range(2):
                gi, w = poolG.acquire()
                mm(ones_r128[:, 0:128], rowD[0:1, half * 512:(half + 1) * 512],
                   gslc(gi), start=True, stop=True,
                   waits=w + [(s_msc, ldn), (c_ev, msets_done)],
                   incs=[poolG.pdone()])
                cp("vector", dst[:, half * 512:(half + 1) * 512], gslc(gi),
                   waits=poolG.cwait(), incs=[poolG.cdone()])

        bcast_row(None, bvb, via_rowD=True)
        bcast_row(bprj_d, bprj_bc)
        bcast_row(lb2_d, b2l_bc)
        bcast_row(hb2_d, b2h_bc)
        bcast_row(rt_d, rt_bc)

        pA = Pool(pA_p, pA_c, 4)
        psQ = [ps_stA[:, 0:512], ps_stA[:, 512:1024],
               ps_stB[:, 0:512], ps_stB[:, 512:1024]]

        precast = [(hw1_d[dc * 128:(dc + 1) * 128, :],
                    w1h16_h[dc * 128:(dc + 1) * 128, :], FF_H) for dc in range(DC)]
        precast += [(hw2_d[fc * 128:(fc + 1) * 128, :],
                     w2h16_h[fc * 128:(fc + 1) * 128, :], D) for fc in range(32)]
        pc_i = [0]

        def emit_precast():
            if pc_i[0] >= len(precast):
                return
            src, dst, width = precast[pc_i[0]]
            pc_i[0] += 1
            dma("gpsimd", wcast[:, 0:width], src, g_msc,
                waits=[(g_pcs, g_pcs.n)])
            dma("gpsimd", dst, wcast[:, 0:width], g_pcs,
                waits=[(g_msc, g_msc.n)])

        stats_marks = {}

        def stats_part(tb):
            x0 = s_ldx.n
            for tt4 in range(4):
                c = tb * 4 + tt4
                slot = c % 4
                dma("scalar", xa[:, slot * D:(slot + 1) * D],
                    x_d[c * 128:(c + 1) * 128, :], s_ldx,
                    waits=[(c_xfree, max(0, c - 3))])
            for tt4 in range(4):
                c = tb * 4 + tt4
                slot = c % 4
                act(t14[:, tt4 * D:(tt4 + 1) * D], xa[:, slot * D:(slot + 1) * D],
                    AF.Identity, accum=cA4[:, tt4:tt4 + 1],
                    waits=[(s_ldx, x0 + 16 * (tt4 + 1))])
            act(cB4[:], cA4[:], AF.Identity, scale=-1.0 / D)
            for tt4 in range(4):
                c = tb * 4 + tt4
                slot = c % 4
                act(t14[:, tt4 * D:(tt4 + 1) * D], xa[:, slot * D:(slot + 1) * D],
                    AF.Identity, bias=cB4[:, tt4:tt4 + 1],
                    accum=cA4[:, tt4:tt4 + 1],
                    incs=[(c_xfree, 1)])
            for tt4 in range(4):
                act(jnk[:] if tt4 % 2 == 0 else t1[:],
                    t14[:, tt4 * D:(tt4 + 1) * D],
                    AF.Square, accum=cC4[:, tt4:tt4 + 1])
            act(cE4[:], cC4[:], AF.Ln, bias=eps_t[:, 0:1], scale=1.0 / D)
            act(cE4[:], cE4[:], AF.Exp, scale=-0.5)
            act(cG4[:], cA4[:], AF.Identity, scale=1.0 / D)
            act(cB4[:], cA4[:], AF.Identity, scale=-1.0 / D)
            for j in range(4):
                act(cF4[:, j:j + 1], cG4[:, j:j + 1], AF.Identity,
                    scale=cB4[:, j:j + 1])
            for j in range(4):
                act(cF4[:, j:j + 1], cC4[:, j:j + 1], AF.Identity,
                    scale=1.0 / D, bias=cF4[:, j:j + 1])
            for j in range(4):
                act(cF4[:, j:j + 1], cF4[:, j:j + 1], AF.Identity,
                    scale=cE4[:, j:j + 1])
            for j in range(4):
                act(cF4[:, j:j + 1], cF4[:, j:j + 1], AF.Identity,
                    scale=cE4[:, j:j + 1])
            act(cF4[:], cF4[:], AF.Ln, bias=eps_t[:, 0:1])
            act(cF4[:], cF4[:], AF.Exp, scale=-0.5)
            for j in range(4):
                act(cG4[:, j:j + 1], cE4[:, j:j + 1], AF.Identity,
                    scale=cF4[:, j:j + 1])
            for j in range(4):
                act(cA4[:, j:j + 1], cB4[:, j:j + 1], AF.Identity,
                    scale=cG4[:, j:j + 1])

        def trans_part(tb):
            xb = (tb % 2) * 4096
            xnT_v = xnT[:, xb:xb + 4096].rearrange("p (dc q) -> p dc q", q=512)
            for tt4 in range(4):
                ui = tt4 % 2
                ub = u8 if ui == 0 else u8b
                act(ub[:], t14[:, tt4 * D:(tt4 + 1) * D], AF.Identity,
                    scale=cG4[:, tt4:tt4 + 1], bias=cA4[:, tt4:tt4 + 1],
                    waits=([(pG_p, u8_marks[ui])]
                           if u8_marks.get(ui) else []),
                    incs=[(c_ac, 1)])
                acu = c_ac.n
                for grp in range(2):
                    gi, w = poolG.acquire()
                    for j in range(4):
                        transpose(ps_gen[:, gi * 512 + j * 128: gi * 512 + (j + 1) * 128],
                                  ub[:, (grp * 4 + j) * 128:(grp * 4 + j + 1) * 128],
                                  waits=(w + [(c_ac, acu)] +
                                         ([(s_ldw, ldw0), (c_qkv, max(0, tb - 1))]
                                          if (tt4 == 0 and grp == 0) else [])
                                         if j == 0 else ()),
                                  incs=([poolG.pdone()] if j == 3 else ()))
                    cp("vector",
                       xnT_v[:, grp * 4:(grp + 1) * 4, tt4 * 128:(tt4 + 1) * 128],
                       gslc(gi).rearrange("p (dc q) -> p dc q", q=128),
                       waits=poolG.cwait(), incs=[poolG.cdone()])
                u8_marks[ui] = pG_p.n
                emit_precast()

        def qkv_part(tb):
            xb = (tb % 2) * 4096
            xnT_mark = qkv_marks[tb]
            for which, dst, boff in ((0, qT_h, 0), (1, kT_h, 8)):
                for oc in range(DC):
                    i, w = pA.acquire()
                    pslc = psQ[i]
                    for dc in range(DC):
                        mm(Wq_sb[:, dc * 3072 + (boff + oc) * 128:
                                 dc * 3072 + (boff + oc + 1) * 128],
                           xnT[:, xb + dc * 512: xb + (dc + 1) * 512], pslc,
                           start=(dc == 0), stop=(dc == DC - 1),
                           waits=(w + [(pG_c, xnT_mark)] +
                                  ([(c_ev, wqscale_mark)]
                                   if (tb == 0 and boff == 0 and oc == 0)
                                   else []) if dc == 0 else ()),
                           incs=([pA.pdone()] if dc == DC - 1 else ()))
                    qu = qk_u[0]
                    ssl = stg[:, (qu % 4) * 512:(qu % 4 + 1) * 512]
                    ts("vector", ssl, pslc, bqa[:, boff + oc: boff + oc + 1],
                       None, ALU.add,
                       waits=pA.cwait() + ([qk_stok[qu - 4]] if qu >= 4 else []),
                       incs=[pA.cdone()])
                    qsm = qk_sems[qu % 4]
                    dma("sync", dst[oc * 128:(oc + 1) * 128, tb * 512:(tb + 1) * 512],
                        ssl, qsm, waits=[(pA_c, pA_c.n)])
                    qk_stok.append((qsm, qsm.n))
                    qk_u[0] += 1
            for tt4 in range(4):
                g = tb * 4 + tt4
                for half in range(2):
                    i, w = pA.acquire()
                    pslc = psQ[i]
                    for dc in range(DC):
                        mm(xnT[:, xb + dc * 512 + tt4 * 128:
                               xb + dc * 512 + (tt4 + 1) * 128],
                           Wq_sb[:, dc * 3072 + 2048 + half * 512:
                                 dc * 3072 + 2048 + (half + 1) * 512],
                           pslc, start=(dc == 0), stop=(dc == DC - 1),
                           waits=(w if dc == 0 else ()),
                           incs=([pA.pdone()] if dc == DC - 1 else ()))
                    vu = v_u[0]
                    vbase = (vu % 4) * 520
                    vdst = vstage[:, vbase:vbase + 520].rearrange(
                        "p (h c) -> p h c", c=65)[:, :, 0:64]
                    psrc = pslc.rearrange("p (h c) -> p h c", c=64)
                    bsrc = bvb[:, half * 512:(half + 1) * 512].rearrange(
                        "p (h c) -> p h c", c=64)
                    S.add("vector",
                          lambda e, o=vdst, a=psrc, b=bsrc:
                              e.tensor_tensor(o, a, b, ALU.add),
                          waits=pA.cwait() + [(c_ev, msets_done)] +
                                ([v_stok[vu - 4]] if vu >= 4 else []),
                          incs=[pA.cdone()])
                    vsm = v_sems[vu % 4]
                    dma("sync",
                        v_h[:, half * 4:(half + 1) * 4, g:g + 1, :],
                        vstage[:, vbase:vbase + 520].rearrange(
                            "p (h o c) -> p h o c", h=4, o=1),
                        vsm, waits=[(pA_c, pA_c.n)])
                    v_stok.append((vsm, vsm.n))
                    v_u[0] += 1
            S.add("tensor", lambda e: e.nop(), incs=[(c_qkv, 1)])

        qkv_marks = {}
        u8_marks = {}
        qk_sems = [s_stq, s_stq2, s_stq3, s_stq4]
        v_sems = [s_stv, s_stv2, s_stv3, s_stv4]
        qk_stok, v_stok = [], []
        qk_u, v_u = [0], [0]
        for tb in range(NBLK):
            stats_part(tb)
            if tb >= 1:
                qkv_part(tb - 1)
            trans_part(tb)
            qkv_marks[tb] = pG_c.n
        qkv_part(NBLK - 1)
        while pc_i[0] < len(precast):
            emit_precast()
        phaseA_qkv = ([(s, s.n) for s in (s_stq, s_stq2, s_stq3, s_stq4)] +
                      [(s, s.n) for s in (s_stv, s_stv2, s_stv3, s_stv4)])
        phaseA_msc = s_msc.n
        phaseA_gmsc = g_msc.n

    with nc.sbuf_tensor("Wp_sb", [128, 8 * 1024], BF16) as Wp_sb, \
         nc.sbuf_tensor("W1l_sb", [128, 8 * 512], BF16) as W1l_sb, \
         nc.sbuf_tensor("W2l_sb", [128, 4 * 1024], BF16) as W2l_sb, \
         nc.sbuf_tensor("kTs", [128, 3 * T], BF16) as kTs, \
         nc.sbuf_tensor("vs", [128, 3 * 32 * 130], BF16) as vs, \
         nc.sbuf_tensor("qpad", [128, 2 * 512], BF16) as qpad, \
         nc.sbuf_tensor("eT", [128, 6 * 512], BF16) as eT, \
         nc.sbuf_tensor("cxT", [128, 8 * 512], BF16) as cxT, \
         nc.sbuf_tensor("zTl", [128, 8 * 512], BF16) as zTl, \
         nc.sbuf_tensor("h1l", [128, 4 * 512], BF16) as h1l, \
         nc.sbuf_tensor("u_sb", [65, 2 * 512], F32) as u_sb, \
         nc.sbuf_tensor("rsg", [1, 2 * 512], F32) as rsg:

        wB = [(c_qkv, NBLK), (s_msc, phaseA_msc), (g_msc, phaseA_gmsc)]
        for dc in range(DC):
            dma("gpsimd", Wp_sb[:, dc * 1024:(dc + 1) * 1024],
                wprj_d[dc * 128:(dc + 1) * 128, :], g_ldw,
                waits=(wB if dc == 0 else ()))
        for dc in range(DC):
            dma("gpsimd", W1l_sb[:, dc * 512:(dc + 1) * 512],
                lw1_d[dc * 128:(dc + 1) * 128, :], g_ldw)
        for fc in range(4):
            dma("gpsimd", W2l_sb[:, fc * 1024:(fc + 1) * 1024],
                lw2_d[fc * 128:(fc + 1) * 128, :], g_ldw)
        ldw1 = g_ldw.n
        for dc in range(DC):
            ts("vector", W1l_sb[:, dc * 512:(dc + 1) * 512],
               W1l_sb[:, dc * 512:(dc + 1) * 512], lgam_c[:, dc:dc + 1], None,
               ALU.mult,
               waits=([(g_ldw, ldw1), (s_ldw, ldw0)] if dc == 0 else ()),
               incs=([(c_ev, 1)] if dc == DC - 1 else ()))
        w1l_mark = c_ev.n
        mset("vector", qpad[:], 0.0, waits=wB, incs=[(c_ev, 1)])
        qpad_z = c_ev.n

        gidx = [0]
        pend_tail = []
        zt_w = [0]
        pend_tstate = {}
        lout_mark = [pG_p.n]
        stok = {"x1": 0, "z": 0, "l0": 0, "l1": 0}
        GRP = [2] * 16
        GOF = [sum(GRP[:i]) for i in range(len(GRP))]
        last_z_act = [0]
        prj_mark = [pG_p.n]
        lgt_mark = [pG_p.n]

        for qb in range(NBLK):
            if pend_tail:
                xldns = []
                for tt4 in range(4):
                    c = (qb - 1) * 4 + tt4
                    slot = c % 4
                    dma("sync", xa[:, slot * D:(slot + 1) * D],
                        x_d[c * 128:(c + 1) * 128, :], s_ldx,
                        waits=[(c_xfree, NT + max(0, c - 3))])
                    xldns.append(s_ldx.n)
                pend_tstate["xldn"] = xldns
            def kv_loads(hp):
                idx = qb * 8 + hp
                wk = [(pU_c, max(0, 2 * idx - 4))]
                if idx == 0:
                    wk = wk + phaseA_qkv + [(c_qkv, NBLK)]
                dma("sync", kTs[:, (idx % 3) * T:(idx % 3 + 1) * T],
                    kT_h[hp * 128:(hp + 1) * 128, :], s_ldk, waits=wk)
                dma("gpsimd", vs[:, (idx % 3) * 4160:(idx % 3 + 1) * 4160],
                    v_h[:, hp, :, :].rearrange("p g c -> p (g c)"), g_ldk,
                    waits=wk)

            def qp_load(h):
                hidx = qb * NH + h
                par = h % 2
                wq = [(c_qkp, 16 * max(0, hidx - 1)), (c_ev, qpad_z)]
                if hidx == 0:
                    wq = wq + [phaseA_qkv[qi] for qi in range(4)]
                dma("gpsimd",
                    qpad[par * 64:(par + 1) * 64, par * 512:(par + 1) * 512],
                    qT_h[h * 64:(h + 1) * 64, qb * 512:(qb + 1) * 512],
                    s_ldp, waits=wq)

            ld_sched = {}
            for hp in range(8):
                ld_sched.setdefault(max(0, 32 * hp - 16), []).append(
                    lambda hp=hp: kv_loads(hp))
            for h in range(NH):
                ld_sched.setdefault(max(0, 16 * h - 10), []).append(
                    lambda h=h: qp_load(h))

            units = [(h, g) for h in range(NH) for g in range(16)]
            head_ap = {}
            for h in range(NH):
                hp = h // 2
                idx = qb * 8 + hp
                head_ap[h] = (
                    kTs[:, (idx % 3) * T:(idx % 3 + 1) * T],
                    vs[:, (idx % 3) * 4160:(idx % 3 + 1) * 4160],
                    qpad[:, (h % 2) * 512:(h % 2 + 1) * 512],
                    16 * (qb * 8 + hp + 1),
                    16 * (qb * NH + h + 1),
                )
            g0 = gidx[0]

            def emit_qk(u):
                h, g = units[u]
                kslc, vslc, qslc, ldk_need, ldp_need = head_ap[h]
                gg = g0 + u
                pst = ps_stA if gg % 2 == 0 else ps_stB
                gw = [(c_exp, max(0, gg - 1)), (s_ldk, ldk_need),
                      (g_ldk, ldk_need), (s_ldp, ldp_need)]
                for j in range(GRP[g]):
                    kt = GOF[g] + j
                    mm(kslc[:, kt * 128:(kt + 1) * 128], qslc,
                       pst[:, j * 512:(j + 1) * 512], start=True, stop=True,
                       waits=(gw if j == 0 else ()),
                       incs=([(c_qkp, 1)] if j == GRP[g] - 1 else ()))

            def emit_exp(u):
                h, g = units[u]
                gg = g0 + u
                pst = ps_stA if gg % 2 == 0 else ps_stB
                ring = (gg % 3) * 1024
                act(eT[:, ring:ring + GRP[g] * 512], pst[:, 0:GRP[g] * 512],
                    AF.Exp, scale=SCALE,
                    waits=[(c_qkp, gg + 1), (c_avg, max(0, gg - 2))],
                    incs=[(c_exp, 1)])

            def emit_av(u):
                h, g = units[u]
                kslc, vslc, qslc, _, _ = head_ap[h]
                hidx = qb * NH + h
                par = h % 2
                gg = g0 + u
                ring = (gg % 3) * 1024
                for j in range(GRP[g]):
                    kt = GOF[g] + j
                    avw = [(c_exp, gg + 1)] if j == 0 else []
                    if g == 0 and j == 0:
                        avw = avw + [(pU_c, hidx)]
                    mm(vslc[:, kt * 130 + par * 65: kt * 130 + (par + 1) * 65],
                       eT[:, ring + j * 512: ring + (j + 1) * 512],
                       ps_u[0:65, :], start=(kt == 0), stop=(kt == 31),
                       waits=avw,
                       incs=([(c_avg, 1)] if j == GRP[g] - 1 else ()))

            pend_norm = []

            def emit_norm_dve(h):
                hidx = qb * NH + h
                sl = (h % 2) * 512
                cp("vector", u_sb[:, sl:sl + 512], ps_u[0:65, :],
                   waits=[(c_avg, g0 + 16 * (h + 1))], incs=[(pU_c, 1)])
                recip(rsg[:, sl:sl + 512], u_sb[64:65, sl:sl + 512],
                      incs=[(c_ev, 1)])
                pend_norm.append((h, c_ev.n))

            def emit_norm_mm(h, rs_done):
                hp = h // 2
                par = h % 2
                sl = (h % 2) * 512
                gi, wg = poolG.acquire()
                mm(ones_r64[:, 0:64], rsg[0:1, sl:sl + 512], gslc(gi, 64, 512),
                   start=True, stop=True, waits=wg + [(c_ev, rs_done)],
                   incs=[poolG.pdone()])
                tt("vector", cxT[par * 64:(par + 1) * 64, hp * 512:(hp + 1) * 512],
                   u_sb[0:64, sl:sl + 512], gslc(gi, 64, 512), ALU.mult,
                   waits=poolG.cwait() + [(pG_p, prj_mark[0])],
                   incs=[poolG.cdone()])

            tail_sched = {}
            for k in range(8):
                tail_sched[1 + k] = k
            for k in range(4):
                tail_sched[20 + 14 * k] = 8 + k
                tail_sched[24 + 14 * k] = 12 + k
                tail_sched[27 + 14 * k] = 16 + k
                tail_sched[30 + 14 * k] = 20 + k
                tail_sched[33 + 14 * k] = 24 + k
            for k in range(4):
                tail_sched[84 + 3 * k] = 28 + k
            for k in range(4):
                tail_sched[100 + 14 * k] = 32 + k
            for u in range(len(units)):
                for f in ld_sched.get(u, ()):
                    f()
                emit_qk(u)
                emit_exp(u)
                if u >= 1:
                    emit_av(u - 1)
                    if units[u - 1][1] == 15:
                        emit_norm_dve(units[u - 1][0])
                while pend_norm and pend_norm[0][0] <= units[u][0] - 2:
                    emit_norm_mm(*pend_norm.pop(0))
                if u in tail_sched and tail_sched[u] < len(pend_tail):
                    pend_tail[tail_sched[u]]()
                    pend_tail[tail_sched[u]] = None
            emit_av(len(units) - 1)
            emit_norm_dve(NH - 1)
            while pend_norm:
                emit_norm_mm(*pend_norm.pop(0))
            gidx[0] += len(units)
            cx_mark = pG_c.n

            def mk_tail(qb, cx_mark):
                pieces = []
                tstate = {"zt_mark": 0, "gelu_mark": 0}

                def proj_piece(tt4, oc):
                    def f():
                        c = qb * 4 + tt4
                        gi, w = poolG.acquire()
                        for dcin in range(DC):
                            mm(cxT[:, dcin * 512 + tt4 * 128:
                                   dcin * 512 + (tt4 + 1) * 128],
                               Wp_sb[:, dcin * 1024 + oc * 512:
                                     dcin * 1024 + (oc + 1) * 512],
                               gslc(gi), start=(dcin == 0),
                               stop=(dcin == DC - 1),
                               waits=(w + [(pG_c, cx_mark), (g_ldw, ldw1)]
                                      if dcin == 0 else ()),
                               incs=([poolG.pdone()] if dcin == DC - 1 else ()))
                        if tt4 == 3 and oc == 1:
                            prj_mark[0] = pG_p.n
                        tt("vector", w1k[:, tt4 * D + oc * 512:
                                         tt4 * D + (oc + 1) * 512],
                           gslc(gi),
                           bprj_bc[:, oc * 512:(oc + 1) * 512], ALU.add,
                           waits=poolG.cwait(), incs=[poolG.cdone()])
                    return f

                def x1z_s1(tt4):
                    def f():
                        c = qb * 4 + tt4
                        slot = c % 4
                        xs = xa[:, slot * D:(slot + 1) * D]
                        if "xldn" in tstate:
                            xldn = tstate["xldn"][tt4]
                        else:
                            dma("sync", xs, x_d[c * 128:(c + 1) * 128, :], s_ldx,
                                waits=[(c_xfree, NT + max(0, c - 3))])
                            xldn = s_ldx.n
                        tstate[("xldn2", tt4)] = xldn
                        wk = w1k[:, tt4 * D:(tt4 + 1) * D]
                        red(colA[:], wk, ALU.add)
                        ts("vector", colA[:], colA[:], 1.0 / D, None, ALU.mult)
                        ts("vector", t1[:], wk, colA[:, 0:1], None, ALU.subtract,
                           waits=[(c_ac, last_z_act[0])])
                        stt("vector", jnk[:], t1[:], 1.0, t1[:], ALU.mult,
                            ALU.mult, accum=colC[:, 0:1], incs=[(c_ev, 1)])
                        tstate[("m1", tt4)] = c_ev.n
                    return f

                def x1z_s2(tt4):
                    def f():
                        c = qb * 4 + tt4
                        slot = c % 4
                        xs = xa[:, slot * D:(slot + 1) * D]
                        xldn = tstate[("xldn2", tt4)]
                        act(colD[:], colC[:], AF.Ln, bias=eps_t[:, 0:1],
                            scale=1.0 / D, waits=[(c_ev, tstate[("m1", tt4)])])
                        act(colD[:], colD[:], AF.Exp, scale=-0.5,
                            incs=[(c_ac, 1)])
                        m2 = c_ac.n
                        stt("vector", x1t[:], t1[:], colD[:, 0:1], xs, ALU.mult,
                            ALU.add,
                            waits=[(s_ldx, xldn), (s_stz, stok["x1"]),
                                   (c_ac, m2)],
                            incs=[(c_xfree, 1)])
                        dma("sync", x1_h[c * 128:(c + 1) * 128, :], x1t[:],
                            s_stz, waits=[(c_xfree, c_xfree.n)])
                        stok["x1"] = s_stz.n
                        red(colA[:], x1t[:], ALU.add)
                        ts("vector", colA[:], colA[:], 1.0 / D, None, ALU.mult)
                        ts("vector", t1[:], x1t[:], colA[:, 0:1], None,
                           ALU.subtract)
                        stt("vector", jnk[:], t1[:], 1.0, t1[:], ALU.mult,
                            ALU.mult, accum=colC[:, 0:1], incs=[(c_ev, 1)])
                        tstate[("m3", tt4)] = c_ev.n
                    return f

                def x1z_s3(tt4):
                    def f():
                        dvy = tstate[("m3", tt4)]
                        act(colD[:], colC[:], AF.Ln, bias=eps_t[:, 0:1],
                            scale=1.0 / D, waits=[(c_ev, dvy)])
                        act(colD[:], colD[:], AF.Exp, scale=-0.5,
                            incs=[(c_ac, 1)])
                        act(colE[:], colC[:], AF.Ln)
                        act(colE[:], colE[:], AF.Exp, scale=0.5,
                            incs=[(c_ac, 1)])
                        m4 = c_ac.n
                        tt("vector", colE[:], colE[:], colD[:], ALU.mult,
                           waits=[(c_ac, m4)])
                        ts("vector", colE[:], colE[:], 1e-12, None, ALU.max)
                        recip(colE[:], colE[:])
                        ts("vector", colE[:], colE[:], 32.0, None, ALU.mult)
                        tt("vector", colE[:], colE[:], colD[:], ALU.mult,
                           incs=[(c_ev, 1)])
                        tstate[("dvz", tt4)] = c_ev.n
                    return f

                def x1z_s4(tt4):
                    def f():
                        c = qb * 4 + tt4
                        S.add("scalar",
                              lambda e, o=u8[:], i=t1[:], m=colE[:, 0:1]:
                                  e.mul(out=o, in_=i, mul=m),
                              waits=[(c_ev, tstate[("dvz", tt4)]),
                                     (pG_p, zt_w[0])],
                              incs=[(c_ac, 1)],
                              touch=[u8[:], t1[:], colE[:]])
                        acz = c_ac.n
                        last_z_act[0] = acz
                        tstate[("acz", tt4)] = acz
                        cp("vector", lb16[:], u8[:],
                           waits=[(c_ac, acz), (s_stz, stok["z"])],
                           incs=[(c_ev, 1)])
                        dma("sync", z_h[c * 128:(c + 1) * 128, :], lb16[:],
                            s_stz, waits=[(c_ev, c_ev.n)])
                        stok["z"] = s_stz.n
                        stt("vector", jnk[:], t1[:], 1.0, rt_bc[:], ALU.mult,
                            ALU.mult, accum=colF[:, 0:1])
                        tt("vector", colF[:], colF[:], colD[:], ALU.mult)
                        cp("vector", s_parts[:, c:c + 1], colF[:, 0:1])
                    return f

                def x1zT_piece(tt4):
                    def f():
                        acz = tstate[("acz", tt4)]
                        zT_v = zTl[:].rearrange("p (dc q) -> p dc q", q=512)
                        for grp in range(2):
                            gi2, w = poolG.acquire()
                            for j in range(4):
                                dc = grp * 4 + j
                                transpose(ps_gen[:, gi2 * 512 + j * 128:
                                                 gi2 * 512 + (j + 1) * 128],
                                          u8[:, dc * 128:(dc + 1) * 128],
                                          waits=(w + [(c_ac, acz)] +
                                                 ([(pG_p, lgt_mark[0])]
                                                  if (grp == 0 and tt4 == 0)
                                                  else [])
                                                 if j == 0 else ()),
                                          incs=([poolG.pdone()] if j == 3 else ()))
                            cp("vector",
                               zT_v[:, grp * 4:(grp + 1) * 4,
                                    tt4 * 128:(tt4 + 1) * 128],
                               gslc(gi2).rearrange("p (dc q) -> p dc q", q=128),
                               waits=poolG.cwait(), incs=[poolG.cdone()])
                        zt_w[0] = pG_p.n
                        if tt4 == 3:
                            tstate["zt_mark"] = pG_c.n
                    return f

                def lh1_piece(fc):
                    def f():
                        gi, w = poolG.acquire()
                        for dc in range(DC):
                            mm(W1l_sb[:, dc * 512 + fc * 128:
                                      dc * 512 + (fc + 1) * 128],
                               zTl[:, dc * 512:(dc + 1) * 512], gslc(gi),
                               start=(dc == 0), stop=(dc == DC - 1),
                               waits=(w + [(pG_c, tstate["zt_mark"]),
                                           (c_ev, w1l_mark), (pG_p, lgt_mark[0])]
                                      if dc == 0 else ()),
                               incs=([poolG.pdone()] if dc == DC - 1 else ()))
                        if fc == 3:
                            lgt_mark[0] = pG_p.n
                        act(h1l[:, fc * 512:(fc + 1) * 512], gslc(gi),
                            AF.Gelu, bias=b1l[:, fc:fc + 1],
                            waits=poolG.cwait() + [(pG_p, lout_mark[0])],
                            incs=[poolG.cdone()])
                        if fc == 3:
                            tstate["gelu_mark"] = pG_c.n
                    return f

                def lout_piece(tt4):
                    def f():
                        c = qb * 4 + tt4
                        ls = (tt4 % 2) * D
                        lkey = "l%d" % (tt4 % 2)
                        for oc in range(2):
                            gi, w = poolG.acquire()
                            for fc in range(4):
                                mm(h1l[:, fc * 512 + tt4 * 128:
                                       fc * 512 + (tt4 + 1) * 128],
                                   W2l_sb[:, fc * 1024 + oc * 512:
                                         fc * 1024 + (oc + 1) * 512],
                                   gslc(gi), start=(fc == 0),
                                   stop=(fc == 3),
                                   waits=(w + [(pG_c, tstate["gelu_mark"])]
                                          if fc == 0 else ()),
                                   incs=([poolG.pdone()] if fc == 3 else ()))
                            tt("vector", lstg[:, ls + oc * 512:ls + (oc + 1) * 512],
                               gslc(gi),
                               b2l_bc[:, oc * 512:(oc + 1) * 512], ALU.add,
                               waits=poolG.cwait() + [(s_stz, stok[lkey])],
                               incs=[poolG.cdone()])
                        dma("sync", light_h[c * 128:(c + 1) * 128, :],
                            lstg[:, ls:ls + D], s_stz, waits=[(pG_c, pG_c.n)])
                        stok[lkey] = s_stz.n
                        if tt4 == 3:
                            lout_mark[0] = pG_p.n
                    return f

                for tt4 in range(4):
                    for oc in range(2):
                        pieces.append(proj_piece(tt4, oc))
                for tt4 in range(4):
                    pieces.append(x1z_s1(tt4))
                for tt4 in range(4):
                    pieces.append(x1z_s2(tt4))
                for tt4 in range(4):
                    pieces.append(x1z_s3(tt4))
                for tt4 in range(4):
                    pieces.append(x1z_s4(tt4))
                for tt4 in range(4):
                    pieces.append(x1zT_piece(tt4))
                for fc in range(4):
                    pieces.append(lh1_piece(fc))
                for tt4 in range(4):
                    pieces.append(lout_piece(tt4))
                return pieces, tstate

            flush_order = list(range(8)) + \
                [8 + k0 + 4 * s0 for k0 in range(4) for s0 in range(5)] + \
                list(range(28, 36))
            for fi in flush_order:
                if fi < len(pend_tail) and pend_tail[fi] is not None:
                    pend_tail[fi]()
            pend_tail.clear()
            new_pieces, pend_tstate = mk_tail(qb, cx_mark)
            pend_tail.extend(new_pieces)
        for fi in (list(range(8)) +
                   [8 + k0 + 4 * s0 for k0 in range(4) for s0 in range(5)] +
                   list(range(28, 36))):
            if fi < len(pend_tail) and pend_tail[fi] is not None:
                pend_tail[fi]()
        pend_tail.clear()
        phaseB_stz = s_stz.n
        attn_done = [(pU_c, 2 * NBLK * 8), (pG_p, lgt_mark[0]),
                     (pG_c, pG_c.n), (c_exp, gidx[0])]

    ts("vector", bb32[:], s_parts[:], 1.0, None, ALU.mult, incs=[(c_ev, 1)])
    if dbg:
        dma("sync", dbg_s, s_parts[:], s_msc, waits=[(c_ev, c_ev.n)])

    def colsum_11(src_col, waits):
        gi, w = poolG.acquire()
        mm(src_col, ones_c128[:, 0:1], gslc(gi, 1, 1), start=True, stop=True,
           waits=w + list(waits), incs=[poolG.pdone()])
        return gi

    def bcast_col(src11, waits):
        gi, w = poolG.acquire()
        mm(ones_r128[:, 0:128], src11, gslc(gi, 128, 1), start=True, stop=True,
           waits=w + list(waits), incs=[poolG.pdone()])
        cp("vector", colB[:], gslc(gi, 128, 1), waits=poolG.cwait(),
           incs=[poolG.cdone()])

    relu_mark = [c_ac.n]
    for it in range(NITER):
        tt("vector", tmp32[:], s_parts[:], bb32[:], ALU.subtract,
           waits=[(c_ac, relu_mark[0])], incs=[(c_ev, 1)])
        act(e32[:], tmp32[:], AF.Exp, accum=colA[:, 0:1],
            waits=[(c_ev, c_ev.n)], incs=[(c_ac, 1)])
        gi = colsum_11(colA[:, 0:1], [(c_ac, c_ac.n)])
        act(sc_t1[:], gslc(gi, 1, 1), AF.Ln, waits=poolG.cwait(),
            incs=[poolG.cdone()])
        gi2, w2 = poolG.acquire()
        mm(ones_r128[:, 0:128], sc_t1[0:1, 0:1], gslc(gi2, 128, 1),
           start=True, stop=True, waits=w2 + [(pG_c, pG_c.n)],
           incs=[poolG.pdone()])
        act(colB[:], gslc(gi2, 128, 1), AF.Identity, scale=-1.0,
            bias=logk_t[:, 0:1],
            waits=poolG.cwait(), incs=[poolG.cdone()])
        act(bb32[:], s_parts[:], AF.Relu, bias=colB[:, 0:1],
            incs=[(c_ac, 1)])
        relu_mark[0] = c_ac.n
    ts("vector", u32[:], s_parts[:], colB[:, 0:1], None, ALU.add,
       waits=[(c_ac, c_ac.n)])
    tt("vector", u32[:], u32[:], bb32[:], ALU.subtract, incs=[(c_ev, 1)])
    u_done = c_ev.n
    if dbg:
        dma("sync", dbg_u, u32[:], s_msc, waits=[(c_ev, u_done)])
    act(q32[:], u32[:], AF.Exp, waits=[(c_ev, u_done)], incs=[(c_ac, 1)])
    ts("vector", tmp32[:], q32[:], -1.0, 1.0, ALU.mult, ALU.add,
       waits=[(c_ac, c_ac.n)])
    tt("vector", m_t[:], q32[:], tmp32[:], ALU.add)
    red(colA[:], u32[:], ALU.min, incs=[(c_ev, 1)])
    dma("sync", red_h.rearrange("(p o) -> p o", o=1), colA[:, 0:1], s_msc,
        waits=[(c_ev, c_ev.n)])
    dma("sync", row128[:], red_h.rearrange("(o p) -> o p", o=1), s_msc,
        waits=[(s_msc, s_msc.n)])
    red(sc_lo[:], row128[:], ALU.min, waits=[(s_msc, s_msc.n)])
    ts("vector", sc_lo[:], sc_lo[:], 1.0, None, ALU.subtract, incs=[(c_ev, 1)])
    for it in range(NBIS):
        tt("vector", sc_mid[:], sc_lo[:], sc_hi[:], ALU.add, incs=[(c_ev, 1)])
        gib, wb = poolG.acquire()
        mm(half_r128[:, 0:128], sc_mid[0:1, 0:1], gslc(gib, 128, 1),
           start=True, stop=True, waits=wb + [(c_ev, c_ev.n)],
           incs=[poolG.pdone()])
        act(colB[:], gslc(gib, 128, 1), AF.Identity,
            waits=poolG.cwait(), incs=[poolG.cdone()])
        ts("vector", tmp32[:], u32[:], colB[:, 0:1], None, ALU.is_gt,
           accum=colA[:, 0:1], waits=[(pG_c, pG_c.n)], incs=[(c_ev, 1)])
        gi = colsum_11(colA[:, 0:1], [(c_ev, c_ev.n)])
        act(sc_cnt[:], gslc(gi, 1, 1), AF.Identity, waits=poolG.cwait(),
            incs=[poolG.cdone()])
        ts("vector", sc_fu[:], sc_cnt[:], float(NSEL), None, ALU.is_ge,
           waits=[(pG_c, pG_c.n)])
        ts("vector", sc_nu[:], sc_cnt[:], float(NSEL), None, ALU.is_lt)
        S.add("vector",
              lambda e, o=sc_lo[:], m=sc_fu[:], d=colB[0:1, 0:1]:
                  e.copy_predicated(out=o, mask=m, data=d),
              touch=[sc_lo[:], sc_fu[:], colB[:]])
        S.add("vector",
              lambda e, o=sc_hi[:], m=sc_nu[:], d=colB[0:1, 0:1]:
                  e.copy_predicated(out=o, mask=m, data=d),
              touch=[sc_hi[:], sc_nu[:], colB[:]], incs=[(c_ev, 1)])
    bcast_col(sc_hi[0:1, 0:1], [(c_ev, c_ev.n)])
    ts("vector", mku[:], u32[:], colB[:, 0:1], None, ALU.is_gt)
    cp("vector", um32[:], u32[:])
    S.add("vector",
          lambda e: e.copy_predicated(out=um32[:], mask=mku[:], data=nbig32[:]),
          touch=[um32[:], mku[:], nbig32[:]])
    red(colA[:], um32[:], ALU.max, incs=[(c_ev, 1)])
    dma("sync", red_h.rearrange("(p o) -> p o", o=1), colA[:, 0:1], s_msc,
        waits=[(c_ev, c_ev.n)])
    dma("sync", row128[:], red_h.rearrange("(o p) -> o p", o=1), s_msc,
        waits=[(s_msc, s_msc.n)])
    red(sc_vs[:], row128[:], ALU.max, waits=[(s_msc, s_msc.n)],
        incs=[(c_ev, 1)])
    bcast_col(sc_vs[0:1, 0:1], [(c_ev, c_ev.n)])
    cp("vector", colI[:], colB[:])
    ts("vector", sel32[:], u32[:], colI[:, 0:1], None, ALU.is_gt,
       accum=colA[:, 0:1], incs=[(c_ev, 1)])
    gi = colsum_11(colA[:, 0:1], [(c_ev, c_ev.n)])
    cp("vector", sc_cnt[:], gslc(gi, 1, 1), waits=poolG.cwait(),
       incs=[poolG.cdone()])
    ts("vector", sc_teq[:], sc_cnt[:], -1.0, float(NSEL), ALU.mult, ALU.add)
    ts("vector", eqm32[:], u32[:], colI[:, 0:1], None, ALU.is_equal,
       incs=[(c_ev, 1)])

    def rank_and_prefix(mask32, out32):
        mk_done = c_ev.n
        gi, w = poolG.acquire()
        mm(lt_f[:], mask32[:], gslc(gi, 128, 32), start=True, stop=True,
           waits=w + [(c_ev, mk_done)], incs=[poolG.pdone()])
        cp("vector", rkp32[:], gslc(gi, 128, 32), waits=poolG.cwait(),
           incs=[poolG.cdone()])
        gi, w = poolG.acquire()
        mm(mask32[:], ones_c128[:, 0:1], gslc(gi, 32, 1), start=True, stop=True,
           waits=w + [(c_ev, mk_done)], incs=[poolG.pdone()])
        cp("vector", colA[0:32, 0:1], gslc(gi, 32, 1), waits=poolG.cwait(),
           incs=[poolG.cdone()])
        dma("sync", red_h[0:32].rearrange("(p o) -> p o", o=1), colA[0:32, 0:1],
            s_msc, waits=[(pG_c, pG_c.n)])
        dma("sync", row32a[:], red_h[0:32].rearrange("(o p) -> o p", o=1), s_msc,
            waits=[(s_msc, s_msc.n)])
        cp("vector", row32b[:], row32a[:], waits=[(s_msc, s_msc.n)])
        for k in (1, 2, 4, 8, 16):
            cp("vector", row32c[0:1, 0:k], row32b[0:1, 0:k])
            tt("vector", row32c[0:1, k:32], row32b[0:1, k:32],
               row32b[0:1, 0:32 - k], ALU.add)
            cp("vector", row32b[:], row32c[:])
        tt("vector", row32b[:], row32b[:], row32a[:], ALU.subtract,
           incs=[(c_ev, 1)])
        gi, w = poolG.acquire()
        mm(ones_r128[:, 0:128], row32b[0:1, :], gslc(gi, 128, 32), start=True,
           stop=True, waits=w + [(c_ev, c_ev.n)], incs=[poolG.pdone()])
        tt("vector", out32[:], rkp32[:], gslc(gi, 128, 32), ALU.add,
           waits=poolG.cwait(), incs=[poolG.cdone()])

    rank_and_prefix(eqm32, rk32)
    bcast_col(sc_teq[0:1, 0:1], [(v_self, v_self.n)])
    ts("vector", tmp32[:], rk32[:], colB[:, 0:1], None, ALU.is_lt)
    tt("vector", tmp32[:], tmp32[:], eqm32[:], ALU.mult)
    tt("vector", sel32[:], sel32[:], tmp32[:], ALU.add, incs=[(c_ev, 1)])
    rank_and_prefix(sel32, rk32)
    ts("vector", tmp32[:], sel32[:], -1.0, 1.0, ALU.mult, ALU.add)
    tt("vector", dest_f[:], iota_f[:], rk32[:], ALU.subtract)
    ts("vector", dest_f[:], dest_f[:], float(NSEL), None, ALU.add)
    tt("vector", dest_f[:], dest_f[:], tmp32[:], ALU.mult)
    tt("vector", tmp32[:], sel32[:], rk32[:], ALU.mult)
    tt("vector", dest_f[:], dest_f[:], tmp32[:], ALU.add)
    cp("vector", dest_i[:], dest_f[:], incs=[(c_ev, 1)])
    dest_done = c_ev.n
    if dbg:
        dma("sync", dbg_dest, dest_f[:], s_msc, waits=[(c_ev, dest_done)])

    with nc.sbuf_tensor("W64", [128, 8 * FF_H], BF16) as W64, \
         nc.sbuf_tensor("h1h", [128, 32 * 512], BF16) as h1h, \
         nc.sbuf_tensor("zsT", [128, 8 * 512], BF16) as zsT, \
         nc.sbuf_tensor("hstg", [128, 2 * D], BF16) as hstg:

        wF = attn_done + [(s_stz, phaseB_stz), (s_msc, s_msc.n),
                          (g_msc, g_msc.n), (c_ev, dest_done)]
        for c in range(NT):
            slot = c % 2
            wz = (list(wF) if c < 2 else
                  [(g_sct, 16 * (c - 1)), (c_ev, dest_done)])
            dma("sync", zsc[:, slot * D:(slot + 1) * D],
                z_h[c * 128:(c + 1) * 128, :], s_fin, waits=wz)
            S.add("gpsimd",
                  lambda e, o=zperm_h, i=zsc[:, slot * D:(slot + 1) * D],
                         ix=dest_i[:, c:c + 1]:
                      e.indirect_dma_start(out=o,
                                           out_offset=IndirectOffsetOnAxis(ix, 0),
                                           in_=i, in_offset=None),
                  waits=[(s_fin, s_fin.n), (c_ev, dest_done)],
                  incs=[(g_sct, 16)], touch=[zperm_h, zsc[:], dest_i[:]])
        sct_zp = g_sct.n

        def load_W(dram16, nchunk, width, waits):
            for k in range(nchunk):
                dma("sync", W64[:, k * width:(k + 1) * width],
                    dram16[k * 128:(k + 1) * 128, :], s_ldh,
                    waits=(list(waits) if k == 0 else ()))

        pA3 = Pool(pA_p, pA_c, 2)
        pB3 = Pool(pB_p, pB_c, 2)
        load_W(w1h16_h, DC, FF_H,
               attn_done + [(s_stz, phaseB_stz)])
        tp_mark = [pB_p.n]
        for sb_i in range(2):
            ldh_wa = s_ldh.n
            for tt4 in range(4):
                g = sb_i * 4 + tt4
                slot = g % 2
                dma("gpsimd", zsc[:, slot * D:(slot + 1) * D],
                    zperm_h[g * 128:(g + 1) * 128, :], g_ldh,
                    waits=[(g_sct, sct_zp)] +
                          ([(pB_p, tp_mark[0])] if g >= 2 else []))
                cp("vector", u8[:], zsc[:, slot * D:(slot + 1) * D],
                   waits=[(g_ldh, g_ldh.n)] +
                         (list(wF) if (sb_i == 0 and tt4 == 0) else []),
                   incs=[(c_ev, 1)])
                zbd = c_ev.n
                for dc in range(DC):
                    i, w = pB3.acquire()
                    transpose(ps_stB[:, i * 512:i * 512 + 128],
                              u8[:, dc * 128:(dc + 1) * 128],
                              waits=w + [(c_ev, zbd)],
                              incs=[pB3.pdone()])
                    ts("vector",
                       zsT[:, dc * 512 + tt4 * 128: dc * 512 + (tt4 + 1) * 128],
                       ps_stB[:, i * 512:i * 512 + 128], hgam_c[:, dc:dc + 1], None,
                       ALU.mult, waits=pB3.cwait(), incs=[pB3.cdone()])
                tp_mark[0] = pB_p.n
            zsT_mark = pB_c.n
            for fc in range(32):
                i, w = pA3.acquire()
                pslc = ps_stA[:, i * 512:(i + 1) * 512]
                for dc in range(DC):
                    mm(W64[:, dc * FF_H + fc * 128: dc * FF_H + (fc + 1) * 128],
                       zsT[:, dc * 512:(dc + 1) * 512], pslc,
                       start=(dc == 0), stop=(dc == DC - 1),
                       waits=((w + [(pB_c, zsT_mark)] if dc == 0 else []) +
                              ([(s_ldh, ldh_wa - 128 + 16 * (dc + 1))]
                               if fc == 0 else [])),
                       incs=([pA3.pdone()] if dc == DC - 1 else ()))
                act(h1h[:, fc * 512:(fc + 1) * 512], pslc, AF.Gelu,
                    bias=b1h[:, fc:fc + 1], waits=pA3.cwait(),
                    incs=[pA3.cdone()])
            h1mm_mark = pA_p.n
            h1_mark = pA_c.n
            load_W(w2h16_h, 32, D, [(pA_p, h1mm_mark)])
            ldh_wb = s_ldh.n
            for tt4 in range(4):
                g = sb_i * 4 + tt4
                for oc in range(2):
                    i, w = pB3.acquire()
                    pslc = ps_stB[:, i * 512:(i + 1) * 512]
                    for fc in range(32):
                        mm(h1h[:, fc * 512 + tt4 * 128: fc * 512 + (tt4 + 1) * 128],
                           W64[:, fc * 1024 + oc * 512: fc * 1024 + (oc + 1) * 512],
                           pslc, start=(fc == 0), stop=(fc == 31),
                           waits=((w + [(pA_c, h1_mark)] if fc == 0 else []) +
                                  ([(s_ldh, ldh_wb - 512 + 16 * (fc + 1))]
                                   if (tt4 == 0 and oc == 0) else [])),
                           incs=([pB3.pdone()] if fc == 31 else ()))
                    tt("vector",
                       hstg[:, (g % 2) * D + oc * 512:(g % 2) * D + (oc + 1) * 512],
                       pslc, b2h_bc[:, oc * 512:(oc + 1) * 512], ALU.add,
                       waits=pB3.cwait() +
                             [(s_sct, max(0, s_sct.n - 16))],
                       incs=[pB3.cdone()])
                dma("sync", h2b_h[g * 128:(g + 1) * 128, :],
                    hstg[:, (g % 2) * D:(g % 2 + 1) * D], s_sct,
                    waits=[(pB_c, pB_c.n)])
            h2mm_mark = pB_p.n
            if sb_i == 0:
                load_W(w1h16_h, DC, FF_H, [(pB_p, h2mm_mark)])
        h2_all = s_sct.n

    with nc.sbuf_tensor("hg4", [128, 2 * 4 * D], BF16) as hg4, \
         nc.sbuf_tensor("lg4", [128, 2 * 4 * D], BF16) as lg4, \
         nc.sbuf_tensor("xf8", [128, 2 * 4 * D], F32) as xf8, \
         nc.sbuf_tensor("out2", [128, 2 * D], F32) as out2, \
         nc.sbuf_tensor("fA4", [128, 8], F32) as fA4, \
         nc.sbuf_tensor("fB4", [128, 8], F32) as fB4, \
         nc.sbuf_tensor("fC4", [128, 8], F32) as fC4, \
         nc.sbuf_tensor("fD4", [128, 8], F32) as fD4:

        gf_sems = [g_fin, g_fin2, g_fin3, g_fin4]
        ffn_use = {-2: 0, -1: 0}
        ld_toks = {}
        gl_toks = {}

        def emit_wave_gathers(w8):
            par = w8 % 2
            pb = par * 4 * D
            gate = [(c_ev, ffn_use[w8 - 2]), (s_sct, h2_all),
                    (g_zf, zfill_cnt)]
            for tt4 in range(4):
                c = w8 * 4 + tt4
                gs = gf_sems[tt4]
                S.add("gpsimd",
                      lambda e, o=hg4[:, pb + tt4 * D:pb + (tt4 + 1) * D],
                             i=h2b_h, ix=dest_i[:, c:c + 1]:
                          e.indirect_dma_start(out=o, out_offset=None, in_=i,
                                               in_offset=IndirectOffsetOnAxis(ix, 0)),
                      waits=(gate if tt4 == 0 else ()),
                      incs=[(gs, 16)])
            gl_toks[w8] = [(gs2, gs2.n) for gs2 in gf_sems]

        def emit_wave_sloads(w8):
            par = w8 % 2
            pb = par * 4 * D
            gate = [(c_ev, ffn_use[w8 - 2]), (s_sct, h2_all),
                    (c_xfree, 2 * NT + 4 * max(0, w8 - 1))]
            for tt4 in range(4):
                c = w8 * 4 + tt4
                extra = [(s_stz, phaseB_stz)] if c == 0 else []
                dma("sync", xf8[:, pb + tt4 * D:pb + (tt4 + 1) * D],
                    x1_h[c * 128:(c + 1) * 128, :], s_fin,
                    waits=gate + extra)
                dma("scalar", lg4[:, pb + tt4 * D:pb + (tt4 + 1) * D],
                    light_h[c * 128:(c + 1) * 128, :], s_fl,
                    waits=gate + extra)
            ld_toks[w8] = [(s_fin, s_fin.n), (s_fl, s_fl.n)]

        emit_wave_gathers(0)
        emit_wave_sloads(0)
        for w8 in range(8):
            par = w8 % 2
            pb = par * 4 * D
            cb = par * 4
            if w8 + 1 < 8:
                emit_wave_gathers(w8 + 1)
                emit_wave_sloads(w8 + 1)
            for tt4 in range(4):
                c = w8 * 4 + tt4
                stt("vector", w1k[:, tt4 * D:(tt4 + 1) * D],
                    hg4[:, pb + tt4 * D:pb + (tt4 + 1) * D], m_t[:, c:c + 1],
                    lg4[:, pb + tt4 * D:pb + (tt4 + 1) * D], ALU.mult, ALU.add,
                    accum=fA4[:, cb + tt4:cb + tt4 + 1],
                    waits=((ld_toks[w8] + gl_toks[w8]) if tt4 == 0 else ()),
                    incs=([(c_ev, 1)] if tt4 == 3 else ()))
            ffn_use[w8] = c_ev.n
            ffn_mark = c_ev.n
            for tt4 in range(4):
                act(jnk[:] if tt4 % 2 == 0 else t1[:],
                    w1k[:, tt4 * D:(tt4 + 1) * D], AF.Square,
                    accum=fC4[:, cb + tt4:cb + tt4 + 1],
                    waits=([(c_ev, ffn_mark)] if tt4 == 0 else ()))
            act(fB4[:, cb:cb + 4], fA4[:, cb:cb + 4], AF.Identity,
                scale=1.0 / D)
            act(fA4[:, cb:cb + 4], fA4[:, cb:cb + 4], AF.Identity,
                scale=-1.0 / D)
            for tt4 in range(4):
                act(fD4[:, cb + tt4:cb + tt4 + 1],
                    fB4[:, cb + tt4:cb + tt4 + 1], AF.Identity,
                    scale=fA4[:, cb + tt4:cb + tt4 + 1])
            for tt4 in range(4):
                act(fD4[:, cb + tt4:cb + tt4 + 1],
                    fC4[:, cb + tt4:cb + tt4 + 1], AF.Identity,
                    scale=1.0 / D, bias=fD4[:, cb + tt4:cb + tt4 + 1])
            act(fD4[:, cb:cb + 4], fD4[:, cb:cb + 4], AF.Ln,
                bias=eps_t[:, 0:1])
            act(fD4[:, cb:cb + 4], fD4[:, cb:cb + 4], AF.Exp, scale=-0.5,
                incs=[(c_ac, 1)])
            stat_mark = c_ac.n
            for tt4 in range(4):
                c = w8 * 4 + tt4
                ob = (c % 2) * D
                ts("vector", t1[:], w1k[:, tt4 * D:(tt4 + 1) * D],
                   fB4[:, cb + tt4:cb + tt4 + 1],
                   fD4[:, cb + tt4:cb + tt4 + 1],
                   ALU.subtract, ALU.mult,
                   waits=([(c_ac, stat_mark)] if tt4 == 0 else ()))
                tt("vector", out2[:, ob:ob + D], t1[:],
                   xf8[:, pb + tt4 * D:pb + (tt4 + 1) * D],
                   ALU.add, waits=[(s_out, 16 * max(0, c - 1))],
                   incs=[(c_xfree, 1)])
                dma("sync", out_d[c * 128:(c + 1) * 128, :],
                    out2[:, ob:ob + D], s_out,
                    waits=[(c_xfree, c_xfree.n)])
        waitonly("sync", [(s_out, 16 * NT)])

    with nc.Block() as blk:
        blk.sync(lambda e: S.replay(e, "sync"))
        blk.gpsimd(lambda e: S.replay(e, "gpsimd"))
        blk.vector(lambda e: S.replay(e, "vector"))
        blk.scalar(lambda e: S.replay(e, "scalar"))
        blk.tensor(lambda e: S.replay(e, "tensor"))

    es.close()
    return nc


_CONSTS = None


def _consts():
    global _CONSTS
    if _CONSTS is None:
        _CONSTS = {
            "cc_ident": np.eye(128, dtype=np.float32),
            "cc_lt": np.triu(np.ones((128, 128), np.float32), 1),
            "cc_iota": (np.arange(32, dtype=np.float32)[None, :] * 128.0
                        + np.arange(128, dtype=np.float32)[:, None]),
        }
    return _CONSTS


def kernel(L0_attn_x, L0_attn_ln_gamma, L0_attn_ln_beta, L0_attn_W_qkv,
           L0_attn_b_qkv, L0_attn_W_proj, L0_attn_b_proj, L0_ffn_routing_token,
           L0_ffn_light_gamma, L0_ffn_light_w1, L0_ffn_light_b1, L0_ffn_light_w2,
           L0_ffn_light_b2, L0_ffn_heavy_gamma, L0_ffn_heavy_w1, L0_ffn_heavy_b1,
           L0_ffn_heavy_w2, L0_ffn_heavy_b2):
    x = np.ascontiguousarray(np.asarray(L0_attn_x, dtype=np.float32))
    f32c = lambda a: np.ascontiguousarray(np.asarray(a, np.float32))
    shared = {
        "ln_gamma": f32c(L0_attn_ln_gamma),
        "ln_beta": f32c(L0_attn_ln_beta),
        "W_qkv": f32c(L0_attn_W_qkv),
        "b_qkv": f32c(L0_attn_b_qkv),
        "W_proj": f32c(L0_attn_W_proj),
        "b_proj": f32c(L0_attn_b_proj),
        "routing_token": f32c(L0_ffn_routing_token),
        "light_gamma": f32c(L0_ffn_light_gamma),
        "light_w1": f32c(L0_ffn_light_w1),
        "light_b1": f32c(L0_ffn_light_b1),
        "light_w2": f32c(L0_ffn_light_w2),
        "light_b2": f32c(L0_ffn_light_b2),
        "heavy_gamma": f32c(L0_ffn_heavy_gamma),
        "heavy_w1": f32c(L0_ffn_heavy_w1),
        "heavy_b1": f32c(L0_ffn_heavy_b1),
        "heavy_w2": f32c(L0_ffn_heavy_w2),
        "heavy_b2": f32c(L0_ffn_heavy_b2),
    }
    shared.update(_consts())
    nc = build_nc()
    in_maps = [dict(shared, x=np.ascontiguousarray(x[b])) for b in range(8)]
    res = run_bass_kernel_spmd(nc, in_maps, core_ids=list(range(8)))
    out = np.stack([res.results[b]["out"] for b in range(8)], axis=0)
    return out


# version 26 (final)
# speedup vs baseline: 1.0264x; 1.0226x over previous
import numpy as np
from contextlib import ExitStack

import concourse.bass as bass
import concourse.mybir as mybir
from concourse.bass_utils import run_bass_kernel_spmd
from concourse.bass_core_rust import IndirectOffsetOnAxis

F32 = mybir.dt.float32
BF16 = mybir.dt.bfloat16
I32 = mybir.dt.int32
U32 = mybir.dt.uint32
AF = mybir.ActivationFunctionType
ALU = mybir.AluOpType
AX = mybir.AxisListType

T = 4096
D = 1024
NH = 16
SCALE = 0.125
EPS = 1e-5
NT = 32
NBLK = 8
DC = 8
FF_L = 512
FF_H = 4096
NSEL = 1024
LOGK = float(np.log(1152.0))
NITER = 50
NBIS = 48

ENGINES = ("sync", "gpsimd", "vector", "scalar", "tensor")


class Sem:
    def __init__(self, handle):
        self.h = handle
        self.n = 0


def _apname(a):
    try:
        return a.tensor.name
    except AttributeError:
        return None


class Sched:

    AUTOSYNC = ("vector", "scalar", "gpsimd", "sync")

    def __init__(self):
        self.ops = {e: [] for e in ENGINES}
        self.pending = {e: {} for e in ENGINES}
        self.selfsem = {}

    def set_selfsems(self, d):
        self.selfsem = d

    def add(self, eng, emit, waits=(), incs=(), touch=()):
        ws = [(s, v) for (s, v) in waits if v > 0]
        ics = list(incs)
        assert len(ics) <= 1, "one semaphore update per instruction"
        names = [n for n in (_apname(a) if not isinstance(a, str) else a
                             for a in touch) if n]
        if eng in self.AUTOSYNC and names:
            pend = self.pending[eng]
            extra = {}
            for n in names:
                if n in pend:
                    s, v = pend.pop(n)
                    if extra.get(id(s), (None, 0))[1] < v:
                        extra[id(s)] = (s, v)
            ws = ws + list(extra.values())
            if not ics and eng in self.selfsem:
                ics = [(self.selfsem[eng], 1)]
        self.ops[eng].append((ws, emit, ics))
        for s, k in ics:
            s.n += k
        if eng in self.AUTOSYNC and names and ics:
            marker = (ics[0][0], ics[0][0].n)
            pend = self.pending[eng]
            for n in names:
                pend[n] = marker

    def replay(self, eng_obj, name):
        for ws, emit, ics in self.ops[name]:
            for s, v in ws:
                eng_obj.wait_ge(s.h, v)
            ins = emit(eng_obj)
            for s, k in ics:
                ins.then_inc(s.h, k)


class Pool:

    def __init__(self, p, c, nslots):
        self.p = p
        self.c = c
        self.nslots = nslots
        self.uses = 0

    def acquire(self):
        i = self.uses % self.nslots
        need = self.uses - self.nslots + 1
        w = [(self.c, self.c.n - (self.uses - need))] if need > 0 else []
        self.uses += 1
        return i, w

    def pdone(self):
        return (self.p, 1)

    def cwait(self):
        return [(self.p, self.p.n)]

    def cdone(self):
        return (self.c, 1)


def build_nc(dbg=False):
    nc = bass.Bass("TRN2", target_bir_lowering=False,
                   detect_race_conditions=False)

    def din(name, shape):
        return nc.dram_tensor(name, list(shape), F32, kind="ExternalInput").ap()

    x_d = din("x", (T, D))
    gam_d = din("ln_gamma", (D,))
    bet_d = din("ln_beta", (D,))
    wqkv_d = din("W_qkv", (D, 3 * D))
    bqkv_d = din("b_qkv", (3 * D,))
    wprj_d = din("W_proj", (D, D))
    bprj_d = din("b_proj", (D,))
    rt_d = din("routing_token", (D,))
    lgam_d = din("light_gamma", (D,))
    lw1_d = din("light_w1", (D, FF_L))
    lb1_d = din("light_b1", (FF_L,))
    lw2_d = din("light_w2", (FF_L, D))
    lb2_d = din("light_b2", (D,))
    hgam_d = din("heavy_gamma", (D,))
    hw1_d = din("heavy_w1", (D, FF_H))
    hb1_d = din("heavy_b1", (FF_H,))
    hw2_d = din("heavy_w2", (FF_H, D))
    hb2_d = din("heavy_b2", (D,))
    idn_d = din("cc_ident", (128, 128))
    lt_d = din("cc_lt", (128, 128))
    iota_d = din("cc_iota", (128, 32))

    out_d = nc.dram_tensor("out", [T, D], F32, kind="ExternalOutput").ap()

    KD = "ExternalOutput" if dbg else "Internal"
    qT_h = nc.dram_tensor("qT_h", [D, T], BF16, kind=KD).ap()
    kT_h = nc.dram_tensor("kT_h", [D, T], BF16, kind=KD).ap()
    v_h = nc.dram_tensor("v_h", [128, 8, 32, 130], BF16, kind=KD).ap()
    z_h = nc.dram_tensor("z_h", [T, D], BF16, kind=KD).ap()
    zperm_h = nc.dram_tensor("zperm_h", [T, D], BF16, kind=KD).ap()
    x1_h = nc.dram_tensor("x1_h", [T, D], F32, kind=KD).ap()
    light_h = nc.dram_tensor("light_h", [T, D], BF16, kind=KD).ap()
    h2b_h = nc.dram_tensor("h2b_h", [T, D], BF16, kind=KD).ap()
    w1h16_h = nc.dram_tensor("w1h16_h", [D, FF_H], BF16, kind="Internal").ap()
    w2h16_h = nc.dram_tensor("w2h16_h", [FF_H, D], BF16, kind="Internal").ap()
    bias_h = nc.dram_tensor("bias_h", [D], F32, kind="Internal").ap()
    red_h = nc.dram_tensor("red_h", [128], F32, kind="Internal").ap()
    if dbg:
        dbg_s = nc.dram_tensor("dbg_s", [128, 32], F32, kind="ExternalOutput").ap()
        dbg_u = nc.dram_tensor("dbg_u", [128, 32], F32, kind="ExternalOutput").ap()
        dbg_dest = nc.dram_tensor("dbg_dest", [128, 32], F32, kind="ExternalOutput").ap()
        dbg_u8 = nc.dram_tensor("dbg_u8", [T, D], F32, kind="ExternalOutput").ap()
        dbg_xnT = nc.dram_tensor("dbg_xnT", [NBLK, 128, 8 * 512], BF16,
                                 kind="ExternalOutput").ap()

    S = Sched()
    es = ExitStack()

    def sem(name):
        return Sem(es.enter_context(nc.semaphore(name)))

    s_ldw = sem("ldw")
    g_ldw = sem("gldw")
    g_ldk = sem("gldk")
    g_msc = sem("gmsc")
    g_pcs = sem("gpcs")
    g_zf = sem("gzf")
    g_ldh = sem("gldh")
    g_sct = sem("gsct")
    g_fin = sem("gfin")
    g_fin2 = sem("gfin2")
    g_fin3 = sem("gfin3")
    g_fin4 = sem("gfin4")
    s_ldx = sem("ldx")
    s_stq = sem("stq")
    s_stq2 = sem("stq2")
    s_stq3 = sem("stq3")
    s_stq4 = sem("stq4")
    s_stv = sem("stv")
    s_stv2 = sem("stv2")
    s_stv3 = sem("stv3")
    s_stv4 = sem("stv4")
    s_ldk = sem("ldk")
    s_ldp = sem("ldp")
    s_stz = sem("stz")
    s_msc = sem("msc")
    s_ldh = sem("ldh")
    s_sct = sem("sct")
    s_fin = sem("fin")
    s_fl = sem("fl")
    s_out = sem("out")
    c_ev = sem("ev")
    c_ac = sem("ac")
    c_xfree = sem("xfree")
    c_qkv = sem("qkv")
    c_qkp = sem("qkp")
    c_exp = sem("exp")
    c_avg = sem("avg")
    pA_p, pA_c = sem("pA_p"), sem("pA_c")
    pB_p, pB_c = sem("pB_p"), sem("pB_c")
    pU_c = sem("pU_c")
    pG_p, pG_c = sem("pG_p"), sem("pG_c")
    v_self = sem("v_self")
    a_self = sem("a_self")
    g_self = sem("g_self")
    S.set_selfsems({"vector": v_self, "scalar": a_self, "gpsimd": g_self})

    def sb(name, shape, dt):
        return es.enter_context(nc.sbuf_tensor(name, list(shape), dt))

    ident_f = sb("ident_f", (128, 128), F32)
    lt_f = sb("lt_f", (128, 128), F32)
    iota_f = sb("iota_f", (128, 32), F32)
    ones_r128 = sb("ones_r128", (1, 128), F32)
    half_r128 = sb("half_r128", (1, 128), F32)
    ones_r64 = sb("ones_r64", (1, 64), F32)
    ones_c128 = sb("ones_c128", (128, 1), F32)
    eps_t = sb("eps_t", (128, 1), F32)
    logk_t = sb("logk_t", (128, 1), F32)
    rt_bc = sb("rt_bc", (128, D), F32)
    row_g = sb("row_g", (1, D), F32)
    bqa = sb("bqa", (128, 24), F32)
    bkv = sb("bkv", (128, 24), F32)
    gam_c = sb("gam_c", (128, 8), F32)
    lgam_c = sb("lgam_c", (128, 8), F32)
    hgam_c = sb("hgam_c", (128, 8), F32)
    bprj_bc = sb("bprj_bc", (128, D), BF16)
    b1l = sb("b1l", (128, 4), F32)
    b2l_bc = sb("b2l_bc", (128, D), BF16)
    b1h = sb("b1h", (128, 32), F32)
    b2h_bc = sb("b2h_bc", (128, D), BF16)
    xa = sb("xa", (128, 4 * D), F32)
    t1 = sb("t1", (128, D), F32)
    w1k = sb("w1k", (128, 4 * D), F32)
    jnk = sb("jnk", (128, D), F32)
    u8 = sb("u8", (128, D), F32)
    x1t = sb("x1t", (128, D), F32)
    lb16 = sb("lb16", (128, D), BF16)
    out_t = sb("out_t", (128, D), F32)
    zro16 = sb("zro16", (128, 512), BF16)
    zsc = sb("zsc", (128, 2 * D), BF16)
    lstg = sb("lstg", (128, 2 * D), BF16)
    colA = sb("colA", (128, 1), F32)
    colB = sb("colB", (128, 1), F32)
    colC = sb("colC", (128, 1), F32)
    colD = sb("colD", (128, 1), F32)
    colE = sb("colE", (128, 1), F32)
    colF = sb("colF", (128, 1), F32)
    colG = sb("colG", (128, 1), F32)
    colI = sb("colI", (128, 1), F32)
    s_parts = sb("s_parts", (128, 32), F32)
    m_t = sb("m_t", (128, 32), F32)
    bb32 = sb("bb32", (128, 32), F32)
    tmp32 = sb("tmp32", (128, 32), F32)
    e32 = sb("e32", (128, 32), F32)
    u32 = sb("u32", (128, 32), F32)
    q32 = sb("q32", (128, 32), F32)
    um32 = sb("um32", (128, 32), F32)
    mku = sb("mku", (128, 32), U32)
    nbig32 = sb("nbig32", (128, 32), F32)
    eqm32 = sb("eqm32", (128, 32), F32)
    sel32 = sb("sel32", (128, 32), F32)
    rkp32 = sb("rkp32", (128, 32), F32)
    rk32 = sb("rk32", (128, 32), F32)
    dest_f = sb("dest_f", (128, 32), F32)
    dest_i = sb("dest_i", (128, 32), I32)
    row32a = sb("row32a", (1, 32), F32)
    row32b = sb("row32b", (1, 32), F32)
    row32c = sb("row32c", (1, 32), F32)
    row128 = sb("row128", (1, 128), F32)
    sc_a = sb("sc_a", (1, 1), F32)
    sc_lo = sb("sc_lo", (1, 1), F32)
    sc_hi = sb("sc_hi", (1, 1), F32)
    sc_mid = sb("sc_mid", (1, 1), F32)
    sc_t1 = sb("sc_t1", (1, 1), F32)
    sc_t2 = sb("sc_t2", (1, 1), F32)
    sc_cnt = sb("sc_cnt", (1, 1), F32)
    sc_flg = sb("sc_flg", (1, 1), F32)
    sc_vs = sb("sc_vs", (1, 1), F32)
    sc_teq = sb("sc_teq", (1, 1), F32)
    sc_fu = sb("sc_fu", (1, 1), U32)
    sc_nu = sb("sc_nu", (1, 1), U32)

    def psum(name, shape, dt):
        return es.enter_context(nc.psum_tensor(name, list(shape), dt))

    ps_stA = psum("ps_stA", (128, 1024), F32)
    ps_stB = psum("ps_stB", (128, 1024), F32)
    ps_u = psum("ps_u", (65, 512), F32)
    ps_gen = psum("ps_gen", (128, 1536), F32)

    def dma(q, out_ap, in_ap, sem_, waits=(), slow=False):
        def emit(e, o=out_ap, i=in_ap, sl=slow):
            if sl:
                return e.dma_start(out=o, in_=i, allow_slow_non_contiguous=True)
            return e.dma_start(out=o, in_=i)
        S.add(q, emit, waits=waits, incs=[(sem_, 16)], touch=[out_ap, in_ap])

    def mm(lhsT, rhs, out, start, stop, waits=(), incs=()):
        S.add("tensor",
              lambda e, l=lhsT, r=rhs, o=out, st=start, sp=stop:
                  e.matmul(o, l, r, start=st, stop=sp),
              waits=waits, incs=incs)

    def ts(eng, out, in0, s1, s2, op0, op1=None, accum=None, waits=(), incs=()):
        if accum is not None and op1 is None:
            op1, s2 = ALU.add, None
        def emit(e, o=out, i=in0, a=s1, b=s2, p0=op0, p1=op1, ac=accum):
            kw = {}
            if ac is not None:
                kw["accum_out"] = ac
            if p1 is None:
                return e.tensor_scalar(o, i, a, None, p0, **kw)
            return e.tensor_scalar(o, i, a, b, p0, p1, **kw)
        tch = [out, in0] + [a for a in (s1, s2, accum) if not isinstance(a, (float, int, type(None)))]
        S.add(eng, emit, waits=waits, incs=incs, touch=tch)

    def tt(eng, out, in0, in1, op, waits=(), incs=()):
        S.add(eng, lambda e, o=out, a=in0, b=in1, p=op: e.tensor_tensor(o, a, b, p),
              waits=waits, incs=incs, touch=[out, in0, in1])

    def stt(eng, out, in0, sc, in1, op0, op1, accum=None, waits=(), incs=()):
        def emit(e, o=out, i0=in0, s=sc, i1=in1, p0=op0, p1=op1, ac=accum):
            kw = {}
            if ac is not None:
                kw["accum_out"] = ac
            return e.scalar_tensor_tensor(o, i0, s, i1, p0, p1, **kw)
        tch = [out, in0, in1] + [a for a in (sc, accum) if not isinstance(a, (float, int, type(None)))]
        S.add(eng, emit, waits=waits, incs=incs, touch=tch)

    def act(out, in_, func, bias=None, scale=None, accum=None, waits=(), incs=()):
        def emit(e, o=out, i=in_, f=func, b=bias, sc=scale, ac=accum):
            kw = {}
            if b is not None:
                kw["bias"] = b
            if sc is not None:
                kw["scale"] = sc
            if ac is not None:
                kw["accum_out"] = ac
            return e.activation(out=o, in_=i, func=f, **kw)
        tch = [out, in_] + [a for a in (bias, scale, accum) if not isinstance(a, (float, int, type(None)))]
        S.add("scalar", emit, waits=waits, incs=incs, touch=tch)

    def cp(eng, out, in_, waits=(), incs=()):
        S.add(eng, lambda e, o=out, i=in_: e.tensor_copy(out=o, in_=i),
              waits=waits, incs=incs, touch=[out, in_])

    def recip(out, in_, waits=(), incs=()):
        S.add("vector", lambda e, o=out, i=in_: e.reciprocal(out=o, in_=i),
              waits=waits, incs=incs, touch=[out, in_])

    def red(out, in_, op, waits=(), incs=()):
        S.add("vector", lambda e, o=out, i=in_, p=op: e.tensor_reduce(o, i, AX.X, p),
              waits=waits, incs=incs, touch=[out, in_])

    def mset(eng, ap, val, waits=(), incs=()):
        S.add(eng, lambda e, a=ap, v=val: e.memset(a, v), waits=waits, incs=incs,
              touch=[ap])

    def transpose(out, in_, waits=(), incs=()):
        S.add("tensor",
              lambda e, o=out, i=in_: e.transpose(out=o, in_=i, identity=ident_f[:]),
              waits=waits, incs=incs)

    def waitonly(eng, waits):
        S.add(eng, lambda e: None, waits=waits)

    def gslc(i, rows=128, cols=512):
        return ps_gen[0:rows, i * 512:i * 512 + cols]

    def bcast_row_b(src_row_dram, dst, extraw=()):
        dma("sync", row_g[:], src_row_dram.rearrange("(o d) -> o d", o=1),
            s_msc, waits=list(extraw) + [(pG_p, pG_p.n)])
        ldn = s_msc.n
        for half in range(2):
            gi, w = poolG.acquire()
            mm(ones_r128[:, 0:128], row_g[0:1, half * 512:(half + 1) * 512],
               gslc(gi), start=True, stop=True,
               waits=w + [(s_msc, ldn)], incs=[poolG.pdone()])
            cp("vector", dst[:, half * 512:(half + 1) * 512], gslc(gi),
               waits=poolG.cwait(), incs=[poolG.cdone()])

    pT_p, pT_c = sem("pT_p"), sem("pT_c")
    poolG = Pool(pG_p, pG_c, 3)

    with nc.sbuf_tensor("Wq_sb", [128, 8 * 3072], BF16) as Wq_sb, \
         nc.sbuf_tensor("beta16", [128, 8], BF16) as beta16, \
         nc.sbuf_tensor("xnT", [128, 2 * 8 * 512], BF16) as xnT, \
         nc.sbuf_tensor("vstage", [128, 2080], BF16) as vstage, \
         nc.sbuf_tensor("bvb", [128, 1024], BF16) as bvb, \
         nc.sbuf_tensor("rowD", [1, D], F32) as rowD, \
         nc.sbuf_tensor("wcast", [128, FF_H], BF16) as wcast, \
         nc.sbuf_tensor("stg", [128, 4 * 512], BF16) as stg, \
         nc.sbuf_tensor("u8b", [128, D], F32) as u8b, \
         nc.sbuf_tensor("t14", [128, 4 * D], F32) as t14, \
         nc.sbuf_tensor("cA4", [128, 4], F32) as cA4, \
         nc.sbuf_tensor("cB4", [128, 4], F32) as cB4, \
         nc.sbuf_tensor("cC4", [128, 4], F32) as cC4, \
         nc.sbuf_tensor("cE4", [128, 4], F32) as cE4, \
         nc.sbuf_tensor("cF4", [128, 4], F32) as cF4, \
         nc.sbuf_tensor("cG4", [128, 4], F32) as cG4:

        for dc in range(DC):
            dma("gpsimd", Wq_sb[:, dc * 3072:(dc + 1) * 3072],
                wqkv_d[dc * 128:(dc + 1) * 128, :], g_ldw)
        dma("sync", ident_f[:], idn_d, s_ldw)
        dma("sync", lt_f[:], lt_d, s_ldw)
        dma("sync", iota_f[:], iota_d, s_ldw)
        dma("sync", gam_c[:], gam_d.rearrange("(c p) -> p c", p=128), s_ldw, slow=True)
        dma("sync", lgam_c[:], lgam_d.rearrange("(c p) -> p c", p=128), s_ldw, slow=True)
        dma("sync", hgam_c[:], hgam_d.rearrange("(c p) -> p c", p=128), s_ldw, slow=True)
        dma("gpsimd", beta16[:], bet_d.rearrange("(c p) -> p c", p=128), g_ldw, slow=True)
        dma("sync", bkv[:], bqkv_d.rearrange("(c p) -> p c", p=128), s_ldw, slow=True)
        dma("sync", b1l[:], lb1_d.rearrange("(c p) -> p c", p=128), s_ldw, slow=True)
        dma("sync", b1h[:], hb1_d.rearrange("(c p) -> p c", p=128), s_ldw, slow=True)
        ldw0 = s_ldw.n
        gldw0 = g_ldw.n

        mset("vector", ones_r128[:], 1.0)
        mset("vector", half_r128[:], 0.5)
        mset("vector", ones_r64[:], 1.0)
        mset("vector", ones_c128[:], 1.0)
        mset("vector", eps_t[:], EPS)
        mset("vector", logk_t[:], LOGK)
        mset("vector", zro16[:], 0.0)
        mset("vector", nbig32[:], -3.0e38)
        mset("vector", sc_hi[:], 1.0)
        mset("vector",
             vstage[:].rearrange("p (h c) -> p h c", c=65)[:, :, 64:65], 1.0,
             incs=[(c_ev, 1)])
        msets_done = c_ev.n

        for j in range(24):
            for hh in range(2):
                dma("gpsimd",
                    h2b_h[NSEL + j * 128: NSEL + (j + 1) * 128,
                          hh * 512:(hh + 1) * 512],
                    zro16[:], g_zf, waits=[(c_ev, msets_done)])
        zfill_cnt = g_zf.n

        for oc in range(24):
            gi, w = poolG.acquire()
            for dc in range(DC):
                mm(Wq_sb[:, dc * 3072 + oc * 128: dc * 3072 + (oc + 1) * 128],
                   beta16[:, dc:dc + 1], gslc(gi, 128, 1),
                   start=(dc == 0), stop=(dc == DC - 1),
                   waits=(w + [(s_ldw, ldw0), (g_ldw, gldw0)]
                          if dc == 0 else ()),
                   incs=([poolG.pdone()] if dc == DC - 1 else ()))
            tt("vector", bqa[:, oc:oc + 1], gslc(gi, 128, 1), bkv[:, oc:oc + 1],
               ALU.add, waits=poolG.cwait(), incs=[poolG.cdone()])
        bqa_done = pG_c.n

        for dc in range(DC):
            ts("vector", Wq_sb[:, dc * 3072:(dc + 1) * 3072],
               Wq_sb[:, dc * 3072:(dc + 1) * 3072], gam_c[:, dc:dc + 1], None,
               ALU.mult,
               waits=([(pG_c, bqa_done), (s_ldw, ldw0)] if dc == 0 else ()),
               incs=([(c_ev, 1)] if dc == DC - 1 else ()))
        wqscale_mark = c_ev.n

        dma("sync", bias_h.rearrange("(c p) -> p c", p=128), bqa[:, 16:24], s_msc,
            waits=[(pG_c, bqa_done)], slow=True)
        dma("sync", rowD[:], bias_h.rearrange("(o d) -> o d", o=1), s_msc,
            waits=[(s_msc, s_msc.n)])

        def bcast_row(src_row_dram, dst, via_rowD=False):
            if not via_rowD:
                dma("sync", rowD[:], src_row_dram.rearrange("(o d) -> o d", o=1),
                    s_msc, waits=[(pG_p, pG_p.n)])
            ldn = s_msc.n
            for half in range(2):
                gi, w = poolG.acquire()
                mm(ones_r128[:, 0:128], rowD[0:1, half * 512:(half + 1) * 512],
                   gslc(gi), start=True, stop=True,
                   waits=w + [(s_msc, ldn), (c_ev, msets_done)],
                   incs=[poolG.pdone()])
                cp("vector", dst[:, half * 512:(half + 1) * 512], gslc(gi),
                   waits=poolG.cwait(), incs=[poolG.cdone()])

        bcast_row(None, bvb, via_rowD=True)
        bcast_row(bprj_d, bprj_bc)
        bcast_row(lb2_d, b2l_bc)
        bcast_row(hb2_d, b2h_bc)
        bcast_row(rt_d, rt_bc)

        pA = Pool(pA_p, pA_c, 4)
        psQ = [ps_stA[:, 0:512], ps_stA[:, 512:1024],
               ps_stB[:, 0:512], ps_stB[:, 512:1024]]

        precast = [(hw1_d[dc * 128:(dc + 1) * 128, :],
                    w1h16_h[dc * 128:(dc + 1) * 128, :], FF_H) for dc in range(DC)]
        precast += [(hw2_d[fc * 128:(fc + 1) * 128, :],
                     w2h16_h[fc * 128:(fc + 1) * 128, :], D) for fc in range(32)]
        pc_i = [0]

        def emit_precast():
            if pc_i[0] >= len(precast):
                return
            src, dst, width = precast[pc_i[0]]
            pc_i[0] += 1
            dma("gpsimd", wcast[:, 0:width], src, g_msc,
                waits=[(g_pcs, g_pcs.n)])
            dma("gpsimd", dst, wcast[:, 0:width], g_pcs,
                waits=[(g_msc, g_msc.n)])

        stats_marks = {}

        def stats_part(tb):
            x0 = s_ldx.n
            for tt4 in range(4):
                c = tb * 4 + tt4
                slot = c % 4
                dma("scalar", xa[:, slot * D:(slot + 1) * D],
                    x_d[c * 128:(c + 1) * 128, :], s_ldx,
                    waits=[(c_xfree, max(0, c - 3))])
            for tt4 in range(4):
                c = tb * 4 + tt4
                slot = c % 4
                act(t14[:, tt4 * D:(tt4 + 1) * D], xa[:, slot * D:(slot + 1) * D],
                    AF.Identity, accum=cA4[:, tt4:tt4 + 1],
                    waits=[(s_ldx, x0 + 16 * (tt4 + 1))])
            act(cB4[:], cA4[:], AF.Identity, scale=-1.0 / D)
            for tt4 in range(4):
                c = tb * 4 + tt4
                slot = c % 4
                act(t14[:, tt4 * D:(tt4 + 1) * D], xa[:, slot * D:(slot + 1) * D],
                    AF.Identity, bias=cB4[:, tt4:tt4 + 1],
                    accum=cA4[:, tt4:tt4 + 1],
                    incs=[(c_xfree, 1)])
            for tt4 in range(4):
                act(jnk[:] if tt4 % 2 == 0 else t1[:],
                    t14[:, tt4 * D:(tt4 + 1) * D],
                    AF.Square, accum=cC4[:, tt4:tt4 + 1])
            act(cE4[:], cC4[:], AF.Ln, bias=eps_t[:, 0:1], scale=1.0 / D)
            act(cE4[:], cE4[:], AF.Exp, scale=-0.5)
            act(cG4[:], cA4[:], AF.Identity, scale=1.0 / D)
            act(cB4[:], cA4[:], AF.Identity, scale=-1.0 / D)
            for j in range(4):
                act(cF4[:, j:j + 1], cG4[:, j:j + 1], AF.Identity,
                    scale=cB4[:, j:j + 1])
            for j in range(4):
                act(cF4[:, j:j + 1], cC4[:, j:j + 1], AF.Identity,
                    scale=1.0 / D, bias=cF4[:, j:j + 1])
            for j in range(4):
                act(cF4[:, j:j + 1], cF4[:, j:j + 1], AF.Identity,
                    scale=cE4[:, j:j + 1])
            for j in range(4):
                act(cF4[:, j:j + 1], cF4[:, j:j + 1], AF.Identity,
                    scale=cE4[:, j:j + 1])
            act(cF4[:], cF4[:], AF.Ln, bias=eps_t[:, 0:1])
            act(cF4[:], cF4[:], AF.Exp, scale=-0.5)
            for j in range(4):
                act(cG4[:, j:j + 1], cE4[:, j:j + 1], AF.Identity,
                    scale=cF4[:, j:j + 1])
            for j in range(4):
                act(cA4[:, j:j + 1], cB4[:, j:j + 1], AF.Identity,
                    scale=cG4[:, j:j + 1])

        def trans_part(tb):
            xb = (tb % 2) * 4096
            xnT_v = xnT[:, xb:xb + 4096].rearrange("p (dc q) -> p dc q", q=512)
            for tt4 in range(4):
                ui = tt4 % 2
                ub = u8 if ui == 0 else u8b
                act(ub[:], t14[:, tt4 * D:(tt4 + 1) * D], AF.Identity,
                    scale=cG4[:, tt4:tt4 + 1], bias=cA4[:, tt4:tt4 + 1],
                    waits=([(pG_p, u8_marks[ui])]
                           if u8_marks.get(ui) else []),
                    incs=[(c_ac, 1)])
                acu = c_ac.n
                for grp in range(2):
                    gi, w = poolG.acquire()
                    for j in range(4):
                        transpose(ps_gen[:, gi * 512 + j * 128: gi * 512 + (j + 1) * 128],
                                  ub[:, (grp * 4 + j) * 128:(grp * 4 + j + 1) * 128],
                                  waits=(w + [(c_ac, acu)] +
                                         ([(s_ldw, ldw0), (c_qkv, max(0, tb - 1))]
                                          if (tt4 == 0 and grp == 0) else [])
                                         if j == 0 else ()),
                                  incs=([poolG.pdone()] if j == 3 else ()))
                    cp("vector",
                       xnT_v[:, grp * 4:(grp + 1) * 4, tt4 * 128:(tt4 + 1) * 128],
                       gslc(gi).rearrange("p (dc q) -> p dc q", q=128),
                       waits=poolG.cwait(), incs=[poolG.cdone()])
                u8_marks[ui] = pG_p.n
                emit_precast()

        def qkv_part(tb):
            xb = (tb % 2) * 4096
            xnT_mark = qkv_marks[tb]
            for which, dst, boff in ((0, qT_h, 0), (1, kT_h, 8)):
                for oc in range(DC):
                    i, w = pA.acquire()
                    pslc = psQ[i]
                    for dc in range(DC):
                        mm(Wq_sb[:, dc * 3072 + (boff + oc) * 128:
                                 dc * 3072 + (boff + oc + 1) * 128],
                           xnT[:, xb + dc * 512: xb + (dc + 1) * 512], pslc,
                           start=(dc == 0), stop=(dc == DC - 1),
                           waits=(w + [(pG_c, xnT_mark)] +
                                  ([(c_ev, wqscale_mark)]
                                   if (tb == 0 and boff == 0 and oc == 0)
                                   else []) if dc == 0 else ()),
                           incs=([pA.pdone()] if dc == DC - 1 else ()))
                    qu = qk_u[0]
                    ssl = stg[:, (qu % 4) * 512:(qu % 4 + 1) * 512]
                    ts("vector", ssl, pslc, bqa[:, boff + oc: boff + oc + 1],
                       None, ALU.add,
                       waits=pA.cwait() + ([qk_stok[qu - 4]] if qu >= 4 else []),
                       incs=[pA.cdone()])
                    qsm = qk_sems[qu % 4]
                    dma("sync", dst[oc * 128:(oc + 1) * 128, tb * 512:(tb + 1) * 512],
                        ssl, qsm, waits=[(pA_c, pA_c.n)])
                    qk_stok.append((qsm, qsm.n))
                    qk_u[0] += 1
            for tt4 in range(4):
                g = tb * 4 + tt4
                for half in range(2):
                    i, w = pA.acquire()
                    pslc = psQ[i]
                    for dc in range(DC):
                        mm(xnT[:, xb + dc * 512 + tt4 * 128:
                               xb + dc * 512 + (tt4 + 1) * 128],
                           Wq_sb[:, dc * 3072 + 2048 + half * 512:
                                 dc * 3072 + 2048 + (half + 1) * 512],
                           pslc, start=(dc == 0), stop=(dc == DC - 1),
                           waits=(w if dc == 0 else ()),
                           incs=([pA.pdone()] if dc == DC - 1 else ()))
                    vu = v_u[0]
                    vbase = (vu % 4) * 520
                    vdst = vstage[:, vbase:vbase + 520].rearrange(
                        "p (h c) -> p h c", c=65)[:, :, 0:64]
                    psrc = pslc.rearrange("p (h c) -> p h c", c=64)
                    bsrc = bvb[:, half * 512:(half + 1) * 512].rearrange(
                        "p (h c) -> p h c", c=64)
                    S.add("vector",
                          lambda e, o=vdst, a=psrc, b=bsrc:
                              e.tensor_tensor(o, a, b, ALU.add),
                          waits=pA.cwait() + [(c_ev, msets_done)] +
                                ([v_stok[vu - 4]] if vu >= 4 else []),
                          incs=[pA.cdone()])
                    vsm = v_sems[vu % 4]
                    dma("sync",
                        v_h[:, half * 4:(half + 1) * 4, g:g + 1, :],
                        vstage[:, vbase:vbase + 520].rearrange(
                            "p (h o c) -> p h o c", h=4, o=1),
                        vsm, waits=[(pA_c, pA_c.n)])
                    v_stok.append((vsm, vsm.n))
                    v_u[0] += 1
            S.add("tensor", lambda e: e.nop(), incs=[(c_qkv, 1)])

        qkv_marks = {}
        u8_marks = {}
        qk_sems = [s_stq, s_stq2, s_stq3, s_stq4]
        v_sems = [s_stv, s_stv2, s_stv3, s_stv4]
        qk_stok, v_stok = [], []
        qk_u, v_u = [0], [0]
        for tb in range(NBLK):
            stats_part(tb)
            if tb >= 1:
                qkv_part(tb - 1)
            trans_part(tb)
            qkv_marks[tb] = pG_c.n
        qkv_part(NBLK - 1)
        while pc_i[0] < len(precast):
            emit_precast()
        phaseA_qkv = ([(s, s.n) for s in (s_stq, s_stq2, s_stq3, s_stq4)] +
                      [(s, s.n) for s in (s_stv, s_stv2, s_stv3, s_stv4)])
        phaseA_msc = s_msc.n
        phaseA_gmsc = g_msc.n

    with nc.sbuf_tensor("Wp_sb", [128, 8 * 1024], BF16) as Wp_sb, \
         nc.sbuf_tensor("W1l_sb", [128, 8 * 512], BF16) as W1l_sb, \
         nc.sbuf_tensor("W2l_sb", [128, 4 * 1024], BF16) as W2l_sb, \
         nc.sbuf_tensor("kTs", [128, 3 * T], BF16) as kTs, \
         nc.sbuf_tensor("vs", [128, 3 * 32 * 130], BF16) as vs, \
         nc.sbuf_tensor("qpad", [128, 2 * 512], BF16) as qpad, \
         nc.sbuf_tensor("eT", [128, 6 * 512], BF16) as eT, \
         nc.sbuf_tensor("cxT", [128, 8 * 512], BF16) as cxT, \
         nc.sbuf_tensor("zTl", [128, 8 * 512], BF16) as zTl, \
         nc.sbuf_tensor("h1l", [128, 4 * 512], BF16) as h1l, \
         nc.sbuf_tensor("u_sb", [65, 2 * 512], F32) as u_sb, \
         nc.sbuf_tensor("rsg", [1, 2 * 512], F32) as rsg:

        wB = [(c_qkv, NBLK), (s_msc, phaseA_msc), (g_msc, phaseA_gmsc)]
        for dc in range(DC):
            dma("gpsimd", Wp_sb[:, dc * 1024:(dc + 1) * 1024],
                wprj_d[dc * 128:(dc + 1) * 128, :], g_ldw,
                waits=(wB if dc == 0 else ()))
        for dc in range(DC):
            dma("gpsimd", W1l_sb[:, dc * 512:(dc + 1) * 512],
                lw1_d[dc * 128:(dc + 1) * 128, :], g_ldw)
        for fc in range(4):
            dma("gpsimd", W2l_sb[:, fc * 1024:(fc + 1) * 1024],
                lw2_d[fc * 128:(fc + 1) * 128, :], g_ldw)
        ldw1 = g_ldw.n
        for dc in range(DC):
            ts("vector", W1l_sb[:, dc * 512:(dc + 1) * 512],
               W1l_sb[:, dc * 512:(dc + 1) * 512], lgam_c[:, dc:dc + 1], None,
               ALU.mult,
               waits=([(g_ldw, ldw1), (s_ldw, ldw0)] if dc == 0 else ()),
               incs=([(c_ev, 1)] if dc == DC - 1 else ()))
        w1l_mark = c_ev.n
        mset("vector", qpad[:], 0.0, waits=wB, incs=[(c_ev, 1)])
        qpad_z = c_ev.n

        gidx = [0]
        pend_tail = []
        zt_w = [0]
        pend_tstate = {}
        lout_mark = [pG_p.n]
        stok = {"x1": 0, "z": 0, "l0": 0, "l1": 0}
        GRP = [2] * 16
        GOF = [sum(GRP[:i]) for i in range(len(GRP))]
        last_z_act = [0]
        prj_mark = [pG_p.n]
        lgt_mark = [pG_p.n]

        for qb in range(NBLK):
            if pend_tail:
                xldns = []
                for tt4 in range(4):
                    c = (qb - 1) * 4 + tt4
                    slot = c % 4
                    dma("sync", xa[:, slot * D:(slot + 1) * D],
                        x_d[c * 128:(c + 1) * 128, :], s_ldx,
                        waits=[(c_xfree, NT + max(0, c - 3))])
                    xldns.append(s_ldx.n)
                pend_tstate["xldn"] = xldns
            def kv_loads(hp):
                idx = qb * 8 + hp
                wk = [(pU_c, max(0, 2 * idx - 4))]
                if idx == 0:
                    wk = wk + phaseA_qkv + [(c_qkv, NBLK)]
                dma("sync", kTs[:, (idx % 3) * T:(idx % 3 + 1) * T],
                    kT_h[hp * 128:(hp + 1) * 128, :], s_ldk, waits=wk)
                dma("gpsimd", vs[:, (idx % 3) * 4160:(idx % 3 + 1) * 4160],
                    v_h[:, hp, :, :].rearrange("p g c -> p (g c)"), g_ldk,
                    waits=wk)

            def qp_load(h):
                hidx = qb * NH + h
                par = h % 2
                wq = [(c_qkp, 16 * max(0, hidx - 1)), (c_ev, qpad_z)]
                if hidx == 0:
                    wq = wq + [phaseA_qkv[qi] for qi in range(4)]
                dma("gpsimd",
                    qpad[par * 64:(par + 1) * 64, par * 512:(par + 1) * 512],
                    qT_h[h * 64:(h + 1) * 64, qb * 512:(qb + 1) * 512],
                    s_ldp, waits=wq)

            ld_sched = {}
            for hp in range(8):
                ld_sched.setdefault(max(0, 32 * hp - 16), []).append(
                    lambda hp=hp: kv_loads(hp))
            for h in range(NH):
                ld_sched.setdefault(max(0, 16 * h - 10), []).append(
                    lambda h=h: qp_load(h))

            units = [(h, g) for h in range(NH) for g in range(16)]
            head_ap = {}
            for h in range(NH):
                hp = h // 2
                idx = qb * 8 + hp
                head_ap[h] = (
                    kTs[:, (idx % 3) * T:(idx % 3 + 1) * T],
                    vs[:, (idx % 3) * 4160:(idx % 3 + 1) * 4160],
                    qpad[:, (h % 2) * 512:(h % 2 + 1) * 512],
                    16 * (qb * 8 + hp + 1),
                    16 * (qb * NH + h + 1),
                )
            g0 = gidx[0]

            def emit_qk(u):
                h, g = units[u]
                kslc, vslc, qslc, ldk_need, ldp_need = head_ap[h]
                gg = g0 + u
                pst = ps_stA if gg % 2 == 0 else ps_stB
                gw = [(c_exp, max(0, gg - 1)), (s_ldk, ldk_need),
                      (g_ldk, ldk_need), (s_ldp, ldp_need)]
                for j in range(GRP[g]):
                    kt = GOF[g] + j
                    mm(kslc[:, kt * 128:(kt + 1) * 128], qslc,
                       pst[:, j * 512:(j + 1) * 512], start=True, stop=True,
                       waits=(gw if j == 0 else ()),
                       incs=([(c_qkp, 1)] if j == GRP[g] - 1 else ()))

            def emit_exp(u):
                h, g = units[u]
                gg = g0 + u
                pst = ps_stA if gg % 2 == 0 else ps_stB
                ring = (gg % 3) * 1024
                act(eT[:, ring:ring + GRP[g] * 512], pst[:, 0:GRP[g] * 512],
                    AF.Exp, scale=SCALE,
                    waits=[(c_qkp, gg + 1), (c_avg, max(0, gg - 2))],
                    incs=[(c_exp, 1)])

            def emit_av(u):
                h, g = units[u]
                kslc, vslc, qslc, _, _ = head_ap[h]
                hidx = qb * NH + h
                par = h % 2
                gg = g0 + u
                ring = (gg % 3) * 1024
                for j in range(GRP[g]):
                    kt = GOF[g] + j
                    avw = [(c_exp, gg + 1)] if j == 0 else []
                    if g == 0 and j == 0:
                        avw = avw + [(pU_c, hidx)]
                    mm(vslc[:, kt * 130 + par * 65: kt * 130 + (par + 1) * 65],
                       eT[:, ring + j * 512: ring + (j + 1) * 512],
                       ps_u[0:65, :], start=(kt == 0), stop=(kt == 31),
                       waits=avw,
                       incs=([(c_avg, 1)] if j == GRP[g] - 1 else ()))

            pend_norm = []

            def emit_norm_dve(h):
                hidx = qb * NH + h
                sl = (h % 2) * 512
                cp("vector", u_sb[:, sl:sl + 512], ps_u[0:65, :],
                   waits=[(c_avg, g0 + 16 * (h + 1))], incs=[(pU_c, 1)])
                recip(rsg[:, sl:sl + 512], u_sb[64:65, sl:sl + 512],
                      incs=[(c_ev, 1)])
                pend_norm.append((h, c_ev.n))

            def emit_norm_mm(h, rs_done):
                hp = h // 2
                par = h % 2
                sl = (h % 2) * 512
                gi, wg = poolG.acquire()
                mm(ones_r64[:, 0:64], rsg[0:1, sl:sl + 512], gslc(gi, 64, 512),
                   start=True, stop=True, waits=wg + [(c_ev, rs_done)],
                   incs=[poolG.pdone()])
                tt("vector", cxT[par * 64:(par + 1) * 64, hp * 512:(hp + 1) * 512],
                   u_sb[0:64, sl:sl + 512], gslc(gi, 64, 512), ALU.mult,
                   waits=poolG.cwait() + [(pG_p, prj_mark[0])],
                   incs=[poolG.cdone()])

            tail_sched = {}
            for k in range(8):
                tail_sched[1 + k] = k
            for k in range(4):
                tail_sched[20 + 14 * k] = 8 + k
                tail_sched[24 + 14 * k] = 12 + k
                tail_sched[27 + 14 * k] = 16 + k
                tail_sched[30 + 14 * k] = 20 + k
                tail_sched[33 + 14 * k] = 24 + k
            for k in range(4):
                tail_sched[84 + 3 * k] = 28 + k
            for k in range(4):
                tail_sched[100 + 14 * k] = 32 + k
            for u in range(len(units)):
                for f in ld_sched.get(u, ()):
                    f()
                emit_qk(u)
                emit_exp(u)
                if u >= 1:
                    emit_av(u - 1)
                    if units[u - 1][1] == 15:
                        emit_norm_dve(units[u - 1][0])
                while pend_norm and pend_norm[0][0] <= units[u][0] - 2:
                    emit_norm_mm(*pend_norm.pop(0))
                if u in tail_sched and tail_sched[u] < len(pend_tail):
                    pend_tail[tail_sched[u]]()
                    pend_tail[tail_sched[u]] = None
            emit_av(len(units) - 1)
            emit_norm_dve(NH - 1)
            while pend_norm:
                emit_norm_mm(*pend_norm.pop(0))
            gidx[0] += len(units)
            cx_mark = pG_c.n

            def mk_tail(qb, cx_mark):
                pieces = []
                tstate = {"zt_mark": 0, "gelu_mark": 0}

                def proj_piece(tt4, oc):
                    def f():
                        c = qb * 4 + tt4
                        gi, w = poolG.acquire()
                        for dcin in range(DC):
                            mm(cxT[:, dcin * 512 + tt4 * 128:
                                   dcin * 512 + (tt4 + 1) * 128],
                               Wp_sb[:, dcin * 1024 + oc * 512:
                                     dcin * 1024 + (oc + 1) * 512],
                               gslc(gi), start=(dcin == 0),
                               stop=(dcin == DC - 1),
                               waits=(w + [(pG_c, cx_mark), (g_ldw, ldw1)]
                                      if dcin == 0 else ()),
                               incs=([poolG.pdone()] if dcin == DC - 1 else ()))
                        if tt4 == 3 and oc == 1:
                            prj_mark[0] = pG_p.n
                        tt("vector", w1k[:, tt4 * D + oc * 512:
                                         tt4 * D + (oc + 1) * 512],
                           gslc(gi),
                           bprj_bc[:, oc * 512:(oc + 1) * 512], ALU.add,
                           waits=poolG.cwait(), incs=[poolG.cdone()])
                    return f

                def x1z_s1(tt4):
                    def f():
                        c = qb * 4 + tt4
                        slot = c % 4
                        xs = xa[:, slot * D:(slot + 1) * D]
                        if "xldn" in tstate:
                            xldn = tstate["xldn"][tt4]
                        else:
                            dma("sync", xs, x_d[c * 128:(c + 1) * 128, :], s_ldx,
                                waits=[(c_xfree, NT + max(0, c - 3))])
                            xldn = s_ldx.n
                        tstate[("xldn2", tt4)] = xldn
                        wk = w1k[:, tt4 * D:(tt4 + 1) * D]
                        red(colA[:], wk, ALU.add)
                        ts("vector", colA[:], colA[:], 1.0 / D, None, ALU.mult)
                        ts("vector", t1[:], wk, colA[:, 0:1], None, ALU.subtract,
                           waits=[(c_ac, last_z_act[0])])
                        stt("vector", jnk[:], t1[:], 1.0, t1[:], ALU.mult,
                            ALU.mult, accum=colC[:, 0:1], incs=[(c_ev, 1)])
                        tstate[("m1", tt4)] = c_ev.n
                    return f

                def x1z_s2(tt4):
                    def f():
                        c = qb * 4 + tt4
                        slot = c % 4
                        xs = xa[:, slot * D:(slot + 1) * D]
                        xldn = tstate[("xldn2", tt4)]
                        act(colD[:], colC[:], AF.Ln, bias=eps_t[:, 0:1],
                            scale=1.0 / D, waits=[(c_ev, tstate[("m1", tt4)])])
                        act(colD[:], colD[:], AF.Exp, scale=-0.5,
                            incs=[(c_ac, 1)])
                        m2 = c_ac.n
                        stt("vector", x1t[:], t1[:], colD[:, 0:1], xs, ALU.mult,
                            ALU.add,
                            waits=[(s_ldx, xldn), (s_stz, stok["x1"]),
                                   (c_ac, m2)],
                            incs=[(c_xfree, 1)])
                        dma("sync", x1_h[c * 128:(c + 1) * 128, :], x1t[:],
                            s_stz, waits=[(c_xfree, c_xfree.n)])
                        stok["x1"] = s_stz.n
                        red(colA[:], x1t[:], ALU.add)
                        ts("vector", colA[:], colA[:], 1.0 / D, None, ALU.mult)
                        ts("vector", t1[:], x1t[:], colA[:, 0:1], None,
                           ALU.subtract)
                        stt("vector", jnk[:], t1[:], 1.0, t1[:], ALU.mult,
                            ALU.mult, accum=colC[:, 0:1], incs=[(c_ev, 1)])
                        tstate[("m3", tt4)] = c_ev.n
                    return f

                def x1z_s3(tt4):
                    def f():
                        dvy = tstate[("m3", tt4)]
                        act(colD[:], colC[:], AF.Ln, bias=eps_t[:, 0:1],
                            scale=1.0 / D, waits=[(c_ev, dvy)])
                        act(colD[:], colD[:], AF.Exp, scale=-0.5,
                            incs=[(c_ac, 1)])
                        act(colE[:], colC[:], AF.Ln)
                        act(colE[:], colE[:], AF.Exp, scale=0.5,
                            incs=[(c_ac, 1)])
                        m4 = c_ac.n
                        tt("vector", colE[:], colE[:], colD[:], ALU.mult,
                           waits=[(c_ac, m4)])
                        ts("vector", colE[:], colE[:], 1e-12, None, ALU.max)
                        recip(colE[:], colE[:])
                        ts("vector", colE[:], colE[:], 32.0, None, ALU.mult)
                        tt("vector", colE[:], colE[:], colD[:], ALU.mult,
                           incs=[(c_ev, 1)])
                        tstate[("dvz", tt4)] = c_ev.n
                    return f

                def x1z_s4(tt4):
                    def f():
                        c = qb * 4 + tt4
                        S.add("scalar",
                              lambda e, o=u8[:], i=t1[:], m=colE[:, 0:1]:
                                  e.mul(out=o, in_=i, mul=m),
                              waits=[(c_ev, tstate[("dvz", tt4)]),
                                     (pG_p, zt_w[0])],
                              incs=[(c_ac, 1)],
                              touch=[u8[:], t1[:], colE[:]])
                        acz = c_ac.n
                        last_z_act[0] = acz
                        tstate[("acz", tt4)] = acz
                        cp("vector", lb16[:], u8[:],
                           waits=[(c_ac, acz), (s_stz, stok["z"])],
                           incs=[(c_ev, 1)])
                        dma("sync", z_h[c * 128:(c + 1) * 128, :], lb16[:],
                            s_stz, waits=[(c_ev, c_ev.n)])
                        stok["z"] = s_stz.n
                        stt("vector", jnk[:], t1[:], 1.0, rt_bc[:], ALU.mult,
                            ALU.mult, accum=colF[:, 0:1])
                        tt("vector", colF[:], colF[:], colD[:], ALU.mult)
                        cp("vector", s_parts[:, c:c + 1], colF[:, 0:1])
                    return f

                def x1zT_piece(tt4):
                    def f():
                        acz = tstate[("acz", tt4)]
                        zT_v = zTl[:].rearrange("p (dc q) -> p dc q", q=512)
                        for grp in range(2):
                            gi2, w = poolG.acquire()
                            for j in range(4):
                                dc = grp * 4 + j
                                transpose(ps_gen[:, gi2 * 512 + j * 128:
                                                 gi2 * 512 + (j + 1) * 128],
                                          u8[:, dc * 128:(dc + 1) * 128],
                                          waits=(w + [(c_ac, acz)] +
                                                 ([(pG_p, lgt_mark[0])]
                                                  if (grp == 0 and tt4 == 0)
                                                  else [])
                                                 if j == 0 else ()),
                                          incs=([poolG.pdone()] if j == 3 else ()))
                            cp("vector",
                               zT_v[:, grp * 4:(grp + 1) * 4,
                                    tt4 * 128:(tt4 + 1) * 128],
                               gslc(gi2).rearrange("p (dc q) -> p dc q", q=128),
                               waits=poolG.cwait(), incs=[poolG.cdone()])
                        zt_w[0] = pG_p.n
                        if tt4 == 3:
                            tstate["zt_mark"] = pG_c.n
                    return f

                def lh1_piece(fc):
                    def f():
                        gi, w = poolG.acquire()
                        for dc in range(DC):
                            mm(W1l_sb[:, dc * 512 + fc * 128:
                                      dc * 512 + (fc + 1) * 128],
                               zTl[:, dc * 512:(dc + 1) * 512], gslc(gi),
                               start=(dc == 0), stop=(dc == DC - 1),
                               waits=(w + [(pG_c, tstate["zt_mark"]),
                                           (c_ev, w1l_mark), (pG_p, lgt_mark[0])]
                                      if dc == 0 else ()),
                               incs=([poolG.pdone()] if dc == DC - 1 else ()))
                        if fc == 3:
                            lgt_mark[0] = pG_p.n
                        act(h1l[:, fc * 512:(fc + 1) * 512], gslc(gi),
                            AF.Gelu, bias=b1l[:, fc:fc + 1],
                            waits=poolG.cwait() + [(pG_p, lout_mark[0])],
                            incs=[poolG.cdone()])
                        if fc == 3:
                            tstate["gelu_mark"] = pG_c.n
                    return f

                def lout_piece(tt4):
                    def f():
                        c = qb * 4 + tt4
                        ls = (tt4 % 2) * D
                        lkey = "l%d" % (tt4 % 2)
                        for oc in range(2):
                            gi, w = poolG.acquire()
                            for fc in range(4):
                                mm(h1l[:, fc * 512 + tt4 * 128:
                                       fc * 512 + (tt4 + 1) * 128],
                                   W2l_sb[:, fc * 1024 + oc * 512:
                                         fc * 1024 + (oc + 1) * 512],
                                   gslc(gi), start=(fc == 0),
                                   stop=(fc == 3),
                                   waits=(w + [(pG_c, tstate["gelu_mark"])]
                                          if fc == 0 else ()),
                                   incs=([poolG.pdone()] if fc == 3 else ()))
                            tt("vector", lstg[:, ls + oc * 512:ls + (oc + 1) * 512],
                               gslc(gi),
                               b2l_bc[:, oc * 512:(oc + 1) * 512], ALU.add,
                               waits=poolG.cwait() + [(s_stz, stok[lkey])],
                               incs=[poolG.cdone()])
                        dma("sync", light_h[c * 128:(c + 1) * 128, :],
                            lstg[:, ls:ls + D], s_stz, waits=[(pG_c, pG_c.n)])
                        stok[lkey] = s_stz.n
                        if tt4 == 3:
                            lout_mark[0] = pG_p.n
                    return f

                for tt4 in range(4):
                    for oc in range(2):
                        pieces.append(proj_piece(tt4, oc))
                for tt4 in range(4):
                    pieces.append(x1z_s1(tt4))
                for tt4 in range(4):
                    pieces.append(x1z_s2(tt4))
                for tt4 in range(4):
                    pieces.append(x1z_s3(tt4))
                for tt4 in range(4):
                    pieces.append(x1z_s4(tt4))
                for tt4 in range(4):
                    pieces.append(x1zT_piece(tt4))
                for fc in range(4):
                    pieces.append(lh1_piece(fc))
                for tt4 in range(4):
                    pieces.append(lout_piece(tt4))
                return pieces, tstate

            flush_order = list(range(8)) + \
                [8 + k0 + 4 * s0 for k0 in range(4) for s0 in range(5)] + \
                list(range(28, 36))
            for fi in flush_order:
                if fi < len(pend_tail) and pend_tail[fi] is not None:
                    pend_tail[fi]()
            pend_tail.clear()
            new_pieces, pend_tstate = mk_tail(qb, cx_mark)
            pend_tail.extend(new_pieces)
        for fi in (list(range(8)) +
                   [8 + k0 + 4 * s0 for k0 in range(4) for s0 in range(5)] +
                   list(range(28, 36))):
            if fi < len(pend_tail) and pend_tail[fi] is not None:
                pend_tail[fi]()
        pend_tail.clear()
        phaseB_stz = s_stz.n
        attn_done = [(pU_c, 2 * NBLK * 8), (pG_p, lgt_mark[0]),
                     (pG_c, pG_c.n), (c_exp, gidx[0])]

    ts("vector", bb32[:], s_parts[:], 1.0, None, ALU.mult, incs=[(c_ev, 1)])
    if dbg:
        dma("sync", dbg_s, s_parts[:], s_msc, waits=[(c_ev, c_ev.n)])

    def colsum_11(src_col, waits):
        gi, w = poolG.acquire()
        mm(src_col, ones_c128[:, 0:1], gslc(gi, 1, 1), start=True, stop=True,
           waits=w + list(waits), incs=[poolG.pdone()])
        return gi

    def bcast_col(src11, waits):
        gi, w = poolG.acquire()
        mm(ones_r128[:, 0:128], src11, gslc(gi, 128, 1), start=True, stop=True,
           waits=w + list(waits), incs=[poolG.pdone()])
        cp("vector", colB[:], gslc(gi, 128, 1), waits=poolG.cwait(),
           incs=[poolG.cdone()])

    relu_mark = [c_ac.n]
    for it in range(NITER):
        tt("vector", tmp32[:], s_parts[:], bb32[:], ALU.subtract,
           waits=[(c_ac, relu_mark[0])], incs=[(c_ev, 1)])
        act(e32[:], tmp32[:], AF.Exp, accum=colA[:, 0:1],
            waits=[(c_ev, c_ev.n)], incs=[(c_ac, 1)])
        gi = colsum_11(colA[:, 0:1], [(c_ac, c_ac.n)])
        act(sc_t1[:], gslc(gi, 1, 1), AF.Ln, waits=poolG.cwait(),
            incs=[poolG.cdone()])
        gi2, w2 = poolG.acquire()
        mm(ones_r128[:, 0:128], sc_t1[0:1, 0:1], gslc(gi2, 128, 1),
           start=True, stop=True, waits=w2 + [(pG_c, pG_c.n)],
           incs=[poolG.pdone()])
        act(colB[:], gslc(gi2, 128, 1), AF.Identity, scale=-1.0,
            bias=logk_t[:, 0:1],
            waits=poolG.cwait(), incs=[poolG.cdone()])
        act(bb32[:], s_parts[:], AF.Relu, bias=colB[:, 0:1],
            incs=[(c_ac, 1)])
        relu_mark[0] = c_ac.n
    ts("vector", u32[:], s_parts[:], colB[:, 0:1], None, ALU.add,
       waits=[(c_ac, c_ac.n)])
    tt("vector", u32[:], u32[:], bb32[:], ALU.subtract, incs=[(c_ev, 1)])
    u_done = c_ev.n
    if dbg:
        dma("sync", dbg_u, u32[:], s_msc, waits=[(c_ev, u_done)])
    act(q32[:], u32[:], AF.Exp, waits=[(c_ev, u_done)], incs=[(c_ac, 1)])
    ts("vector", tmp32[:], q32[:], -1.0, 1.0, ALU.mult, ALU.add,
       waits=[(c_ac, c_ac.n)])
    tt("vector", m_t[:], q32[:], tmp32[:], ALU.add)
    red(colA[:], u32[:], ALU.min, incs=[(c_ev, 1)])
    giT, wT = poolG.acquire()
    S.add("tensor",
          lambda e, o=ps_gen[0:1, giT * 512:giT * 512 + 128], i=colA[:, 0:1]:
              e.transpose(out=o, in_=i, identity=ident_f[:]),
          waits=wT + [(c_ev, c_ev.n)], incs=[poolG.pdone()])
    red(sc_lo[:], ps_gen[0:1, giT * 512:giT * 512 + 128], ALU.min,
        waits=poolG.cwait(), incs=[poolG.cdone()])
    ts("vector", sc_lo[:], sc_lo[:], 1.0, None, ALU.subtract, incs=[(c_ev, 1)])
    for it in range(NBIS):
        tt("vector", sc_mid[:], sc_lo[:], sc_hi[:], ALU.add, incs=[(c_ev, 1)])
        gib, wb = poolG.acquire()
        mm(half_r128[:, 0:128], sc_mid[0:1, 0:1], gslc(gib, 128, 1),
           start=True, stop=True, waits=wb + [(c_ev, c_ev.n)],
           incs=[poolG.pdone()])
        act(colB[:], gslc(gib, 128, 1), AF.Identity,
            waits=poolG.cwait(), incs=[poolG.cdone()])
        ts("vector", tmp32[:], u32[:], colB[:, 0:1], None, ALU.is_gt,
           accum=colA[:, 0:1], waits=[(pG_c, pG_c.n)], incs=[(c_ev, 1)])
        gi = colsum_11(colA[:, 0:1], [(c_ev, c_ev.n)])
        act(sc_cnt[:], gslc(gi, 1, 1), AF.Identity, waits=poolG.cwait(),
            incs=[poolG.cdone()])
        ts("vector", sc_fu[:], sc_cnt[:], float(NSEL), None, ALU.is_ge,
           waits=[(pG_c, pG_c.n)])
        ts("vector", sc_nu[:], sc_cnt[:], float(NSEL), None, ALU.is_lt)
        S.add("vector",
              lambda e, o=sc_lo[:], m=sc_fu[:], d=colB[0:1, 0:1]:
                  e.copy_predicated(out=o, mask=m, data=d),
              touch=[sc_lo[:], sc_fu[:], colB[:]])
        S.add("vector",
              lambda e, o=sc_hi[:], m=sc_nu[:], d=colB[0:1, 0:1]:
                  e.copy_predicated(out=o, mask=m, data=d),
              touch=[sc_hi[:], sc_nu[:], colB[:]], incs=[(c_ev, 1)])
    bcast_col(sc_hi[0:1, 0:1], [(c_ev, c_ev.n)])
    ts("vector", mku[:], u32[:], colB[:, 0:1], None, ALU.is_gt)
    cp("vector", um32[:], u32[:])
    S.add("vector",
          lambda e: e.copy_predicated(out=um32[:], mask=mku[:], data=nbig32[:]),
          touch=[um32[:], mku[:], nbig32[:]])
    red(colA[:], um32[:], ALU.max, incs=[(c_ev, 1)])
    giT2, wT2 = poolG.acquire()
    S.add("tensor",
          lambda e, o=ps_gen[0:1, giT2 * 512:giT2 * 512 + 128], i=colA[:, 0:1]:
              e.transpose(out=o, in_=i, identity=ident_f[:]),
          waits=wT2 + [(c_ev, c_ev.n)], incs=[poolG.pdone()])
    red(sc_vs[:], ps_gen[0:1, giT2 * 512:giT2 * 512 + 128], ALU.max,
        waits=poolG.cwait(), incs=[(c_ev, 1)])
    S.add("vector", lambda e: e.nop(), incs=[poolG.cdone()])
    bcast_col(sc_vs[0:1, 0:1], [(c_ev, c_ev.n)])
    cp("vector", colI[:], colB[:])
    ts("vector", sel32[:], u32[:], colI[:, 0:1], None, ALU.is_gt,
       accum=colA[:, 0:1], incs=[(c_ev, 1)])
    gi = colsum_11(colA[:, 0:1], [(c_ev, c_ev.n)])
    cp("vector", sc_cnt[:], gslc(gi, 1, 1), waits=poolG.cwait(),
       incs=[poolG.cdone()])
    ts("vector", sc_teq[:], sc_cnt[:], -1.0, float(NSEL), ALU.mult, ALU.add)
    ts("vector", eqm32[:], u32[:], colI[:, 0:1], None, ALU.is_equal,
       incs=[(c_ev, 1)])

    def rank_and_prefix(mask32, out32):
        mk_done = c_ev.n
        gi, w = poolG.acquire()
        mm(lt_f[:], mask32[:], gslc(gi, 128, 32), start=True, stop=True,
           waits=w + [(c_ev, mk_done)], incs=[poolG.pdone()])
        cp("vector", rkp32[:], gslc(gi, 128, 32), waits=poolG.cwait(),
           incs=[poolG.cdone()])
        gi, w = poolG.acquire()
        mm(mask32[:], ones_c128[:, 0:1], gslc(gi, 32, 1), start=True, stop=True,
           waits=w + [(c_ev, mk_done)], incs=[poolG.pdone()])
        cp("vector", colA[0:32, 0:1], gslc(gi, 32, 1), waits=poolG.cwait(),
           incs=[poolG.cdone()])
        dma("sync", red_h[0:32].rearrange("(p o) -> p o", o=1), colA[0:32, 0:1],
            s_msc, waits=[(pG_c, pG_c.n)])
        dma("sync", row32a[:], red_h[0:32].rearrange("(o p) -> o p", o=1), s_msc,
            waits=[(s_msc, s_msc.n)])
        cp("vector", row32b[:], row32a[:], waits=[(s_msc, s_msc.n)])
        for k in (1, 2, 4, 8, 16):
            cp("vector", row32c[0:1, 0:k], row32b[0:1, 0:k])
            tt("vector", row32c[0:1, k:32], row32b[0:1, k:32],
               row32b[0:1, 0:32 - k], ALU.add)
            cp("vector", row32b[:], row32c[:])
        tt("vector", row32b[:], row32b[:], row32a[:], ALU.subtract,
           incs=[(c_ev, 1)])
        gi, w = poolG.acquire()
        mm(ones_r128[:, 0:128], row32b[0:1, :], gslc(gi, 128, 32), start=True,
           stop=True, waits=w + [(c_ev, c_ev.n)], incs=[poolG.pdone()])
        tt("vector", out32[:], rkp32[:], gslc(gi, 128, 32), ALU.add,
           waits=poolG.cwait(), incs=[poolG.cdone()])

    rank_and_prefix(eqm32, rk32)
    bcast_col(sc_teq[0:1, 0:1], [(v_self, v_self.n)])
    ts("vector", tmp32[:], rk32[:], colB[:, 0:1], None, ALU.is_lt)
    tt("vector", tmp32[:], tmp32[:], eqm32[:], ALU.mult)
    tt("vector", sel32[:], sel32[:], tmp32[:], ALU.add, incs=[(c_ev, 1)])
    rank_and_prefix(sel32, rk32)
    ts("vector", tmp32[:], sel32[:], -1.0, 1.0, ALU.mult, ALU.add)
    tt("vector", dest_f[:], iota_f[:], rk32[:], ALU.subtract)
    ts("vector", dest_f[:], dest_f[:], float(NSEL), None, ALU.add)
    tt("vector", dest_f[:], dest_f[:], tmp32[:], ALU.mult)
    tt("vector", tmp32[:], sel32[:], rk32[:], ALU.mult)
    tt("vector", dest_f[:], dest_f[:], tmp32[:], ALU.add)
    cp("vector", dest_i[:], dest_f[:], incs=[(c_ev, 1)])
    dest_done = c_ev.n
    if dbg:
        dma("sync", dbg_dest, dest_f[:], s_msc, waits=[(c_ev, dest_done)])

    with nc.sbuf_tensor("W64", [128, 8 * FF_H], BF16) as W64, \
         nc.sbuf_tensor("h1h", [128, 32 * 512], BF16) as h1h, \
         nc.sbuf_tensor("zsT", [128, 8 * 512], BF16) as zsT, \
         nc.sbuf_tensor("hstg", [128, 2 * D], BF16) as hstg:

        wF = attn_done + [(s_stz, phaseB_stz), (s_msc, s_msc.n),
                          (g_msc, g_msc.n), (c_ev, dest_done)]
        for c in range(NT):
            slot = c % 2
            wz = (list(wF) if c < 2 else
                  [(g_sct, 16 * (c - 1)), (c_ev, dest_done)])
            dma("sync", zsc[:, slot * D:(slot + 1) * D],
                z_h[c * 128:(c + 1) * 128, :], s_fin, waits=wz)
            S.add("gpsimd",
                  lambda e, o=zperm_h, i=zsc[:, slot * D:(slot + 1) * D],
                         ix=dest_i[:, c:c + 1]:
                      e.indirect_dma_start(out=o,
                                           out_offset=IndirectOffsetOnAxis(ix, 0),
                                           in_=i, in_offset=None),
                  waits=[(s_fin, s_fin.n), (c_ev, dest_done)],
                  incs=[(g_sct, 16)], touch=[zperm_h, zsc[:], dest_i[:]])
        sct_zp = g_sct.n

        def load_W(dram16, nchunk, width, waits):
            for k in range(nchunk):
                dma("sync", W64[:, k * width:(k + 1) * width],
                    dram16[k * 128:(k + 1) * 128, :], s_ldh,
                    waits=(list(waits) if k == 0 else ()))

        pA3 = Pool(pA_p, pA_c, 2)
        pB3 = Pool(pB_p, pB_c, 2)
        load_W(w1h16_h, DC, FF_H,
               attn_done + [(s_stz, phaseB_stz)])
        tp_mark = [pB_p.n]
        for sb_i in range(2):
            ldh_wa = s_ldh.n
            for tt4 in range(4):
                g = sb_i * 4 + tt4
                slot = g % 2
                dma("gpsimd", zsc[:, slot * D:(slot + 1) * D],
                    zperm_h[g * 128:(g + 1) * 128, :], g_ldh,
                    waits=[(g_sct, sct_zp)] +
                          ([(pB_p, tp_mark[0])] if g >= 2 else []))
                cp("vector", u8[:], zsc[:, slot * D:(slot + 1) * D],
                   waits=[(g_ldh, g_ldh.n)] +
                         (list(wF) if (sb_i == 0 and tt4 == 0) else []),
                   incs=[(c_ev, 1)])
                zbd = c_ev.n
                for dc in range(DC):
                    i, w = pB3.acquire()
                    transpose(ps_stB[:, i * 512:i * 512 + 128],
                              u8[:, dc * 128:(dc + 1) * 128],
                              waits=w + [(c_ev, zbd)],
                              incs=[pB3.pdone()])
                    ts("vector",
                       zsT[:, dc * 512 + tt4 * 128: dc * 512 + (tt4 + 1) * 128],
                       ps_stB[:, i * 512:i * 512 + 128], hgam_c[:, dc:dc + 1], None,
                       ALU.mult, waits=pB3.cwait(), incs=[pB3.cdone()])
                tp_mark[0] = pB_p.n
            zsT_mark = pB_c.n
            for fc in range(32):
                i, w = pA3.acquire()
                pslc = ps_stA[:, i * 512:(i + 1) * 512]
                for dc in range(DC):
                    mm(W64[:, dc * FF_H + fc * 128: dc * FF_H + (fc + 1) * 128],
                       zsT[:, dc * 512:(dc + 1) * 512], pslc,
                       start=(dc == 0), stop=(dc == DC - 1),
                       waits=((w + [(pB_c, zsT_mark)] if dc == 0 else []) +
                              ([(s_ldh, ldh_wa - 128 + 16 * (dc + 1))]
                               if fc == 0 else [])),
                       incs=([pA3.pdone()] if dc == DC - 1 else ()))
                act(h1h[:, fc * 512:(fc + 1) * 512], pslc, AF.Gelu,
                    bias=b1h[:, fc:fc + 1], waits=pA3.cwait(),
                    incs=[pA3.cdone()])
            h1mm_mark = pA_p.n
            h1_mark = pA_c.n
            load_W(w2h16_h, 32, D, [(pA_p, h1mm_mark)])
            ldh_wb = s_ldh.n
            for tt4 in range(4):
                g = sb_i * 4 + tt4
                for oc in range(2):
                    i, w = pB3.acquire()
                    pslc = ps_stB[:, i * 512:(i + 1) * 512]
                    for fc in range(32):
                        mm(h1h[:, fc * 512 + tt4 * 128: fc * 512 + (tt4 + 1) * 128],
                           W64[:, fc * 1024 + oc * 512: fc * 1024 + (oc + 1) * 512],
                           pslc, start=(fc == 0), stop=(fc == 31),
                           waits=((w + [(pA_c, h1_mark)] if fc == 0 else []) +
                                  ([(s_ldh, ldh_wb - 512 + 16 * (fc + 1))]
                                   if (tt4 == 0 and oc == 0) else [])),
                           incs=([pB3.pdone()] if fc == 31 else ()))
                    tt("vector",
                       hstg[:, (g % 2) * D + oc * 512:(g % 2) * D + (oc + 1) * 512],
                       pslc, b2h_bc[:, oc * 512:(oc + 1) * 512], ALU.add,
                       waits=pB3.cwait() +
                             [(s_sct, max(0, s_sct.n - 16))],
                       incs=[pB3.cdone()])
                dma("sync", h2b_h[g * 128:(g + 1) * 128, :],
                    hstg[:, (g % 2) * D:(g % 2 + 1) * D], s_sct,
                    waits=[(pB_c, pB_c.n)])
            h2mm_mark = pB_p.n
            if sb_i == 0:
                load_W(w1h16_h, DC, FF_H, [(pB_p, h2mm_mark)])
        h2_all = s_sct.n

    with nc.sbuf_tensor("hg4", [128, 2 * 4 * D], BF16) as hg4, \
         nc.sbuf_tensor("lg4", [128, 2 * 4 * D], BF16) as lg4, \
         nc.sbuf_tensor("xf8", [128, 2 * 4 * D], F32) as xf8, \
         nc.sbuf_tensor("out2", [128, 2 * D], F32) as out2, \
         nc.sbuf_tensor("fA4", [128, 8], F32) as fA4, \
         nc.sbuf_tensor("fB4", [128, 8], F32) as fB4, \
         nc.sbuf_tensor("fC4", [128, 8], F32) as fC4, \
         nc.sbuf_tensor("fD4", [128, 8], F32) as fD4:

        gf_sems = [g_fin, g_fin2, g_fin3, g_fin4]
        ffn_use = {-2: 0, -1: 0}
        ld_toks = {}
        gl_toks = {}

        def emit_wave_gathers(w8):
            par = w8 % 2
            pb = par * 4 * D
            gate = [(c_ev, ffn_use[w8 - 2]), (s_sct, h2_all),
                    (g_zf, zfill_cnt)]
            for tt4 in range(4):
                c = w8 * 4 + tt4
                gs = gf_sems[tt4]
                S.add("gpsimd",
                      lambda e, o=hg4[:, pb + tt4 * D:pb + (tt4 + 1) * D],
                             i=h2b_h, ix=dest_i[:, c:c + 1]:
                          e.indirect_dma_start(out=o, out_offset=None, in_=i,
                                               in_offset=IndirectOffsetOnAxis(ix, 0)),
                      waits=(gate if tt4 == 0 else ()),
                      incs=[(gs, 16)])
            gl_toks[w8] = [(gs2, gs2.n) for gs2 in gf_sems]

        def emit_wave_sloads(w8):
            par = w8 % 2
            pb = par * 4 * D
            gate = [(c_ev, ffn_use[w8 - 2]), (s_sct, h2_all),
                    (c_xfree, 2 * NT + 4 * max(0, w8 - 1))]
            for tt4 in range(4):
                c = w8 * 4 + tt4
                extra = [(s_stz, phaseB_stz)] if c == 0 else []
                dma("sync", xf8[:, pb + tt4 * D:pb + (tt4 + 1) * D],
                    x1_h[c * 128:(c + 1) * 128, :], s_fin,
                    waits=gate + extra)
                dma("scalar", lg4[:, pb + tt4 * D:pb + (tt4 + 1) * D],
                    light_h[c * 128:(c + 1) * 128, :], s_fl,
                    waits=gate + extra)
            ld_toks[w8] = [(s_fin, s_fin.n), (s_fl, s_fl.n)]

        emit_wave_gathers(0)
        emit_wave_sloads(0)
        for w8 in range(8):
            par = w8 % 2
            pb = par * 4 * D
            cb = par * 4
            if w8 + 1 < 8:
                emit_wave_gathers(w8 + 1)
                emit_wave_sloads(w8 + 1)
            for tt4 in range(4):
                c = w8 * 4 + tt4
                stt("vector", w1k[:, tt4 * D:(tt4 + 1) * D],
                    hg4[:, pb + tt4 * D:pb + (tt4 + 1) * D], m_t[:, c:c + 1],
                    lg4[:, pb + tt4 * D:pb + (tt4 + 1) * D], ALU.mult, ALU.add,
                    accum=fA4[:, cb + tt4:cb + tt4 + 1],
                    waits=((ld_toks[w8] + gl_toks[w8]) if tt4 == 0 else ()),
                    incs=([(c_ev, 1)] if tt4 == 3 else ()))
            ffn_use[w8] = c_ev.n
            ffn_mark = c_ev.n
            for tt4 in range(4):
                act(jnk[:] if tt4 % 2 == 0 else t1[:],
                    w1k[:, tt4 * D:(tt4 + 1) * D], AF.Square,
                    accum=fC4[:, cb + tt4:cb + tt4 + 1],
                    waits=([(c_ev, ffn_mark)] if tt4 == 0 else ()))
            act(fB4[:, cb:cb + 4], fA4[:, cb:cb + 4], AF.Identity,
                scale=1.0 / D)
            act(fA4[:, cb:cb + 4], fA4[:, cb:cb + 4], AF.Identity,
                scale=-1.0 / D)
            for tt4 in range(4):
                act(fD4[:, cb + tt4:cb + tt4 + 1],
                    fB4[:, cb + tt4:cb + tt4 + 1], AF.Identity,
                    scale=fA4[:, cb + tt4:cb + tt4 + 1])
            for tt4 in range(4):
                act(fD4[:, cb + tt4:cb + tt4 + 1],
                    fC4[:, cb + tt4:cb + tt4 + 1], AF.Identity,
                    scale=1.0 / D, bias=fD4[:, cb + tt4:cb + tt4 + 1])
            act(fD4[:, cb:cb + 4], fD4[:, cb:cb + 4], AF.Ln,
                bias=eps_t[:, 0:1])
            act(fD4[:, cb:cb + 4], fD4[:, cb:cb + 4], AF.Exp, scale=-0.5,
                incs=[(c_ac, 1)])
            stat_mark = c_ac.n
            for tt4 in range(4):
                c = w8 * 4 + tt4
                ob = (c % 2) * D
                ts("vector", t1[:], w1k[:, tt4 * D:(tt4 + 1) * D],
                   fB4[:, cb + tt4:cb + tt4 + 1],
                   fD4[:, cb + tt4:cb + tt4 + 1],
                   ALU.subtract, ALU.mult,
                   waits=([(c_ac, stat_mark)] if tt4 == 0 else ()))
                tt("vector", out2[:, ob:ob + D], t1[:],
                   xf8[:, pb + tt4 * D:pb + (tt4 + 1) * D],
                   ALU.add, waits=[(s_out, 16 * max(0, c - 1))],
                   incs=[(c_xfree, 1)])
                dma("sync", out_d[c * 128:(c + 1) * 128, :],
                    out2[:, ob:ob + D], s_out,
                    waits=[(c_xfree, c_xfree.n)])
        waitonly("sync", [(s_out, 16 * NT)])

    with nc.Block() as blk:
        blk.sync(lambda e: S.replay(e, "sync"))
        blk.gpsimd(lambda e: S.replay(e, "gpsimd"))
        blk.vector(lambda e: S.replay(e, "vector"))
        blk.scalar(lambda e: S.replay(e, "scalar"))
        blk.tensor(lambda e: S.replay(e, "tensor"))

    es.close()
    return nc


_CONSTS = None


def _consts():
    global _CONSTS
    if _CONSTS is None:
        _CONSTS = {
            "cc_ident": np.eye(128, dtype=np.float32),
            "cc_lt": np.triu(np.ones((128, 128), np.float32), 1),
            "cc_iota": (np.arange(32, dtype=np.float32)[None, :] * 128.0
                        + np.arange(128, dtype=np.float32)[:, None]),
        }
    return _CONSTS


def kernel(L0_attn_x, L0_attn_ln_gamma, L0_attn_ln_beta, L0_attn_W_qkv,
           L0_attn_b_qkv, L0_attn_W_proj, L0_attn_b_proj, L0_ffn_routing_token,
           L0_ffn_light_gamma, L0_ffn_light_w1, L0_ffn_light_b1, L0_ffn_light_w2,
           L0_ffn_light_b2, L0_ffn_heavy_gamma, L0_ffn_heavy_w1, L0_ffn_heavy_b1,
           L0_ffn_heavy_w2, L0_ffn_heavy_b2):
    x = np.ascontiguousarray(np.asarray(L0_attn_x, dtype=np.float32))
    f32c = lambda a: np.ascontiguousarray(np.asarray(a, np.float32))
    shared = {
        "ln_gamma": f32c(L0_attn_ln_gamma),
        "ln_beta": f32c(L0_attn_ln_beta),
        "W_qkv": f32c(L0_attn_W_qkv),
        "b_qkv": f32c(L0_attn_b_qkv),
        "W_proj": f32c(L0_attn_W_proj),
        "b_proj": f32c(L0_attn_b_proj),
        "routing_token": f32c(L0_ffn_routing_token),
        "light_gamma": f32c(L0_ffn_light_gamma),
        "light_w1": f32c(L0_ffn_light_w1),
        "light_b1": f32c(L0_ffn_light_b1),
        "light_w2": f32c(L0_ffn_light_w2),
        "light_b2": f32c(L0_ffn_light_b2),
        "heavy_gamma": f32c(L0_ffn_heavy_gamma),
        "heavy_w1": f32c(L0_ffn_heavy_w1),
        "heavy_b1": f32c(L0_ffn_heavy_b1),
        "heavy_w2": f32c(L0_ffn_heavy_w2),
        "heavy_b2": f32c(L0_ffn_heavy_b2),
    }
    shared.update(_consts())
    nc = build_nc()
    in_maps = [dict(shared, x=np.ascontiguousarray(x[b])) for b in range(8)]
    res = run_bass_kernel_spmd(nc, in_maps, core_ids=list(range(8)))
    out = np.stack([res.results[b]["out"] for b in range(8)], axis=0)
    return out
